# version 12
# baseline (speedup 1.0000x reference)
import math
from collections import defaultdict
from contextlib import ExitStack

import numpy as np
import ml_dtypes

import concourse.bass as bass
import concourse.mybir as mybir
from concourse.bass_utils import run_bass_kernel_spmd

F32 = mybir.dt.float32
BF16 = mybir.dt.bfloat16
AF = mybir.ActivationFunctionType
ALU = mybir.AluOpType
AX = mybir.AxisListType

D, KT, N, NG = 1024, 8, 2048, 4
EPS = 1e-5
GLN = 16.0
H4, DK, DV, KD, VD = 4, 128, 256, 512, 1024
H8, HD = 8, 128
INV_SQRT_HD = 1.0 / math.sqrt(128.0)
E8, HF = 8, 4096

BUILD_PHASES = ("attn_a", "ffn1", "attn_b", "ffn2")

ENGINES = ("sync", "tensor", "vector", "scalar", "gpsimd")


class Builder:
    def __init__(self, nc):
        self.nc = nc
        self.st = {e: [] for e in ENGINES}
        self.semh = {}
        self.cnt = defaultdict(int)
        self.lastwait = {}

    def sem(self, name, stack):
        self.semh[name] = stack.enter_context(self.nc.semaphore(name))

    def wait(self, eng, nm, val):
        if val is None or val <= 0:
            return
        key = (eng, nm)
        if self.lastwait.get(key, -1) >= val:
            return
        self.lastwait[key] = val
        h = self.semh[nm]
        self.st[eng].append(lambda E, h=h, v=val: E.wait_ge(h, v))

    AUTO_SELF = {"vector": "v", "scalar": "s", "gpsimd": "g"}

    def emit(self, eng, f, waits=(), inc=None, k=1):
        own = self.AUTO_SELF.get(eng)
        if own is not None and self.cnt[own] > 0:
            self.wait(eng, own, self.cnt[own])
        for wt in waits:
            if wt is None:
                continue
            if isinstance(wt, list):
                for w2 in wt:
                    if w2 is not None:
                        self.wait(eng, w2[0], w2[1])
            else:
                self.wait(eng, wt[0], wt[1])
        if inc is not None:
            self.cnt[inc] += k
            h = self.semh[inc]
            self.st[eng].append(lambda E, f=f, h=h, kk=k: f(E).then_inc(h, kk))
            return self.cnt[inc]
        self.st[eng].append(f)
        return None

    def dma(self, out_ap, in_ap, waits=(), q="dX"):
        return self.emit(
            "sync", lambda E, o=out_ap, i=in_ap: E.dma_start(out=o, in_=i),
            waits=waits, inc=q, k=16)


class Bank:
    def __init__(self, t):
        self.t = t
        self.free = None

    def v(self, lo=0, hi=512):
        return self.t[:, lo:hi]


class Reg:
    def __init__(self):
        self.free = None

    def take(self):
        w, self.free = self.free, None
        if w is None:
            return []
        return w if isinstance(w, list) else [w]

    def freed(self, *ms):
        self.free = [m for m in ms if m is not None]


def build_program():
    nc = bass.Bass("TRN2", target_bir_lowering=False)
    B = Builder(nc)

    def din(name, shape, dt=F32):
        return nc.dram_tensor(name, shape, dt, kind="ExternalInput").ap()

    xin = din("xin", [D, N])
    out = nc.dram_tensor("out", [D, N], F32, kind="ExternalOutput").ap()
    w_aq = din("w_aq", [D, KD]); w_ak = din("w_ak", [D, KD])
    w_av = din("w_av", [D, VD]); w_ag = din("w_ag", [D, VD])
    w_ao = din("w_ao", [VD, D])
    w_gk1 = din("w_gk1", [D, 16])
    gk2pad = din("gk2pad", [128, KD])
    rmsw2 = din("rmsw2", [128, 2])
    w_bq = din("w_bq", [D, D]); w_bk = din("w_bk", [D, D])
    w_bv = din("w_bv", [D, D]); w_bu = din("w_bu", [D, D])
    bias_exp = din("bias_exp", [H8, 16 * 128, 512])
    w_fg = din("w_fg", [D, E8])
    bg_row = din("bg_row", [1, E8])
    w_f1 = din("w_f1", [E8, D, HF]); w_f2 = din("w_f2", [E8, HF, D])
    w_f3 = din("w_f3", [E8, D, HF])
    c_triu = din("c_triu", [128, 128])
    c_idf = din("c_idf", [128, 128])
    c_idb = din("c_idb", [128, 128], BF16)
    c_iota = din("c_iota", [128, E8])
    c_bmask = din("c_bmask", [128, 4 * 512], BF16)
    c_triu4 = din("c_triu4", [128, 512])

    with ExitStack() as stack:
        E = stack.enter_context
        for s in ("dX", "dA", "dB", "dC", "dD", "dO", "t", "v", "s", "g"):
            B.sem(s, stack)

        xT = E(nc.sbuf_tensor("xT", [128, KT * N], F32))
        arena = E(nc.sbuf_tensor("arena", [128, 24832], F32))
        rows = E(nc.sbuf_tensor("rows", [1, 6144], F32))
        cols = E(nc.sbuf_tensor("cols", [128, 160], F32))
        onesr_f = E(nc.sbuf_tensor("onesr_f", [1, 128], F32))
        onesc_f = E(nc.sbuf_tensor("onesc_f", [128, 1], F32))
        onesc_b = E(nc.sbuf_tensor("onesc_b", [128, 1], BF16))
        t1ext = E(nc.sbuf_tensor("t1ext", [128, N], BF16))
        S32 = E(nc.sbuf_tensor("S32", [128, H4 * DV], F32))
        Sbf = E(nc.sbuf_tensor("Sbf", [128, H4 * DV], BF16))
        triu = E(nc.sbuf_tensor("triu", [128, 128], F32))
        triu4 = E(nc.sbuf_tensor("triu4", [128, 512], F32))
        idf = E(nc.sbuf_tensor("idf", [128, 128], F32))
        idb = E(nc.sbuf_tensor("idb", [128, 128], BF16))
        iota = E(nc.sbuf_tensor("iota", [128, E8], F32))
        bmask = E(nc.sbuf_tensor("bmask", [128, 4 * 512], BF16))
        gk2b = E(nc.sbuf_tensor("gk2b", [128, KD], BF16))
        gk1b = E(nc.sbuf_tensor("gk1b", [128, KT * 16], BF16))
        wgf = E(nc.sbuf_tensor("wgf", [128, KT * E8], F32))
        bgr = E(nc.sbuf_tensor("bgr", [1, E8], F32))
        sbg = E(nc.sbuf_tensor("sbg", [128, 2 * E8], F32))

        P = [Bank(E(nc.psum_tensor(f"p{i}", [128, 512], F32))) for i in range(7)]
        PB = Bank(E(nc.psum_tensor("pb", [128, 1024], BF16)))

        R_M, R_RSTD, R_SCRD, R_T0, R_T1 = 0, 2048, 4096, 5120, 5632

        def row(off, n):
            return rows[0:1, off:off + n]

        def af(off, n):
            return arena[:, off:off + n]

        def ab(off_f32, nbf):
            return arena[:, off_f32:off_f32 + (nbf + 1) // 2] \
                .bitcast(BF16)[:, 0:nbf]

        STG0, STG1 = 0, 2048
        FSC0, FSC1, FSC2 = 4096, 4608, 5120
        CBASE = 5632

        def v_tt(o, a, b, op, waits=(), inc="v"):
            return B.emit("vector",
                          lambda En, oo=o, x=a, y=b, p=op:
                          En.tensor_tensor(out=oo, in0=x, in1=y, op=p),
                          waits=waits, inc=inc)

        def v_ts(o, a, s1, s2, op0, op1=None, waits=(), inc="v"):
            if op1 is None:
                return B.emit("vector",
                              lambda En, oo=o, x=a, aa=s1, p=op0:
                              En.tensor_single_scalar(out=oo, in_=x, scalar=aa,
                                                      op=p),
                              waits=waits, inc=inc)
            return B.emit("vector",
                          lambda En, oo=o, x=a, aa=s1, bb=s2, p0=op0, p1=op1:
                          En.tensor_scalar(out=oo, in0=x, scalar1=aa,
                                           scalar2=bb, op0=p0, op1=p1),
                          waits=waits, inc=inc)

        def v_stt(o, a, s, b, op0, op1, waits=(), inc="v"):
            return B.emit("vector",
                          lambda En, oo=o, x=a, ss=s, y=b, p0=op0, p1=op1:
                          En.scalar_tensor_tensor(out=oo, in0=x, scalar=ss,
                                                  in1=y, op0=p0, op1=p1),
                          waits=waits, inc=inc)

        def v_copy(o, a, waits=(), inc="v"):
            return B.emit("vector",
                          lambda En, oo=o, x=a: En.tensor_copy(out=oo, in_=x),
                          waits=waits, inc=inc)

        def g_copy(o, a, waits=(), inc="g"):
            return B.emit("gpsimd",
                          lambda En, oo=o, x=a: En.tensor_copy(out=oo, in_=x),
                          waits=waits, inc=inc)

        def v_red(o, a, op, waits=(), inc="v"):
            return B.emit("vector",
                          lambda En, oo=o, x=a, p=op:
                          En.tensor_reduce(out=oo, in_=x, axis=AX.X, op=p),
                          waits=waits, inc=inc)

        def v_recip(o, a, waits=(), inc="v"):
            return B.emit("vector",
                          lambda En, oo=o, x=a: En.reciprocal(out=oo, in_=x),
                          waits=waits, inc=inc)

        def v_memset(ap, c, waits=(), inc="v"):
            return B.emit("vector", lambda En, a=ap, cc=c: En.memset(a, cc),
                          waits=waits, inc=inc)

        def s_act(o, a, func, bias=None, scale=None, waits=(), inc="s"):
            kw = {}
            if bias is not None:
                kw["bias"] = bias
            if scale is not None:
                kw["scale"] = scale
            return B.emit("scalar",
                          lambda En, oo=o, x=a, f=func, k=dict(kw):
                          En.activation(out=oo, in_=x, func=f, **k),
                          waits=waits, inc=inc)

        def s_copy(o, a, waits=(), inc="s"):
            return B.emit("scalar",
                          lambda En, oo=o, x=a: En.copy(out=oo, in_=x),
                          waits=waits, inc=inc)

        def mm(bank, o, lhsT, rhs, start, stop, waits=(), inc=None):
            w = list(waits)
            if start and bank.free is not None:
                w.append(bank.free)
                bank.free = None
            return B.emit("tensor",
                          lambda En, oo=o, l=lhsT, r=rhs, st=start, sp=stop:
                          En.matmul(oo, l, r, start=st, stop=sp),
                          waits=w, inc=inc)

        def mm_tr(bank, o, a, ident, waits=(), inc=None):
            w = list(waits)
            if bank.free is not None:
                w.append(bank.free)
                bank.free = None
            return B.emit("tensor",
                          lambda En, oo=o, x=a, i=ident: En.transpose(oo, x, i),
                          waits=w, inc=inc)

        stg = [Reg(), Reg()]

        xin3 = xin.rearrange("(a p) n -> p a n", p=128)
        B.dma(xT[:, 0:8192].rearrange("p (a n) -> p a n", a=4),
              xin3[:, 0:4, :])
        d_x1 = B.dma(xT[:, 8192:16384].rearrange("p (a n) -> p a n", a=4),
                     xin3[:, 4:8, :])
        B.dma(triu[:, :], c_triu)
        B.dma(triu4[:, :], c_triu4)
        B.dma(idf[:, :], c_idf)
        d_idb = B.dma(idb[:, :], c_idb)
        d_io = B.dma(iota[:, :], c_iota)
        d_bm = B.dma(bmask[:, :], c_bmask)
        d_wgf = B.dma(wgf[:, :].rearrange("p (a n) -> p a n", a=8),
                      w_fg.rearrange("(a p) n -> p a n", p=128))
        d_bgr = B.dma(bgr[0:1, :], bg_row)
        d_rms = B.dma(cols[:, 0:2], rmsw2)
        v_memset(onesr_f[0:1, :], 1.0)
        v_memset(onesc_f[:, :], 1.0)
        v_memset(onesc_b[:, :], 1.0)
        cmz = v_memset(t1ext[:, :], 0.0)
        v_memset(t1ext[32:33, :], 1.0, waits=[("v", cmz)])
        v_memset(S32[:, :], 0.0)
        cms = v_memset(Sbf[:, :], 0.0)
        CMS = ("v", cms)
        dg2 = B.dma(af(STG0, 512), gk2pad)
        dg1 = B.dma(af(STG0 + 512, 128).rearrange("p (a n) -> p a n", a=8),
                    w_gk1.rearrange("(a p) n -> p a n", p=128))
        DXA = ("dX", dg1)
        cg2 = s_copy(gk2b[:, :], af(STG0, 512), waits=[DXA])
        cg1 = s_copy(gk1b[:, :], af(STG0 + 512, 128), waits=[DXA])
        stg[0].freed(("s", cg1))
        csm = None
        for kt in range(KT):
            csm = mm(P[2], P[2].t[0:1, 0:E8], onesc_f[:, :],
                     wgf[:, E8 * kt:E8 * kt + E8],
                     start=(kt == 0), stop=(kt == KT - 1),
                     waits=([DXA, CMS] if kt == 0 else []),
                     inc=("t" if kt == KT - 1 else None))
        cS = v_copy(row(R_T0, E8), P[2].t[0:1, 0:E8], waits=[("t", csm)])
        P[2].free = ("v", cS)
        c_sb = mm(P[3], P[3].t[:, 0:E8], onesr_f[0:1, :], row(R_T0, E8),
                  start=True, stop=True, waits=[("v", cS)], inc="t")
        cS2 = v_copy(sbg[:, 0:E8], P[3].t[:, 0:E8], waits=[("t", c_sb)])
        P[3].free = ("v", cS2)
        c_bg = mm(P[4], P[4].t[:, 0:E8], onesr_f[0:1, :], bgr[0:1, :],
                  start=True, stop=True,
                  waits=[DXA, CMS], inc="t")
        cS3 = v_copy(sbg[:, E8:2 * E8], P[4].t[:, 0:E8], waits=[("t", c_bg)])
        P[4].free = ("v", cS3)

        xw = DXA

        sqslot = Reg()
        rowreg = Reg()
        fscreg = Reg()
        stats_ms = [None, None]

        def ln_stats(t0, ntok, sq_off, xw):
            for gg in range(ntok // 512):
                ts = t0 + 512 * gg
                b0, b1 = P[0], P[1]
                sq_ms = sqslot.take()
                mml = [None] * KT
                for kt in range(KT):
                    sl = kt % 2
                    sq = ab(sq_off + sl * 256, 512)
                    xb = ab(sq_off + 512 + sl * 256, 512)
                    wts = [xw] + (sq_ms if kt < 2 else
                                  [("t", mml[kt - 2])])
                    s_act(sq, xT[:, 2048 * kt + ts:2048 * kt + ts + 512],
                          AF.Square, waits=wts)
                    c2 = s_copy(xb, xT[:, 2048 * kt + ts:2048 * kt + ts + 512])
                    mm(b0, b0.t[0:1, 0:512], onesc_b[:, :], xb,
                       start=(kt == 0), stop=(kt == KT - 1),
                       waits=[("s", c2)] + ([CMS] if kt == 0 else []))
                    mml[kt] = mm(b1, b1.t[0:1, 0:512], onesc_b[:, :], sq,
                                 start=(kt == 0), stop=(kt == KT - 1), inc="t")
                lastmm = mml[KT - 1]
                sqslot.freed(("t", lastmm))
                rw = rowreg.take()
                mrow = row(R_M + ts, 512)
                rrow = row(R_RSTD + ts, 512)
                t0r = row(R_T0, 512)
                t1r = row(R_T1, 512)
                c = v_ts(mrow, b0.t[0:1, 0:512], 1.0 / 1024.0, None, ALU.mult,
                         waits=[("t", lastmm)] + rw)
                c2 = v_ts(t0r, b1.t[0:1, 0:512], 1.0 / 1024.0, EPS,
                          ALU.mult, ALU.add)
                b0.free = ("v", c)
                b1.free = ("v", c2)
                c3 = s_act(t1r, mrow, AF.Square, waits=[("v", c2)] + rw)
                c4 = v_tt(t0r, t0r, t1r, ALU.subtract, waits=[("s", c3)])
                c5 = s_act(t1r, t0r, AF.Ln, waits=[("v", c4)])
                c6 = s_act(rrow, t1r, AF.Exp, scale=-0.5, inc="s")
                stats_ms[0] = ("v", c)
                stats_ms[1] = ("s", c6)

        def bcast(bank, rowap, n, waits=(), inc=None):
            return mm(bank, bank.t[:, 0:n], onesr_f[0:1, :], rowap,
                      start=True, stop=True, waits=list(waits) + [CMS], inc=inc)

        def phase_attn_a(xw):
            A_GG = CBASE
            A_LNT = CBASE + 2048
            A_QK = CBASE + 4096
            A_V = CBASE + 6144
            A_GS = CBASE + 8192
            A_WS0 = CBASE + 10240
            A_WS1 = CBASE + 12288
            A_SC = CBASE + 14336
            A_SQ = CBASE + 15872
            A_SSC = CBASE + 16896

            lnT = lambda kt, a, b: ab(A_LNT, 4096)[:, 512 * kt + a:512 * kt + b]
            oT = lambda r, a, b: ab(A_LNT, 4096)[:, 512 * r + a:512 * r + b]
            qTs = lambda a, b: ab(A_QK, 2048)[:, a:b]
            kTs = lambda a, b: ab(A_QK + 1024, 2048)[:, a:b]
            ogd = lambda r: ab(A_QK, 4096)[:, 512 * r:512 * r + 512]
            vv = lambda a, b: ab(A_V, 4096)[:, a:b]
            Gs = lambda r: ab(A_GS, 4096)[:, 512 * r:512 * r + 512]
            gg = lambda a, b: af(A_GG, 2048)[:, a:b]
            wsl = [lambda a, b: ab(A_WS0, 4096)[:, a:b],
                   lambda a, b: ab(A_WS1, 4096)[:, a:b]]
            sc = lambda i: ab(A_SC + 32 * i, 512)

            ws = [Reg(), Reg()]
            r_lnT, r_qk, r_v, r_gs, r_gg = Reg(), Reg(), Reg(), Reg(), Reg()
            sbf_ready = CMS
            scan_free = {}

            def stream_w2(w_hbm, col0, sl, rms_scale=False):
                src3 = w_hbm[:, col0:col0 + 512] \
                    .rearrange("(a p) n -> p a n", p=128)
                slot_ms = ws[sl].take()
                outms = []
                for i in range(2):
                    ssl = i % 2
                    qn = "dA" if ssl == 0 else "dB"
                    stga = af(STG0 if ssl == 0 else STG1, 2048)
                    dc = B.dma(stga.rearrange("p (a n) -> p a n", a=4),
                               src3[:, 4 * i:4 * (i + 1), :],
                               waits=stg[ssl].take(), q=qn)
                    wts = [(qn, dc)] + (slot_ms if i == 0 else [])
                    if not rms_scale:
                        cc = s_copy(wsl[sl](2048 * i, 2048 * (i + 1)), stga,
                                    waits=wts)
                        stg[ssl].freed(("s", cc))
                        outms.append(("s", cc))
                    else:
                        cc = None
                        for j in range(4):
                            kt = 4 * i + j
                            cc = v_ts(wsl[sl](2048 * i + 512 * j,
                                              2048 * i + 512 * j + 512),
                                      stga[:, 512 * j:512 * j + 512],
                                      cols[:, kt % 2:kt % 2 + 1], None,
                                      ALU.mult,
                                      waits=(wts if j == 0 else []) +
                                      ([DXA] if j == 0 else []))
                        stg[ssl].freed(("v", cc))
                        outms.append(("v", cc))
                return outms

            for gt in range(NG):
                ts = 512 * gt
                ln_stats(ts, 512, A_SQ, xw)
                bcast(P[2], row(R_M + ts, 512), 512, waits=[stats_ms[0]])
                cr_ = bcast(P[3], row(R_RSTD + ts, 512), 512,
                            waits=[stats_ms[1]], inc="t")
                lnw_ms = r_lnT.take()
                lastln = None
                for kt in range(KT):
                    f = af(FSC0, 512)
                    v_tt(f, xT[:, 2048 * kt + ts:2048 * kt + ts + 512],
                         P[2].t[:, 0:512], ALU.subtract,
                         waits=[("t", cr_)] +
                         ((lnw_ms + fscreg.take()) if kt == 0 else []))
                    lastln = v_tt(lnT(kt, 0, 512), f, P[3].t[:, 0:512],
                                  ALU.mult)
                P[2].free = ("v", lastln)
                P[3].free = ("v", lastln)
                lnw = ("v", lastln)
                fscreg.freed(lnw)

                qk_ms = r_qk.take()
                msq = stream_w2(w_aq, 0, 0)
                lastq = None
                for mt in range(4):
                    bank = P[mt % 2]
                    cgrp = None
                    for kt in range(KT):
                        pw = [msq[kt // 4], lnw] + (qk_ms if mt == 0 else [])
                        cgrp = mm(bank, bank.v(),
                                  wsl[0](512 * kt + 128 * mt,
                                         512 * kt + 128 * mt + 128),
                                  lnT(kt, 0, 512),
                                  start=(kt == 0), stop=(kt == KT - 1),
                                  waits=(pw if kt in (0, 4) else []),
                                  inc=("t" if kt == KT - 1 else None))
                    lastq = s_copy(qTs(512 * mt, 512 * mt + 512), bank.v(),
                                   waits=[("t", cgrp)], inc="s")
                    bank.free = ("s", lastq)
                    qk_t = cgrp
                msk = stream_w2(w_ak, 0, 1)
                lastk = None
                for mt in range(4):
                    bank = P[2 + mt % 2]
                    cgrp = None
                    for kt in range(KT):
                        pw = [msk[kt // 4], lnw]
                        cgrp = mm(bank, bank.v(),
                                  wsl[1](512 * kt + 128 * mt,
                                         512 * kt + 128 * mt + 128),
                                  lnT(kt, 0, 512),
                                  start=(kt == 0), stop=(kt == KT - 1),
                                  waits=(pw if kt in (0, 4) else []),
                                  inc=("t" if kt == KT - 1 else None))
                    lastk = s_copy(kTs(512 * mt, 512 * mt + 512), bank.v(),
                                   waits=[("t", cgrp)], inc="s")
                    bank.free = ("s", lastk)
                ws[0].freed(("t", qk_t))
                ws[1].freed(("t", cgrp))
                qk_done = ("s", lastk)
                v_ms = r_v.take()
                lastv = None
                for vh in range(2):
                    sl = vh
                    msv = stream_w2(w_av, 512 * vh, sl)
                    cgrp = None
                    for tm in range(4):
                        bank = P[(4 * vh + tm) % 2]
                        for kt in range(KT):
                            pw = [msv[kt // 4], lnw] + \
                                (v_ms if (vh == 0 and tm == 0) else [])
                            cgrp = mm(bank, bank.v(),
                                      lnT(kt, 128 * tm, 128 * tm + 128),
                                      wsl[sl](512 * kt, 512 * kt + 512),
                                      start=(kt == 0), stop=(kt == KT - 1),
                                      waits=(pw if kt in (0, 4) else []),
                                      inc=("t" if kt == KT - 1 else None))
                        lastv = s_copy(vv(1024 * tm + 512 * vh,
                                          1024 * tm + 512 * vh + 512),
                                       bank.v(), waits=[("t", cgrp)], inc="s")
                        bank.free = ("s", lastv)
                    ws[sl].freed(("t", cgrp))
                v_done = ("s", lastv)
                gs_ms = r_gs.take()
                lastg = None
                for vh in range(2):
                    sl = vh
                    msg = stream_w2(w_ag, 512 * vh, sl)
                    cgrp = None
                    for mt in range(4):
                        r = 4 * vh + mt
                        bank = P[2 + r % 2]
                        for kt in range(KT):
                            pw = [msg[kt // 4], lnw] + (gs_ms if r == 0 else [])
                            cgrp = mm(bank, bank.v(),
                                      wsl[sl](512 * kt + 128 * mt,
                                              512 * kt + 128 * mt + 128),
                                      lnT(kt, 0, 512),
                                      start=(kt == 0), stop=(kt == KT - 1),
                                      waits=(pw if kt in (0, 4) else []),
                                      inc=("t" if kt == KT - 1 else None))
                        lastg = s_act(Gs(r), bank.v(), AF.Silu,
                                      waits=[("t", cgrp)], inc="s")
                        bank.free = ("s", lastg)
                    ws[sl].freed(("t", cgrp))
                gs_done = ("s", lastg)
                bank = P[4]
                ct1m = None
                for kt in range(KT):
                    ct1m = mm(bank, bank.t[0:16, 0:512],
                              gk1b[:, 16 * kt:16 * kt + 16],
                              lnT(kt, 0, 512),
                              start=(kt == 0), stop=(kt == KT - 1),
                              waits=([("s", cg1), lnw, CMS] if kt == 0 else []),
                              inc=("t" if kt == KT - 1 else None))
                ct1 = s_copy(t1ext[0:16, ts:ts + 512], bank.t[0:16, 0:512],
                             waits=[("t", ct1m)], inc="s")
                bank.free = ("s", ct1)
                gg_ms = r_gg.take()
                lastgg = None
                fsc_ms = fscreg.take()
                for tm in range(4):
                    bank = P[5 + tm % 2]
                    cz = mm(bank, bank.v(),
                            t1ext[:, ts + 128 * tm:ts + 128 * tm + 128],
                            gk2b[:, :], start=True, stop=True,
                            waits=[("s", ct1), ("s", cg2)], inc="t")
                    f0, f1, f2 = af(FSC0, 512), af(FSC1, 512), af(FSC2, 512)
                    ca = s_act(f0, bank.v(), AF.Abs,
                               waits=[("t", cz)] + fsc_ms)
                    cmin = v_ts(f1, bank.v(), 0.0, None, ALU.min,
                                waits=[("s", ca)] + (gg_ms if tm == 0 else []))
                    bank.free = ("v", cmin)
                    s_act(f2, f0, AF.Exp, scale=-1.0)
                    cl = s_act(f0, f2, AF.Ln, bias=1.0)
                    cs_ = v_tt(f1, f1, f0, ALU.subtract, waits=[("s", cl)])
                    fsc_ms = [("v", cs_)]
                    lastgg = v_ts(gg(512 * tm, 512 * tm + 512), f1,
                                  1.0 / GLN, -3.0, ALU.mult, ALU.max)
                gg_done = ("v", lastgg)
                fscreg.freed(("v", lastgg), ("s", cl))

                qk3 = lambda base: ab(base, 2048) \
                    .rearrange("p (h t) -> p h t", h=4)
                for ci in range(4):
                    tm = ci
                    qs, ks = sc(0), sc(8)
                    ptm, ktl = sc(16), sc(24)
                    eb, en = sc(32), sc(40)
                    bBT, bPT, bO1, bO2 = P[0], P[1], P[2], P[3]
                    bDS, bDS2 = P[4], P[5]
                    cbt = None
                    for hd in range(H4):
                        cbt = mm(bBT, bBT.v(128 * hd, 128 * hd + 128),
                                 gg(512 * tm + 128 * hd,
                                    512 * tm + 128 * hd + 128),
                                 triu[:, :], start=True, stop=True,
                                 waits=([gg_done] if hd == 0 else []),
                                 inc=("t" if hd == H4 - 1 else None))
                    ceb = s_act(eb, bBT.v(0, 512), AF.Exp,
                                waits=[("t", cbt), scan_free.get("eb")])
                    cen = s_act(en, bBT.v(0, 512), AF.Exp, scale=-1.0,
                                waits=[scan_free.get("en")])
                    ced = s_act(cols[:, 2:6]
                                .rearrange("p (a b) -> p a b", a=1),
                                bBT.t.rearrange("p (h t) -> p t h", h=4)
                                [:, 127:128, :],
                                AF.Exp, waits=[scan_free.get("edec")],
                                inc="s")
                    bBT.free = ("s", ced)
                    cqs = v_tt(qs.rearrange("p (h t) -> p h t", h=4),
                               qk3(A_QK)[:, :, 128 * tm:128 * tm + 128],
                               eb.rearrange("p (h t) -> p h t", h=4),
                               ALU.mult,
                               waits=[("s", ceb), qk_done,
                                      scan_free.get("qs")])
                    cks = v_tt(ks.rearrange("p (h t) -> p h t", h=4),
                               qk3(A_QK + 1024)[:, :, 128 * tm:128 * tm + 128],
                               en.rearrange("p (h t) -> p h t", h=4),
                               ALU.mult,
                               waits=[("s", cen), scan_free.get("ks")])
                    scan_free["eb"] = ("v", cqs)
                    scan_free["en"] = ("v", cks)
                    cpt = None
                    for hd in range(H4):
                        cpt = mm(bPT, bPT.v(128 * hd, 128 * hd + 128),
                                 ks[:, 128 * hd:128 * hd + 128],
                                 qs[:, 128 * hd:128 * hd + 128],
                                 start=True, stop=True,
                                 waits=([("v", cks)] if hd == 0 else []),
                                 inc=("t" if hd == H4 - 1 else None))
                    cptm = v_tt(ptm, bPT.v(0, 512), triu4[:, :], ALU.mult,
                                waits=[("t", cpt), scan_free.get("ptm")])
                    bPT.free = ("v", cptm)
                    co_mm = {}
                    for hd in range(H4):
                        bO = bO1 if hd < 2 else bO2
                        off = 256 * (hd % 2)
                        for u in range(2):
                            mm(bO, bO.v(off + 128 * u, off + 128 * u + 128),
                               vv(1024 * tm + 256 * hd + 128 * u,
                                  1024 * tm + 256 * hd + 128 * u + 128),
                               ptm[:, 128 * hd:128 * hd + 128],
                               start=True, stop=False,
                               waits=([("v", cptm), v_done]
                                      if (hd == 0 and u == 0) else []))
                            co_mm[hd, u] = mm(
                                bO, bO.v(off + 128 * u, off + 128 * u + 128),
                                Sbf[:, 256 * hd + 128 * u:
                                    256 * hd + 128 * u + 128],
                                qs[:, 128 * hd:128 * hd + 128],
                                start=False, stop=True,
                                waits=[sbf_ready],
                                inc=("t" if (hd % 2, u) == (1, 1) else None))
                    o_done1 = ("t", co_mm[1, 1])
                    o_done2 = ("t", co_mm[3, 1])
                    last_co1 = None
                    for hd in range(H4):
                        bO = bO1 if hd < 2 else bO2
                        off = 256 * (hd % 2)
                        dst = ab(A_LNT, 4096) \
                            .rearrange("p (r t) -> p r t", r=8) \
                            [:, 2 * hd:2 * hd + 2, 128 * tm:128 * tm + 128]
                        last_co1 = s_copy(
                            dst,
                            bO.t[:, off:off + 256]
                            .rearrange("p (u t) -> p u t", u=2),
                            waits=[o_done1 if hd < 2 else o_done2],
                            inc="s")
                        if hd == 1:
                            bO1.free = ("s", last_co1)
                        if hd == 3:
                            bO2.free = ("s", last_co1)
                    ctr = None
                    for hd in range(H4):
                        ctr = mm_tr(PB, PB.t[:, 128 * hd:128 * hd + 128],
                                    ks[:, 128 * hd:128 * hd + 128], idb[:, :],
                                    waits=[("v", cks), DXA],
                                    inc=("t" if hd == H4 - 1 else None))
                        PB.free = None
                    scan_free["ks"] = ("t", ctr)
                    cktl = v_copy(ktl, PB.t[:, 0:512],
                                  waits=[("t", ctr), scan_free.get("ktl")])
                    PB.free = ("v", cktl)
                    cds = None
                    for hd in range(H4):
                        bD = bDS if hd < 2 else bDS2
                        cds = mm(bD, bD.v(256 * (hd % 2), 256 * (hd % 2) + 256),
                                 ktl[:, 128 * hd:128 * hd + 128],
                                 vv(1024 * tm + 256 * hd,
                                    1024 * tm + 256 * hd + 256),
                                 start=True, stop=True,
                                 waits=([("v", cktl)] if hd == 0 else []),
                                 inc=("t" if hd == H4 - 1 else None))
                    scan_free["ktl"] = ("t", cds)
                    scan_free["qs"] = ("t", cds)
                    ssc = af(A_SSC, 1024)
                    v_tt(ssc[:, 0:512], S32[:, 0:512], bDS.t[:, :], ALU.add,
                         waits=[("t", cds)])
                    ca1 = v_tt(ssc[:, 512:1024], S32[:, 512:1024],
                               bDS2.t[:, :], ALU.add)
                    bDS.free = ("v", ca1)
                    bDS2.free = ("v", ca1)
                    ca2 = None
                    for hd in range(H4):
                        ca2 = v_ts(S32[:, 256 * hd:256 * hd + 256],
                                   ssc[:, 256 * hd:256 * hd + 256],
                                   cols[:, 2 + hd:3 + hd], None, ALU.mult,
                                   waits=([("s", ced)] if hd == 0 else []))
                    scan_free["edec"] = ("v", ca2)
                    csb = s_copy(Sbf[:, :], S32[:, :],
                                 waits=[("v", ca2), o_done2], inc="s")
                    sbf_ready = ("s", csb)

                ot_done = ("s", B.cnt["s"])

                prev_cbb = None
                last_ogd = None
                last_sq = None
                cstat = None
                prev_cstat = None
                for hd in range(H4):
                    bR = P[hd % 2]
                    for rr in range(2):
                        sq = ab(A_SQ + 256 * rr, 512)
                        csq = s_act(sq, oT(2 * hd + rr, 0, 512), AF.Square,
                                    waits=[ot_done, prev_cstat] +
                                    (sqslot.take() if (hd == 0 and rr == 0)
                                     else []))
                        cstat = mm(bR, bR.t[0:1, 0:512], onesc_b[:, :], sq,
                                   start=(rr == 0), stop=(rr == 1),
                                   waits=[("s", csq)],
                                   inc=("t" if rr == 1 else None))
                        last_sq = csq
                    prev_cstat = ("t", cstat)
                    cv0 = v_ts(row(R_T0, 512), bR.t[0:1, 0:512],
                               1.0 / 256.0, EPS, ALU.mult, ALU.add,
                               waits=[("t", cstat), prev_cbb])
                    bR.free = ("v", cv0)
                    cl1 = s_act(row(R_T1, 512), row(R_T0, 512), AF.Ln,
                                waits=[("v", cv0)])
                    cl2 = s_act(row(R_T0, 512), row(R_T1, 512), AF.Exp,
                                scale=-0.5, inc="s")
                    bB = P[2 + hd % 2]
                    cbb = bcast(bB, row(R_T0, 512), 512,
                                waits=[("s", cl2)], inc="t")
                    prev_cbb = ("t", cbb)
                    for u in range(2):
                        r = 2 * hd + u
                        f = af(FSC0 if u == 0 else FSC1, 512)
                        v_tt(f, oT(r, 0, 512), bB.v(), ALU.mult,
                             waits=[("t", cbb)] +
                             (([gs_done] + fscreg.take())
                              if (hd == 0 and u == 0) else []))
                        last_ogd = v_tt(ogd(r), f, Gs(r), ALU.mult)
                    bB.free = ("v", last_ogd)
                ogd_done = ("v", last_ogd)
                fscreg.freed(("v", last_ogd))
                sqslot.freed(("t", cstat))
                rowreg.freed(("t", prev_cbb[1]))

                lastY = None
                for dh in range(2):
                    sl = dh
                    msw = stream_w2(w_ao, 512 * dh, sl, rms_scale=True)
                    cgrp = None
                    for dm in range(4):
                        d_glob = 4 * dh + dm
                        bank = P[4 + dm % 2]
                        for vk in range(KT):
                            cgrp = mm(bank, bank.v(),
                                      wsl[sl](512 * vk + 128 * dm,
                                              512 * vk + 128 * dm + 128),
                                      ogd(vk),
                                      start=(vk == 0), stop=(vk == KT - 1),
                                      waits=([msw[vk // 4], ogd_done]
                                             if vk in (0, 4) else []),
                                      inc=("t" if vk == KT - 1 else None))
                        cY = v_tt(xT[:, 2048 * d_glob + ts:
                                     2048 * d_glob + ts + 512],
                                  xT[:, 2048 * d_glob + ts:
                                     2048 * d_glob + ts + 512],
                                  bank.v(), ALU.add,
                                  waits=[("t", cgrp)], inc="v")
                        bank.free = ("v", cY)
                        lastY = cY
                    ws[sl].freed(("t", cgrp))
                xw = ("v", lastY)

                r_lnT.freed(("v", lastY), ("s", last_sq))
                r_qk.freed(("t", B.cnt["t"]))
                r_v.freed(("t", B.cnt["t"]))
                r_gs.freed(("v", last_ogd))
                r_gg.freed(("t", B.cnt["t"]))
            return xw

        def phase_ffn(xw):
            F_LNTH = CBASE
            F_W1 = CBASE + 4096
            F_W3 = CBASE + 8192
            F_W2 = CBASE + 12288
            F_HT = CBASE + 14336
            F_WB = CBASE + 15360
            F_SL = CBASE + 15872
            F_WEM = CBASE + 16384
            assert F_WEM + 2048 <= 24832
            lnTH = lambda kt, a, b: ab(F_LNTH, 8192)[:, 1024 * kt + a:1024 * kt + b]
            w1s = lambda sl, a, b: ab(F_W1 + 2048 * sl, 4096)[:, a:b]
            w3s = lambda sl, a, b: ab(F_W3 + 2048 * sl, 4096)[:, a:b]
            w2s = lambda a, b: ab(F_W2, 4096)[:, a:b]
            hTs = lambda a, b: ab(F_HT, 2048)[:, a:b]
            F_WB2 = 24192
            wbs = lambda par, gl: ab((F_WB if par == 0 else F_WB2) + 256 * gl,
                                     512)
            sls = lambda i: ab(F_SL + 256 * i, 512)
            wem = arena[0:8, F_WEM:F_WEM + 2048]
            lrw = arena[0:8, F_WEM:F_WEM + 2048]
            wmp = af(F_WEM + 2048, 128)
            C_RM, C_M1, C_R1, C_M2, C_R2, C_DD, C_ED, C_W1c, C_W2c, C_T0 = \
                128, 129, 130, 131, 132, 133, 134, 135, 136, 137
            C_MC, C_RC = 48, 64
            C_LT, C_LT2, C_EQ, C_TT, C_L2 = 80, 88, 96, 104, 112

            fence = [("t", B.cnt["t"]), ("v", B.cnt["v"]), ("s", B.cnt["s"])]
            wsF1 = [Reg(), Reg()]
            wsF3 = [Reg(), Reg()]
            wsF2 = Reg()
            wb_dones = [None, None]
            r_lnTH, r_hT, r_wem = Reg(), Reg(), Reg()
            r_scrd = [Reg(), Reg()]
            r_wb = [Reg(), Reg()]
            dwr_pend = [None, None]
            for rg in (wsF1[0], wsF1[1], wsF3[0], wsF3[1], wsF2, r_lnTH, r_hT,
                       r_wb[0], r_wb[1], r_wem, r_scrd[0], r_scrd[1]):
                rg.freed(*fence)

            ln_stats(0, N, F_HT, xw)
            stm, str_ = stats_ms[0], stats_ms[1]

            lrw_ms = []
            for g in range(NG):
                bank = P[2 + g % 2]
                cg = None
                for kt in range(KT):
                    cg = mm(bank, bank.t[0:8, 0:512],
                            wgf[:, E8 * kt:E8 * kt + E8],
                            xT[:, 2048 * kt + 512 * g:2048 * kt + 512 * g + 512],
                            start=(kt == 0), stop=(kt == KT - 1),
                            waits=([xw] + fence if (g == 0 and kt == 0) else []),
                            inc=("t" if kt == KT - 1 else None))
                ce = s_copy(lrw[0:8, 512 * g:512 * g + 512], bank.t[0:8, 0:512],
                            waits=[("t", cg)] + (fence if g == 0 else []),
                            inc="s")
                bank.free = ("s", ce)
                lrw_ms.append(ce)
            lrw_done = ("s", lrw_ms[-1])

            def emit_lnth(th0):
                lnth_ms = r_lnTH.take() + fscreg.take()
                lastln = None
                lnth_gl = [None, None]
                for gl in range(2):
                    ts = th0 + 512 * gl
                    bcast(P[2], row(R_M + ts, 512), 512, waits=[stm])
                    crr = bcast(P[3], row(R_RSTD + ts, 512), 512,
                                waits=[str_], inc="t")
                    for kt in range(KT):
                        f = af(FSC0, 512)
                        v_tt(f, xT[:, 2048 * kt + ts:2048 * kt + ts + 512],
                             P[2].t[:, 0:512], ALU.subtract,
                             waits=[("t", crr)] +
                             (lnth_ms if (gl == 0 and kt == 0) else []))
                        lastln = v_tt(lnTH(kt, 512 * gl, 512 * gl + 512), f,
                                      P[3].t[:, 0:512], ALU.mult)
                    P[2].free = ("v", lastln)
                    P[3].free = ("v", lastln)
                    lnth_gl[gl] = ("v", lastln)
                fscreg.freed(("v", lastln))
                return lnth_gl

            lnth0_gl = emit_lnth(0)
            last_wem_tr = None
            wem_done = None
            wmp_ms = []
            for tt in range(16):
                bL, bM, bR = P[4], P[5], P[6]
                ctl = B.emit("tensor",
                             lambda En, o=bL.t[:, 0:8],
                             x=lrw[0:8, 128 * tt:128 * tt + 128],
                             i=idf[0:8, 0:8]:
                             En.transpose(o, x, i),
                             waits=[lrw_done] + ([bL.free] if bL.free else []) +
                             (fence if tt == 0 else []),
                             inc="t")
                bL.free = None
                ctm = B.emit("tensor",
                             lambda En, o=bM.t[:, 0:1],
                             x=row(R_M + 128 * tt, 128), i=idf[0:1, 0:1]:
                             En.transpose(o, x, i),
                             waits=[stm] + ([bM.free] if bM.free else []),
                             inc="t")
                bM.free = None
                ctr = B.emit("tensor",
                             lambda En, o=bR.t[:, 0:1],
                             x=row(R_RSTD + 128 * tt, 128),
                             i=idf[0:1, 0:1]:
                             En.transpose(o, x, i),
                             waits=[str_] + ([bR.free] if bR.free else []),
                             inc="t")
                bR.free = None
                mc = cols[:, C_MC + tt:C_MC + tt + 1]
                rc = cols[:, C_RC + tt:C_RC + tt + 1]
                c1 = v_copy(mc, bM.t[:, 0:1], waits=[("t", ctm)])
                c2 = v_copy(rc, bR.t[:, 0:1], waits=[("t", ctr)])
                bM.free = ("v", c1)
                bR.free = ("v", c2)
                lt = cols[:, C_LT:C_LT + 8]
                lt2 = cols[:, C_LT2:C_LT2 + 8]
                cfree = v_ts(lt, bL.t[:, 0:8], rc, None, ALU.mult,
                             waits=[("t", ctl)])
                bL.free = ("v", cfree)
                v_tt(cols[:, C_RM:C_RM + 1], mc, rc, ALU.mult)
                v_ts(lt2, sbg[:, 0:E8], cols[:, C_RM:C_RM + 1], None, ALU.mult)
                v_tt(lt, lt, lt2, ALU.subtract)
                v_tt(lt, lt, sbg[:, E8:2 * E8], ALU.add)
                eq = cols[:, C_EQ:C_EQ + 8]
                ttm = cols[:, C_TT:C_TT + 8]
                l2 = cols[:, C_L2:C_L2 + 8]
                v_red(cols[:, C_M1:C_M1 + 1], lt, ALU.max)
                v_ts(eq, lt, cols[:, C_M1:C_M1 + 1], None, ALU.is_equal)
                v_tt(ttm, eq, iota[:, :], ALU.mult)
                v_red(cols[:, C_R1:C_R1 + 1], ttm, ALU.max)
                v_ts(eq, iota[:, :], cols[:, C_R1:C_R1 + 1], None, ALU.is_equal)
                v_stt(l2, eq, -1e30, lt, ALU.mult, ALU.add)
                v_red(cols[:, C_M2:C_M2 + 1], l2, ALU.max)
                v_ts(ttm, l2, cols[:, C_M2:C_M2 + 1], None, ALU.is_equal)
                v_tt(ttm, ttm, iota[:, :], ALU.mult)
                v_red(cols[:, C_R2:C_R2 + 1], ttm, ALU.max)
                v_ts(ttm, iota[:, :], cols[:, C_R2:C_R2 + 1], None, ALU.is_equal)
                cdd = v_tt(cols[:, C_DD:C_DD + 1], cols[:, C_M2:C_M2 + 1],
                           cols[:, C_M1:C_M1 + 1], ALU.subtract)
                ced_ = s_act(cols[:, C_ED:C_ED + 1], cols[:, C_DD:C_DD + 1],
                             AF.Exp, waits=[("v", cdd)])
                v_ts(cols[:, C_T0:C_T0 + 1], cols[:, C_ED:C_ED + 1], 1.0, None,
                     ALU.add, waits=[("s", ced_)])
                v_recip(cols[:, C_W1c:C_W1c + 1], cols[:, C_T0:C_T0 + 1])
                v_tt(cols[:, C_W2c:C_W2c + 1], cols[:, C_ED:C_ED + 1],
                     cols[:, C_W1c:C_W1c + 1], ALU.mult)
                v_ts(eq, eq, cols[:, C_W1c:C_W1c + 1], None, ALU.mult)
                v_ts(ttm, ttm, cols[:, C_W2c:C_W2c + 1], None, ALU.mult)
                cwm = v_tt(wmp[:, 8 * tt:8 * tt + 8], eq, ttm, ALU.add)
                wmp_ms.append(cwm)
            for tt in range(16):
                bW = P[4]
                ctw = B.emit("tensor",
                             lambda En, o=bW.t[0:8, 0:128],
                             x=wmp[:, 8 * tt:8 * tt + 8], i=idf[:, :]:
                             En.transpose(o, x, i),
                             waits=[("v", wmp_ms[tt])] +
                             ([bW.free] if bW.free else []),
                             inc="t")
                bW.free = None
                cwe = v_copy(wem[0:8, 128 * tt:128 * tt + 128],
                             bW.t[0:8, 0:128],
                             waits=[("t", ctw)] +
                             (r_wem.take() if tt == 0 else []))
                bW.free = ("v", cwe)
                last_wem_tr = ctw
                wem_done = ("v", cwe)


            lnth_gl_by_th = [lnth0_gl, None]
            lastY = None
            sl_free = [None, None]
            prev_csl = None
            rrb = [0]
            jobs = [(TH, e, s) for TH in range(2) for e in range(E8)
                    for s in range(E8)]
            NJ = len(jobs)
            castms = {}
            dinfo = {}

            def issue_dma(gp):
                if gp >= 6 * NJ:
                    return
                j, pc = gp // 6, gp % 6
                TH, e, s = jobs[j]
                par = j % 2
                if pc < 2:
                    src3 = w_f1[e][:, 512 * s:512 * s + 512] \
                        .rearrange("(a p) n -> p a n", p=128)
                    dstf, reg, nsub, i = \
                        (lambda a, b, par=par: w1s(par, a, b)), wsF1[par], 4, pc
                elif pc < 4:
                    src3 = w_f3[e][:, 512 * s:512 * s + 512] \
                        .rearrange("(a p) n -> p a n", p=128)
                    dstf, reg, nsub, i = \
                        (lambda a, b, par=par: w3s(par, a, b)), wsF3[par], 4, pc - 2
                else:
                    src3 = w_f2[e][512 * s:512 * s + 512, :] \
                        .rearrange("(a p) n -> p a n", p=128)
                    dstf, reg, nsub, i = w2s, wsF2, 2, pc - 4
                ssl = gp % 2
                qn = "dA" if ssl == 0 else "dB"
                stga = af(STG0 if ssl == 0 else STG1, 2048)
                dc = B.dma(stga.rearrange("p (a n) -> p a n", a=nsub),
                           src3[:, nsub * i:nsub * (i + 1), :],
                           waits=stg[ssl].take(), q=qn)
                dinfo[gp] = (qn, dc, stga, dstf, 2048 * i, i == 0, reg)

            def emit_cast(gp):
                if gp >= 6 * NJ:
                    return
                qn, dc, stga, dstf, off, first, reg = dinfo.pop(gp)
                wts = [(qn, dc)] + (reg.take() if first else [])
                cc = s_copy(dstf(off, off + 2048), stga, waits=wts)
                ssl = 0 if qn == "dA" else 1
                stg[ssl].freed(("s", cc))
                castms.setdefault(gp // 6, []).append(("s", cc))

            def drip(cur):
                emit_cast(cur[0])
                issue_dma(cur[0] + 2)
                cur[0] += 1

            cur = [0]
            issue_dma(0)
            issue_dma(1)
            for _ in range(6):
                drip(cur)


            for j, (TH, e, s_) in enumerate(jobs):
                th0 = 1024 * TH
                sl = j % 2
                epar = e % 2
                if s_ == 0 and e == 0 and TH == 1:
                    lnth_gl_by_th[1] = emit_lnth(th0)
                if s_ == 0 and e == 0:
                    for gl in range(2):
                        ts = th0 + 512 * gl
                        cmr = v_tt(row(R_T0, 512), row(R_M + ts, 512),
                                   row(R_RSTD + ts, 512), ALU.mult,
                                   waits=rowreg.take() +
                                   (r_scrd[1].free or []))
                        bcast(P[4], row(R_T0, 512), 512, waits=[("v", cmr)])
                        cbr = bcast(P[5], row(R_RSTD + ts, 512), 512, inc="t")
                        lastadd = None
                        for kt in range(KT):
                            f0 = af(FSC0, 512)
                            f1 = af(FSC1, 512)
                            v_tt(f0, xT[:, 2048 * kt + ts:2048 * kt + ts + 512],
                                 P[5].t[:, 0:512], ALU.mult,
                                 waits=[("t", cbr)])
                            v_tt(f1, f0, P[4].t[:, 0:512], ALU.subtract)
                            lastadd = v_tt(
                                xT[:, 2048 * kt + ts:2048 * kt + ts + 512],
                                xT[:, 2048 * kt + ts:2048 * kt + ts + 512],
                                f1, ALU.add)
                        P[4].free = ("v", lastadd)
                        P[5].free = ("v", lastadd)
                        rowreg.freed(("v", lastadd))
                lnth_gl = lnth_gl_by_th[TH]
                if s_ == 0:
                    if dwr_pend[epar] is None:
                        dwr_pend[epar] = B.dma(
                            row(R_SCRD if epar == 0 else R_T0, 1024),
                            wem[e:e + 1, th0:th0 + 1024],
                            waits=[wem_done] + r_scrd[epar].take(),
                            q=("dC" if epar == 0 else "dD"))
                    dwr = dwr_pend[epar]
                    dwr_pend[epar] = None
                    scrd0 = R_SCRD if epar == 0 else R_T0
                    cwb = None
                    cwm_ = None
                    wb_ms = r_wb[epar].take()
                    for gl in range(2):
                        cwm_ = mm(P[6], P[6].t[:, 0:512], onesr_f[0:1, :],
                                  row(scrd0 + 512 * gl, 512),
                                  start=True, stop=True,
                                  waits=[("dC" if epar == 0 else "dD", dwr),
                                         CMS], inc="t")
                        cwb = s_copy(wbs(epar, gl), P[6].t[:, 0:512],
                                     waits=[("t", cwm_)] + wb_ms, inc="s")
                        P[6].free = ("s", cwb)
                    r_scrd[epar].freed(("t", cwm_))
                    wb_dones[epar] = ("s", cwb)
                    nj = j + E8
                    if nj < NJ:
                        nTH, ne, _ = jobs[nj]
                        npar = ne % 2
                        dwr_pend[npar] = B.dma(
                            row(R_SCRD if npar == 0 else R_T0, 1024),
                            wem[ne:ne + 1, 1024 * nTH:1024 * nTH + 1024],
                            waits=[wem_done] + r_scrd[npar].take() +
                            (rowreg.free or []),
                            q=("dC" if npar == 0 else "dD"))
                wb_done = wb_dones[epar]
                ms1 = castms[j][0:2]
                ms3 = castms[j][2:4]
                ms2 = castms[j][4:6]
                ndrip = 4 if j + 1 < NJ else 0
                for gl in range(2):
                    ts = th0 + 512 * gl
                    hT_ms = r_hT.take()
                    lasth = None
                    h_ms = [None] * 4
                    for mt in range(4):
                        bU1 = P[rrb[0] % 6]
                        rrb[0] += 1
                        bU3 = P[rrb[0] % 6]
                        rrb[0] += 1
                        cu1 = cu3 = None
                        for kt in range(KT):
                            w0 = ([ms1[kt // 4], lnth_gl[gl]]
                                  if kt in (0, 4) else [])
                            cu1 = mm(bU1, bU1.v(),
                                     w1s(sl, 512 * kt + 128 * mt,
                                         512 * kt + 128 * mt + 128),
                                     lnTH(kt, 512 * gl, 512 * gl + 512),
                                     start=(kt == 0), stop=(kt == KT - 1),
                                     waits=w0,
                                     inc=("t" if kt == KT - 1 else None))
                        for kt in range(KT):
                            w0 = ([ms3[kt // 4]] if kt in (0, 4) else [])
                            cu3 = mm(bU3, bU3.v(),
                                     w3s(sl, 512 * kt + 128 * mt,
                                         512 * kt + 128 * mt + 128),
                                     lnTH(kt, 512 * gl, 512 * gl + 512),
                                     start=(kt == 0), stop=(kt == KT - 1),
                                     waits=w0,
                                     inc=("t" if kt == KT - 1 else None))
                        f0 = af(FSC0, 512)
                        f1 = af(FSC1, 512)
                        ct1_ = v_tt(f0, bU1.v(), wbs(epar, gl), ALU.mult,
                                    waits=[("t", cu1), wb_done, prev_csl])
                        csl = s_act(sls(mt % 2), f0, AF.Silu,
                                    waits=[("v", ct1_), sl_free[mt % 2]])
                        prev_csl = ("s", csl)
                        if ndrip > 0:
                            drip(cur)
                            ndrip -= 1
                        ct3_ = v_tt(f1, bU3.v(), wbs(epar, gl), ALU.mult,
                                    waits=[("t", cu3)])
                        bU1.free = ("v", ct1_)
                        bU3.free = ("v", ct3_)
                        lasth = v_tt(hTs(512 * mt, 512 * mt + 512),
                                     sls(mt % 2), f1, ALU.mult,
                                     waits=[("s", csl)] +
                                     (hT_ms if mt == 0 else []))
                        sl_free[mt % 2] = ("v", lasth)
                        h_ms[mt] = ("v", lasth)
                    cylast = None
                    for d in range(KT):
                        bY = P[rrb[0] % 6]
                        rrb[0] += 1
                        cy = None
                        for k in range(4):
                            w0 = [h_ms[k]] + \
                                ([ms2[k // 2]] if k in (0, 2) else [])
                            cy = mm(bY, bY.v(),
                                    w2s(1024 * k + 128 * d,
                                        1024 * k + 128 * d + 128),
                                    hTs(512 * k, 512 * k + 512),
                                    start=(k == 0), stop=(k == 3),
                                    waits=w0,
                                    inc=("t" if k == 3 else None))
                            cylast = cy
                        lastY = v_tt(
                            xT[:, 2048 * d + ts:2048 * d + ts + 512],
                            xT[:, 2048 * d + ts:2048 * d + ts + 512],
                            bY.v(), ALU.add,
                            waits=[("t", cy)], inc="v")
                        bY.free = ("v", lastY)
                        lastW2mm = cy
                    r_hT.freed(("t", cylast))
                wsF1[sl].freed(("t", lastW2mm))
                wsF3[sl].freed(("t", lastW2mm))
                wsF2.freed(("t", lastW2mm))
                if j + 1 < NJ:
                    drip(cur)
                    drip(cur)
                if s_ == E8 - 1:
                    r_wb[epar].freed(("v", B.cnt["v"]))
                if s_ == E8 - 1 and e == E8 - 1:
                    fscreg.freed(("v", B.cnt["v"]), ("s", B.cnt["s"]))
                    r_lnTH.freed(("t", lastW2mm))
            return ("v", lastY)

        def phase_attn_b(xw):
            B_LNT = CBASE
            B_QS = CBASE + 8192
            B_KS = CBASE + 9216
            B_VM = CBASE + 10240
            B_AAB = CBASE + 11264
            B_SQ = CBASE + 11392
            B_US = CBASE + 11392
            B_WT = CBASE + 11904
            B_WS = CBASE + 12416
            B_BIAS = CBASE + 13440
            assert B_BIAS + 4096 <= 24832
            lnT = lambda kt, a, b: ab(B_LNT, 16384)[:, 2048 * kt + a:2048 * kt + b]
            QS = lambda a, b: ab(B_QS, 2048)[:, a:b]
            KS = lambda a, b: ab(B_KS, 2048)[:, a:b]
            VM = lambda a, b: ab(B_VM, 2048)[:, a:b]
            aab = lambda i: ab(B_AAB + 64 * i, 128)
            uss = lambda i: ab(B_US + 256 * i, 512)
            B_WT2 = 23680
            B_FSC3 = 23168
            wts = lambda i: ab((B_WT + 256 * (i % 2)) if i < 2
                               else (B_WT2 + 256 * (i - 2)), 512)
            bfsc = lambda i: af([FSC0, FSC1, FSC2, B_FSC3][i], 512)
            wsp = lambda sl: ab(B_WS + 512 * sl, 1024)
            bias = lambda di, a, b: ab(B_BIAS, 8192)[:, 512 * di + a:512 * di + b]

            fence = [("t", B.cnt["t"]), ("v", B.cnt["v"]), ("s", B.cnt["s"])]
            r_lnT, r_qs, r_ks, r_vm, r_bias = Reg(), Reg(), Reg(), Reg(), Reg()
            stg2 = [Reg(), Reg()]
            stg2[0].freed(*(fence + (stg[0].free or [])))
            stg2[1].freed(*(fence + (stg[1].free or [])))
            r_us = [Reg(), Reg()]
            r_wt = [Reg(), Reg(), Reg(), Reg()]
            r_aab = [Reg(), Reg()]
            wsB = [Reg(), Reg()]
            for rg in (r_lnT, r_qs, r_ks, r_vm, r_bias, r_us[0], r_us[1],
                       r_wt[0], r_wt[1], r_wt[2], r_wt[3],
                       r_aab[0], r_aab[1], wsB[0], wsB[1]):
                rg.freed(*fence)

            ln_stats(0, N, B_SQ, xw)
            stm, str_ = stats_ms[0], stats_ms[1]
            ms_stats = ("t", B.cnt["t"])
            for rg in (r_us[0], r_us[1], r_wt[0], r_wt[1]):
                rg.freed(*(fence + [ms_stats]))
            lnT_ms = r_lnT.take() + fscreg.take()
            lastln = None
            for g in range(NG):
                ts = 512 * g
                bcast(P[2], row(R_M + ts, 512), 512, waits=[stm])
                crr = bcast(P[3], row(R_RSTD + ts, 512), 512,
                            waits=[str_], inc="t")
                for kt in range(KT):
                    f = af(FSC0, 512)
                    v_tt(f, xT[:, 2048 * kt + ts:2048 * kt + ts + 512],
                         P[2].t[:, 0:512], ALU.subtract,
                         waits=[("t", crr)] +
                         (lnT_ms if (g == 0 and kt == 0) else []))
                    lastln = v_tt(lnT(kt, ts, ts + 512), f,
                                  P[3].t[:, 0:512], ALU.mult)
                P[2].free = ("v", lastln)
                P[3].free = ("v", lastln)
            lnw = ("v", lastln)
            fscreg.freed(lnw)

            def proj_dma(w_hbm, h, sl):
                src3 = w_hbm[:, 128 * h:128 * h + 128] \
                    .rearrange("(a p) n -> p a n", p=128)
                qn = "dC" if sl == 0 else "dD"
                stga = af((STG0 if sl == 0 else STG1) + 1024, 1024)
                dc = B.dma(stga.rearrange("p (a n) -> p a n", a=8),
                           src3, waits=stg2[sl].take(), q=qn)
                return (qn, dc, stga)

            def proj_cast(dmainfo, sl, reg):
                qn, dc, stga = dmainfo
                cc = s_copy(wsp(sl), stga, waits=[(qn, dc)] + reg.take())
                stg2[sl].freed(("s", cc))
                return ("s", cc)

            lastxT = None
            for h in range(H8):
                dq = proj_dma(w_bq, h, 0)
                dk = proj_dma(w_bk, h, 1)
                bias_ms = r_bias.take()
                bias_dmas = {}
                bsrc = bias_exp[h].rearrange("(a p) n -> p a n", p=128)

                def bias_dma(i):
                    ssl = i % 2
                    qn = "dA" if ssl == 0 else "dB"
                    stga = af(STG0 if ssl == 0 else STG1, 1024)
                    dcb = B.dma(stga.rearrange("p (a n) -> p a n", a=2),
                                bsrc[:, 2 * i:2 * (i + 1), :],
                                waits=stg[ssl].take(), q=qn)
                    bias_dmas[i] = (qn, dcb, stga)
                bias_dma(7)
                bias_dma(6)
                cq = proj_cast(dq, 0, wsB[0])
                ck = proj_cast(dk, 1, wsB[1])
                qs_ms = r_qs.take()
                lastq = None
                for g in range(NG):
                    bank = P[0 + g % 2]
                    cgrp = None
                    for kt in range(KT):
                        cgrp = mm(bank, bank.v(),
                                  wsp(0)[:, 128 * kt:128 * kt + 128],
                                  lnT(kt, 512 * g, 512 * g + 512),
                                  start=(kt == 0), stop=(kt == KT - 1),
                                  waits=([cq, lnw] + (qs_ms if g == 0 else [])
                                         if kt == 0 else []),
                                  inc=("t" if kt == KT - 1 else None))
                    lastq = s_act(QS(512 * g, 512 * g + 512), bank.v(),
                                  AF.Silu, waits=[("t", cgrp)], inc="s")
                    bank.free = ("s", lastq)
                qs_done = ("s", lastq)
                wsB[0].freed(("t", cgrp))
                ks_ms = r_ks.take()
                lastk = None
                for g in range(NG):
                    bank = P[2 + g % 2]
                    cgrp = None
                    for kt in range(KT):
                        cgrp = mm(bank, bank.v(),
                                  wsp(1)[:, 128 * kt:128 * kt + 128],
                                  lnT(kt, 512 * g, 512 * g + 512),
                                  start=(kt == 0), stop=(kt == KT - 1),
                                  waits=([ck, lnw] + (ks_ms if g == 0 else [])
                                         if kt == 0 else []),
                                  inc=("t" if kt == KT - 1 else None))
                    lastk = s_act(KS(512 * g, 512 * g + 512), bank.v(),
                                  AF.Silu, waits=[("t", cgrp)], inc="s")
                    bank.free = ("s", lastk)
                ks_done = ("s", lastk)
                wsB[1].freed(("t", cgrp))
                dv_ = proj_dma(w_bv, h, 0)
                du_ = proj_dma(w_bu, h, 1)
                cv = proj_cast(dv_, 0, wsB[0])
                cu = proj_cast(du_, 1, wsB[1])
                for a in range(16):
                    par = a % 2
                    caa = s_act(aab(par), KS(128 * a, 128 * a + 128), AF.Abs,
                                waits=[ks_done] + r_aab[par].take())
                    cms_ = mm(P[6], P[6].t[:, 0:1], aab(par), onesc_b[:, :],
                              start=True, stop=True,
                              waits=[("s", caa), CMS], inc="t")
                    r_aab[par].freed(("t", cms_))
                    cmk = v_ts(cols[:, 32 + a:33 + a], P[6].t[:, 0:1], 0.0,
                               None, ALU.is_gt, waits=[("t", cms_)])
                    P[6].free = ("v", cmk)
                mk_done = ("v", cmk)
                vm_ms = r_vm.take()
                lastvm = None
                for m in range(16):
                    bank = P[4 + m % 2]
                    cgrp = None
                    for kt in range(KT):
                        cgrp = mm(bank, bank.v(0, 128),
                                  lnT(kt, 128 * m, 128 * m + 128),
                                  wsp(0)[:, 128 * kt:128 * kt + 128],
                                  start=(kt == 0), stop=(kt == KT - 1),
                                  waits=([cv, lnw] if kt == 0 else []),
                                  inc=("t" if kt == KT - 1 else None))
                    par = m % 2
                    csv = s_act(aab(par), bank.v(0, 128), AF.Silu,
                                waits=[("t", cgrp)] + r_aab[par].take())
                    lastvm = v_ts(VM(128 * m, 128 * m + 128), aab(par),
                                  cols[:, 32 + m:33 + m], None, ALU.mult,
                                  waits=[("s", csv), mk_done] +
                                  (vm_ms if m == 0 else []))
                    r_aab[par].freed(("v", lastvm))
                    bank.free = ("v", lastvm)
                vm_done = ("v", lastvm)
                wsB[0].freed(("t", cgrp))
                bias_cms = {}
                first_b = True
                for i in range(7, -1, -1):
                    qn, dcb, stga = bias_dmas[i]
                    cb = s_copy(ab(B_BIAS + 512 * i, 1024), stga,
                                waits=[(qn, dcb)] +
                                (bias_ms if first_b else []))
                    first_b = False
                    stg[0 if qn == "dA" else 1].freed(("s", cb))
                    bias_cms[i] = ("s", cb)
                    if i - 2 >= 0:
                        bias_dma(i - 2)
                for G in range(NG):
                    gp = G % 2
                    bank = P[6]
                    cgrp = None
                    for kt in range(KT):
                        cgrp = mm(bank, bank.v(),
                                  wsp(1)[:, 128 * kt:128 * kt + 128],
                                  lnT(kt, 512 * G, 512 * G + 512),
                                  start=(kt == 0), stop=(kt == KT - 1),
                                  waits=([cu, lnw] if kt == 0 else []),
                                  inc=("t" if kt == KT - 1 else None))
                    cus = s_act(uss(gp), bank.v(), AF.Silu,
                                waits=[("t", cgrp)] + r_us[gp].take(),
                                inc="s")
                    bank.free = ("s", cus)
                    if G == NG - 1:
                        wsB[1].freed(("t", cgrp))
                    bA = P[5]
                    for a in range(4 * G + 4):
                        par = a % 4
                        bS = P[par]
                        di = a - 4 * G + 12
                        mm(bS, bS.v(), KS(128 * a, 128 * a + 128),
                           QS(512 * G, 512 * G + 512),
                           start=True, stop=False,
                           waits=[qs_done, ks_done])
                        csc = mm(bS, bS.v(), idb[:, :], bias(di, 0, 512),
                                 start=False, stop=True,
                                 waits=[bias_cms[di // 2]], inc="t")
                        cwt = s_act(wts(par), bS.v(), AF.Silu,
                                    scale=INV_SQRT_HD,
                                    waits=[("t", csc)] + r_wt[par].take(),
                                    inc="s")
                        bS.free = ("s", cwt)
                        r = a - 4 * G
                        if r >= 0:
                            cwm2 = v_tt(wts(par), wts(par),
                                        bmask[:, 512 * r:512 * r + 512],
                                        ALU.mult, waits=[("s", cwt)])
                            wt_rdy = ("v", cwm2)
                        else:
                            wt_rdy = ("s", cwt)
                        catt = mm(bA, bA.v(), VM(128 * a, 128 * a + 128),
                                  wts(par),
                                  start=(a == 0), stop=(a == 4 * G + 3),
                                  waits=[wt_rdy] +
                                  ([vm_done] if a == 0 else []),
                                  inc="t")
                        r_wt[par].freed(("t", catt))
                    f2 = af(FSC2, 512)
                    cg1_ = v_tt(f2, bA.v(), uss(gp), ALU.mult,
                                waits=[("t", catt), ("s", cus)])
                    bA.free = ("v", cg1_)
                    r_us[gp].freed(("v", cg1_))
                    lastxT = v_tt(
                        xT[:, 2048 * h + 512 * G:2048 * h + 512 * G + 512],
                        xT[:, 2048 * h + 512 * G:2048 * h + 512 * G + 512],
                        f2, ALU.add, inc="v")
                r_bias.freed(("t", csc))
            r_lnT.freed(("t", B.cnt["t"]))
            for _i in range(2):
                stg[_i].free = (stg[_i].free or []) + (stg2[_i].free or [])
            return ("v", lastxT)

        if "attn_a" in BUILD_PHASES:
            xw = phase_attn_a(xw)
        if "ffn1" in BUILD_PHASES:
            xw = phase_ffn(xw)
        if "attn_b" in BUILD_PHASES:
            xw = phase_attn_b(xw)
        if "ffn2" in BUILD_PHASES:
            xw = phase_ffn(xw)

        out3 = out.rearrange("(a p) n -> p a n", p=128)
        dS0 = B.dma(out3[:, 0:4, :],
                    xT[:, 0:8192].rearrange("p (a n) -> p a n", a=4),
                    waits=[xw], q="dO")
        dS1 = B.dma(out3[:, 4:8, :],
                    xT[:, 8192:16384].rearrange("p (a n) -> p a n", a=4),
                    q="dO")
        B.wait("sync", "dO", dS1)

        with nc.Block() as block:
            @block.sync
            def _(En):
                for f in B.st["sync"]:
                    f(En)

            @block.tensor
            def _(En):
                for f in B.st["tensor"]:
                    f(En)

            @block.vector
            def _(En):
                for f in B.st["vector"]:
                    f(En)

            @block.scalar
            def _(En):
                for f in B.st["scalar"]:
                    f(En)

            @block.gpsimd
            def _(En):
                for f in B.st["gpsimd"]:
                    f(En)
    return nc


def _host_inputs(inputs):
    f32 = np.float32
    bf = ml_dtypes.bfloat16
    x = np.asarray(inputs["L0_attn_a_x"], f32)
    gk2pad = np.zeros((128, KD), f32)
    gk2pad[0:16] = np.asarray(inputs["L0_attn_a_Wgk2"], f32)
    gk2pad[32] = np.asarray(inputs["L0_attn_a_bgk2"], f32)
    rmsw2 = np.ascontiguousarray(
        np.asarray(inputs["L0_attn_a_rms_w"], f32).reshape(2, 128).T)
    rel = np.asarray(inputs["L0_attn_b_rel_bias"], f32)
    I = np.arange(512)[None, :] - np.arange(128)[:, None]
    idx = np.stack([I - 128 * (a - 12) + 2047 for a in range(16)])
    bias_exp = np.ascontiguousarray(
        rel[idx].transpose(3, 0, 1, 2).reshape(H8, 16 * 128, 512))
    bmask = np.concatenate(
        [(np.arange(512)[None, :] - 128 * r >= np.arange(128)[:, None])
         .astype(bf) for r in range(4)], axis=1)
    shared = {
        "w_aq": np.asarray(inputs["L0_attn_a_Wq"], f32),
        "w_ak": np.asarray(inputs["L0_attn_a_Wk"], f32),
        "w_av": np.asarray(inputs["L0_attn_a_Wv"], f32),
        "w_ag": np.asarray(inputs["L0_attn_a_Wg"], f32),
        "w_ao": np.asarray(inputs["L0_attn_a_Wout"], f32),
        "w_gk1": np.asarray(inputs["L0_attn_a_Wgk1"], f32),
        "gk2pad": gk2pad, "rmsw2": rmsw2,
        "w_bq": np.asarray(inputs["L0_attn_b_Wq"], f32),
        "w_bk": np.asarray(inputs["L0_attn_b_Wk"], f32),
        "w_bv": np.asarray(inputs["L0_attn_b_Wv"], f32),
        "w_bu": np.asarray(inputs["L0_attn_b_Wu"], f32),
        "bias_exp": bias_exp,
        "w_fg": np.asarray(inputs["L0_ffn_Wg"], f32),
        "bg_row": np.asarray(inputs["L0_ffn_bg"], f32).reshape(1, E8),
        "w_f1": np.asarray(inputs["L0_ffn_W1"], f32),
        "w_f2": np.asarray(inputs["L0_ffn_W2"], f32),
        "w_f3": np.asarray(inputs["L0_ffn_W3"], f32),
        "c_triu": np.triu(np.ones((128, 128), f32)),
        "c_triu4": np.ascontiguousarray(
            np.tile(np.triu(np.ones((128, 128), f32)), (1, 4))),
        "c_idf": np.eye(128, dtype=f32),
        "c_idb": np.eye(128, dtype=bf),
        "c_iota": np.ascontiguousarray(np.broadcast_to(
            np.arange(7, -1, -1, dtype=f32)[None, :], (128, E8))),
        "c_bmask": bmask,
    }
    in_maps = []
    for b in range(8):
        m = dict(shared)
        m["xin"] = np.ascontiguousarray(x[b].T)
        in_maps.append(m)
    return in_maps


def kernel(**inputs):
    in_maps = _host_inputs(inputs)
    nc = build_program()
    res = run_bass_kernel_spmd(nc, in_maps, core_ids=list(range(8)))
    outs = [np.ascontiguousarray(r["out"].T) for r in res.results]
    return np.stack(outs).astype(np.float32)


# version 13
# speedup vs baseline: 1.1417x; 1.1417x over previous
import math
from collections import defaultdict
from contextlib import ExitStack

import numpy as np
import ml_dtypes

import concourse.bass as bass
import concourse.mybir as mybir
from concourse.bass_utils import run_bass_kernel_spmd

F32 = mybir.dt.float32
BF16 = mybir.dt.bfloat16
AF = mybir.ActivationFunctionType
ALU = mybir.AluOpType
AX = mybir.AxisListType

D, KT, N, NG = 1024, 8, 2048, 4
EPS = 1e-5
GLN = 16.0
H4, DK, DV, KD, VD = 4, 128, 256, 512, 1024
H8, HD = 8, 128
INV_SQRT_HD = 1.0 / math.sqrt(128.0)
E8, HF = 8, 4096

BUILD_PHASES = ("attn_a", "ffn1", "attn_b", "ffn2")

ENGINES = ("sync", "tensor", "vector", "scalar", "gpsimd")


class Builder:
    def __init__(self, nc):
        self.nc = nc
        self.st = {e: [] for e in ENGINES}
        self.semh = {}
        self.cnt = defaultdict(int)
        self.lastwait = {}

    def sem(self, name, stack):
        self.semh[name] = stack.enter_context(self.nc.semaphore(name))

    def wait(self, eng, nm, val):
        if val is None or val <= 0:
            return
        key = (eng, nm)
        if self.lastwait.get(key, -1) >= val:
            return
        self.lastwait[key] = val
        h = self.semh[nm]
        self.st[eng].append(lambda E, h=h, v=val: E.wait_ge(h, v))

    AUTO_SELF = {"vector": "v", "scalar": "s", "gpsimd": "g"}

    def emit(self, eng, f, waits=(), inc=None, k=1):
        own = self.AUTO_SELF.get(eng)
        if own is not None and self.cnt[own] > 0:
            self.wait(eng, own, self.cnt[own])
        for wt in waits:
            if wt is None:
                continue
            if isinstance(wt, list):
                for w2 in wt:
                    if w2 is not None:
                        self.wait(eng, w2[0], w2[1])
            else:
                self.wait(eng, wt[0], wt[1])
        if inc is not None:
            self.cnt[inc] += k
            h = self.semh[inc]
            self.st[eng].append(lambda E, f=f, h=h, kk=k: f(E).then_inc(h, kk))
            return self.cnt[inc]
        self.st[eng].append(f)
        return None

    def dma(self, out_ap, in_ap, waits=(), q="dX"):
        return self.emit(
            "sync", lambda E, o=out_ap, i=in_ap: E.dma_start(out=o, in_=i),
            waits=waits, inc=q, k=16)


class Bank:
    def __init__(self, t):
        self.t = t
        self.free = None

    def v(self, lo=0, hi=512):
        return self.t[:, lo:hi]


class Reg:
    def __init__(self):
        self.free = None

    def take(self):
        w, self.free = self.free, None
        if w is None:
            return []
        return w if isinstance(w, list) else [w]

    def freed(self, *ms):
        self.free = [m for m in ms if m is not None]


def build_program():
    nc = bass.Bass("TRN2", target_bir_lowering=False)
    B = Builder(nc)

    def din(name, shape, dt=F32):
        return nc.dram_tensor(name, shape, dt, kind="ExternalInput").ap()

    xin = din("xin", [D, N])
    out = nc.dram_tensor("out", [D, N], F32, kind="ExternalOutput").ap()
    w_aq = din("w_aq", [D, KD]); w_ak = din("w_ak", [D, KD])
    w_av = din("w_av", [D, VD]); w_ag = din("w_ag", [D, VD])
    w_ao = din("w_ao", [VD, D])
    w_gk1 = din("w_gk1", [D, 16])
    gk2pad = din("gk2pad", [128, KD])
    rmsw2 = din("rmsw2", [128, 2])
    w_bq = din("w_bq", [D, D]); w_bk = din("w_bk", [D, D])
    w_bv = din("w_bv", [D, D]); w_bu = din("w_bu", [D, D])
    bias_exp = din("bias_exp", [H8, 16 * 128, 512])
    w_fg = din("w_fg", [D, E8])
    bg_row = din("bg_row", [1, E8])
    w_f1 = din("w_f1", [E8, D, HF]); w_f2 = din("w_f2", [E8, HF, D])
    w_f3 = din("w_f3", [E8, D, HF])
    c_triu = din("c_triu", [128, 128])
    c_idf = din("c_idf", [128, 128])
    c_idb = din("c_idb", [128, 128], BF16)
    c_iota = din("c_iota", [128, E8])
    c_bmask = din("c_bmask", [128, 4 * 512], BF16)

    with ExitStack() as stack:
        E = stack.enter_context
        for s in ("dX", "dA", "dB", "dC", "dD", "dO", "t", "v", "s", "g"):
            B.sem(s, stack)

        xT = E(nc.sbuf_tensor("xT", [128, KT * N], F32))
        arena = E(nc.sbuf_tensor("arena", [128, 24832], F32))
        rows = E(nc.sbuf_tensor("rows", [1, 6144], F32))
        cols = E(nc.sbuf_tensor("cols", [128, 160], F32))
        onesr_f = E(nc.sbuf_tensor("onesr_f", [1, 128], F32))
        onesc_f = E(nc.sbuf_tensor("onesc_f", [128, 1], F32))
        onesc_b = E(nc.sbuf_tensor("onesc_b", [128, 1], BF16))
        t1ext = E(nc.sbuf_tensor("t1ext", [128, N], BF16))
        S32 = E(nc.sbuf_tensor("S32", [128, H4 * DV], F32))
        Sbf = E(nc.sbuf_tensor("Sbf", [128, H4 * DV], BF16))
        triu = E(nc.sbuf_tensor("triu", [128, 128], F32))
        idf = E(nc.sbuf_tensor("idf", [128, 128], F32))
        idb = E(nc.sbuf_tensor("idb", [128, 128], BF16))
        iota = E(nc.sbuf_tensor("iota", [128, E8], F32))
        bmask = E(nc.sbuf_tensor("bmask", [128, 4 * 512], BF16))
        gk2b = E(nc.sbuf_tensor("gk2b", [128, KD], BF16))
        gk1b = E(nc.sbuf_tensor("gk1b", [128, KT * 16], BF16))
        wgf = E(nc.sbuf_tensor("wgf", [128, KT * E8], F32))
        bgr = E(nc.sbuf_tensor("bgr", [1, E8], F32))
        sbg = E(nc.sbuf_tensor("sbg", [128, 2 * E8], F32))

        P = [Bank(E(nc.psum_tensor(f"p{i}", [128, 512], F32))) for i in range(7)]
        PB = Bank(E(nc.psum_tensor("pb", [128, 1024], BF16)))

        R_M, R_RSTD, R_SCRD, R_T0, R_T1 = 0, 2048, 4096, 5120, 5632

        def row(off, n):
            return rows[0:1, off:off + n]

        def af(off, n):
            return arena[:, off:off + n]

        def ab(off_f32, nbf):
            return arena[:, off_f32:off_f32 + (nbf + 1) // 2] \
                .bitcast(BF16)[:, 0:nbf]

        STG0, STG1 = 0, 2048
        FSC0, FSC1, FSC2 = 4096, 4608, 5120
        CBASE = 5632

        def v_tt(o, a, b, op, waits=(), inc="v"):
            return B.emit("vector",
                          lambda En, oo=o, x=a, y=b, p=op:
                          En.tensor_tensor(out=oo, in0=x, in1=y, op=p),
                          waits=waits, inc=inc)

        def v_ts(o, a, s1, s2, op0, op1=None, waits=(), inc="v"):
            if op1 is None:
                return B.emit("vector",
                              lambda En, oo=o, x=a, aa=s1, p=op0:
                              En.tensor_single_scalar(out=oo, in_=x, scalar=aa,
                                                      op=p),
                              waits=waits, inc=inc)
            return B.emit("vector",
                          lambda En, oo=o, x=a, aa=s1, bb=s2, p0=op0, p1=op1:
                          En.tensor_scalar(out=oo, in0=x, scalar1=aa,
                                           scalar2=bb, op0=p0, op1=p1),
                          waits=waits, inc=inc)

        def v_stt(o, a, s, b, op0, op1, waits=(), inc="v"):
            return B.emit("vector",
                          lambda En, oo=o, x=a, ss=s, y=b, p0=op0, p1=op1:
                          En.scalar_tensor_tensor(out=oo, in0=x, scalar=ss,
                                                  in1=y, op0=p0, op1=p1),
                          waits=waits, inc=inc)

        def v_copy(o, a, waits=(), inc="v"):
            return B.emit("vector",
                          lambda En, oo=o, x=a: En.tensor_copy(out=oo, in_=x),
                          waits=waits, inc=inc)

        def g_copy(o, a, waits=(), inc="g"):
            return B.emit("gpsimd",
                          lambda En, oo=o, x=a: En.tensor_copy(out=oo, in_=x),
                          waits=waits, inc=inc)

        def v_red(o, a, op, waits=(), inc="v"):
            return B.emit("vector",
                          lambda En, oo=o, x=a, p=op:
                          En.tensor_reduce(out=oo, in_=x, axis=AX.X, op=p),
                          waits=waits, inc=inc)

        def v_recip(o, a, waits=(), inc="v"):
            return B.emit("vector",
                          lambda En, oo=o, x=a: En.reciprocal(out=oo, in_=x),
                          waits=waits, inc=inc)

        def v_memset(ap, c, waits=(), inc="v"):
            return B.emit("vector", lambda En, a=ap, cc=c: En.memset(a, cc),
                          waits=waits, inc=inc)

        def s_act(o, a, func, bias=None, scale=None, waits=(), inc="s"):
            kw = {}
            if bias is not None:
                kw["bias"] = bias
            if scale is not None:
                kw["scale"] = scale
            return B.emit("scalar",
                          lambda En, oo=o, x=a, f=func, k=dict(kw):
                          En.activation(out=oo, in_=x, func=f, **k),
                          waits=waits, inc=inc)

        def s_copy(o, a, waits=(), inc="s"):
            return B.emit("scalar",
                          lambda En, oo=o, x=a: En.copy(out=oo, in_=x),
                          waits=waits, inc=inc)

        def mm(bank, o, lhsT, rhs, start, stop, waits=(), inc=None):
            w = list(waits)
            if start and bank.free is not None:
                w.append(bank.free)
                bank.free = None
            return B.emit("tensor",
                          lambda En, oo=o, l=lhsT, r=rhs, st=start, sp=stop:
                          En.matmul(oo, l, r, start=st, stop=sp),
                          waits=w, inc=inc)

        def mm_tr(bank, o, a, ident, waits=(), inc=None):
            w = list(waits)
            if bank.free is not None:
                w.append(bank.free)
                bank.free = None
            return B.emit("tensor",
                          lambda En, oo=o, x=a, i=ident: En.transpose(oo, x, i),
                          waits=w, inc=inc)

        stg = [Reg(), Reg()]

        xin3 = xin.rearrange("(a p) n -> p a n", p=128)
        B.dma(xT[:, 0:8192].rearrange("p (a n) -> p a n", a=4),
              xin3[:, 0:4, :])
        d_x1 = B.dma(xT[:, 8192:16384].rearrange("p (a n) -> p a n", a=4),
                     xin3[:, 4:8, :])
        B.dma(triu[:, :], c_triu)
        B.dma(idf[:, :], c_idf)
        d_idb = B.dma(idb[:, :], c_idb)
        d_io = B.dma(iota[:, :], c_iota)
        d_bm = B.dma(bmask[:, :], c_bmask)
        d_wgf = B.dma(wgf[:, :].rearrange("p (a n) -> p a n", a=8),
                      w_fg.rearrange("(a p) n -> p a n", p=128))
        d_bgr = B.dma(bgr[0:1, :], bg_row)
        d_rms = B.dma(cols[:, 0:2], rmsw2)
        v_memset(onesr_f[0:1, :], 1.0)
        v_memset(onesc_f[:, :], 1.0)
        v_memset(onesc_b[:, :], 1.0)
        cmz = v_memset(t1ext[:, :], 0.0)
        v_memset(t1ext[32:33, :], 1.0, waits=[("v", cmz)])
        v_memset(S32[:, :], 0.0)
        cms = v_memset(Sbf[:, :], 0.0)
        CMS = ("v", cms)
        dg2 = B.dma(af(STG0, 512), gk2pad)
        dg1 = B.dma(af(STG0 + 512, 128).rearrange("p (a n) -> p a n", a=8),
                    w_gk1.rearrange("(a p) n -> p a n", p=128))
        DXA = ("dX", dg1)
        cg2 = s_copy(gk2b[:, :], af(STG0, 512), waits=[DXA])
        cg1 = s_copy(gk1b[:, :], af(STG0 + 512, 128), waits=[DXA])
        stg[0].freed(("s", cg1))
        csm = None
        for kt in range(KT):
            csm = mm(P[2], P[2].t[0:1, 0:E8], onesc_f[:, :],
                     wgf[:, E8 * kt:E8 * kt + E8],
                     start=(kt == 0), stop=(kt == KT - 1),
                     waits=([DXA, CMS] if kt == 0 else []),
                     inc=("t" if kt == KT - 1 else None))
        cS = v_copy(row(R_T0, E8), P[2].t[0:1, 0:E8], waits=[("t", csm)])
        P[2].free = ("v", cS)
        c_sb = mm(P[3], P[3].t[:, 0:E8], onesr_f[0:1, :], row(R_T0, E8),
                  start=True, stop=True, waits=[("v", cS)], inc="t")
        cS2 = v_copy(sbg[:, 0:E8], P[3].t[:, 0:E8], waits=[("t", c_sb)])
        P[3].free = ("v", cS2)
        c_bg = mm(P[4], P[4].t[:, 0:E8], onesr_f[0:1, :], bgr[0:1, :],
                  start=True, stop=True,
                  waits=[DXA, CMS], inc="t")
        cS3 = v_copy(sbg[:, E8:2 * E8], P[4].t[:, 0:E8], waits=[("t", c_bg)])
        P[4].free = ("v", cS3)

        xw = DXA

        sqslot = Reg()
        rowreg = Reg()
        fscreg = Reg()
        stats_ms = [None, None]

        def ln_stats(t0, ntok, sq_off, xw):
            for gg in range(ntok // 512):
                ts = t0 + 512 * gg
                b0, b1 = P[0], P[1]
                sq_ms = sqslot.take()
                mml = [None] * KT
                for kt in range(KT):
                    sl = kt % 2
                    sq = ab(sq_off + sl * 256, 512)
                    xb = ab(sq_off + 512 + sl * 256, 512)
                    wts = [xw] + (sq_ms if kt < 2 else
                                  [("t", mml[kt - 2])])
                    s_act(sq, xT[:, 2048 * kt + ts:2048 * kt + ts + 512],
                          AF.Square, waits=wts)
                    c2 = s_copy(xb, xT[:, 2048 * kt + ts:2048 * kt + ts + 512])
                    mm(b0, b0.t[0:1, 0:512], onesc_b[:, :], xb,
                       start=(kt == 0), stop=(kt == KT - 1),
                       waits=[("s", c2)] + ([CMS] if kt == 0 else []))
                    mml[kt] = mm(b1, b1.t[0:1, 0:512], onesc_b[:, :], sq,
                                 start=(kt == 0), stop=(kt == KT - 1), inc="t")
                lastmm = mml[KT - 1]
                sqslot.freed(("t", lastmm))
                rw = rowreg.take()
                mrow = row(R_M + ts, 512)
                rrow = row(R_RSTD + ts, 512)
                t0r = row(R_T0, 512)
                t1r = row(R_T1, 512)
                c = v_ts(mrow, b0.t[0:1, 0:512], 1.0 / 1024.0, None, ALU.mult,
                         waits=[("t", lastmm)] + rw)
                c2 = v_ts(t0r, b1.t[0:1, 0:512], 1.0 / 1024.0, EPS,
                          ALU.mult, ALU.add)
                b0.free = ("v", c)
                b1.free = ("v", c2)
                c3 = s_act(t1r, mrow, AF.Square, waits=[("v", c2)] + rw)
                c4 = v_tt(t0r, t0r, t1r, ALU.subtract, waits=[("s", c3)])
                c5 = s_act(t1r, t0r, AF.Ln, waits=[("v", c4)])
                c6 = s_act(rrow, t1r, AF.Exp, scale=-0.5, inc="s")
                stats_ms[0] = ("v", c)
                stats_ms[1] = ("s", c6)

        def bcast(bank, rowap, n, waits=(), inc=None):
            return mm(bank, bank.t[:, 0:n], onesr_f[0:1, :], rowap,
                      start=True, stop=True, waits=list(waits) + [CMS], inc=inc)

        def phase_attn_a(xw):
            A_GG = CBASE
            A_LNT = CBASE + 2048
            A_QK = CBASE + 4096
            A_V = CBASE + 6144
            A_GS = CBASE + 8192
            A_WS0 = CBASE + 10240
            A_WS1 = CBASE + 12288
            A_SC = CBASE + 14336
            A_SQ = CBASE + 15360
            A_SSC = CBASE + 16384

            lnT = lambda kt, a, b: ab(A_LNT, 4096)[:, 512 * kt + a:512 * kt + b]
            oT = lambda r, a, b: ab(A_LNT, 4096)[:, 512 * r + a:512 * r + b]
            qTs = lambda a, b: ab(A_QK, 2048)[:, a:b]
            kTs = lambda a, b: ab(A_QK + 1024, 2048)[:, a:b]
            ogd = lambda r: ab(A_QK, 4096)[:, 512 * r:512 * r + 512]
            vv = lambda a, b: ab(A_V, 4096)[:, a:b]
            Gs = lambda r: ab(A_GS, 4096)[:, 512 * r:512 * r + 512]
            gg = lambda a, b: af(A_GG, 2048)[:, a:b]
            wsl = [lambda a, b: ab(A_WS0, 4096)[:, a:b],
                   lambda a, b: ab(A_WS1, 4096)[:, a:b]]
            sc = lambda i: ab(A_SC + 64 * i, 128)

            ws = [Reg(), Reg()]
            r_lnT, r_qk, r_v, r_gs, r_gg = Reg(), Reg(), Reg(), Reg(), Reg()
            edec_free = [None, None]
            sbf_ready = [CMS] * H4
            scan_free = {}

            def stream_w2(w_hbm, col0, sl, rms_scale=False):
                src3 = w_hbm[:, col0:col0 + 512] \
                    .rearrange("(a p) n -> p a n", p=128)
                slot_ms = ws[sl].take()
                outms = []
                for i in range(2):
                    ssl = i % 2
                    qn = "dA" if ssl == 0 else "dB"
                    stga = af(STG0 if ssl == 0 else STG1, 2048)
                    dc = B.dma(stga.rearrange("p (a n) -> p a n", a=4),
                               src3[:, 4 * i:4 * (i + 1), :],
                               waits=stg[ssl].take(), q=qn)
                    wts = [(qn, dc)] + (slot_ms if i == 0 else [])
                    if not rms_scale:
                        cc = s_copy(wsl[sl](2048 * i, 2048 * (i + 1)), stga,
                                    waits=wts)
                        stg[ssl].freed(("s", cc))
                        outms.append(("s", cc))
                    else:
                        cc = None
                        for j in range(4):
                            kt = 4 * i + j
                            cc = v_ts(wsl[sl](2048 * i + 512 * j,
                                              2048 * i + 512 * j + 512),
                                      stga[:, 512 * j:512 * j + 512],
                                      cols[:, kt % 2:kt % 2 + 1], None,
                                      ALU.mult,
                                      waits=(wts if j == 0 else []) +
                                      ([DXA] if j == 0 else []))
                        stg[ssl].freed(("v", cc))
                        outms.append(("v", cc))
                return outms

            for gt in range(NG):
                ts = 512 * gt
                ln_stats(ts, 512, A_SQ, xw)
                bcast(P[2], row(R_M + ts, 512), 512, waits=[stats_ms[0]])
                cr_ = bcast(P[3], row(R_RSTD + ts, 512), 512,
                            waits=[stats_ms[1]], inc="t")
                lnw_ms = r_lnT.take()
                lastln = None
                for kt in range(KT):
                    f = af(FSC0, 512)
                    v_tt(f, xT[:, 2048 * kt + ts:2048 * kt + ts + 512],
                         P[2].t[:, 0:512], ALU.subtract,
                         waits=[("t", cr_)] +
                         ((lnw_ms + fscreg.take()) if kt == 0 else []))
                    lastln = v_tt(lnT(kt, 0, 512), f, P[3].t[:, 0:512],
                                  ALU.mult)
                P[2].free = ("v", lastln)
                P[3].free = ("v", lastln)
                lnw = ("v", lastln)
                fscreg.freed(lnw)

                qk_ms = r_qk.take()
                msq = stream_w2(w_aq, 0, 0)
                lastq = None
                for mt in range(4):
                    bank = P[mt % 2]
                    cgrp = None
                    for kt in range(KT):
                        pw = [msq[kt // 4], lnw] + (qk_ms if mt == 0 else [])
                        cgrp = mm(bank, bank.v(),
                                  wsl[0](512 * kt + 128 * mt,
                                         512 * kt + 128 * mt + 128),
                                  lnT(kt, 0, 512),
                                  start=(kt == 0), stop=(kt == KT - 1),
                                  waits=(pw if kt in (0, 4) else []),
                                  inc=("t" if kt == KT - 1 else None))
                    lastq = s_copy(qTs(512 * mt, 512 * mt + 512), bank.v(),
                                   waits=[("t", cgrp)], inc="s")
                    bank.free = ("s", lastq)
                    qk_t = cgrp
                msk = stream_w2(w_ak, 0, 1)
                lastk = None
                for mt in range(4):
                    bank = P[2 + mt % 2]
                    cgrp = None
                    for kt in range(KT):
                        pw = [msk[kt // 4], lnw]
                        cgrp = mm(bank, bank.v(),
                                  wsl[1](512 * kt + 128 * mt,
                                         512 * kt + 128 * mt + 128),
                                  lnT(kt, 0, 512),
                                  start=(kt == 0), stop=(kt == KT - 1),
                                  waits=(pw if kt in (0, 4) else []),
                                  inc=("t" if kt == KT - 1 else None))
                    lastk = s_copy(kTs(512 * mt, 512 * mt + 512), bank.v(),
                                   waits=[("t", cgrp)], inc="s")
                    bank.free = ("s", lastk)
                ws[0].freed(("t", qk_t))
                ws[1].freed(("t", cgrp))
                qk_done = ("s", lastk)
                v_ms = r_v.take()
                lastv = None
                for vh in range(2):
                    sl = vh
                    msv = stream_w2(w_av, 512 * vh, sl)
                    cgrp = None
                    for tm in range(4):
                        bank = P[(4 * vh + tm) % 2]
                        for kt in range(KT):
                            pw = [msv[kt // 4], lnw] + \
                                (v_ms if (vh == 0 and tm == 0) else [])
                            cgrp = mm(bank, bank.v(),
                                      lnT(kt, 128 * tm, 128 * tm + 128),
                                      wsl[sl](512 * kt, 512 * kt + 512),
                                      start=(kt == 0), stop=(kt == KT - 1),
                                      waits=(pw if kt in (0, 4) else []),
                                      inc=("t" if kt == KT - 1 else None))
                        lastv = s_copy(vv(1024 * tm + 512 * vh,
                                          1024 * tm + 512 * vh + 512),
                                       bank.v(), waits=[("t", cgrp)], inc="s")
                        bank.free = ("s", lastv)
                    ws[sl].freed(("t", cgrp))
                v_done = ("s", lastv)
                gs_ms = r_gs.take()
                lastg = None
                for vh in range(2):
                    sl = vh
                    msg = stream_w2(w_ag, 512 * vh, sl)
                    cgrp = None
                    for mt in range(4):
                        r = 4 * vh + mt
                        bank = P[2 + r % 2]
                        for kt in range(KT):
                            pw = [msg[kt // 4], lnw] + (gs_ms if r == 0 else [])
                            cgrp = mm(bank, bank.v(),
                                      wsl[sl](512 * kt + 128 * mt,
                                              512 * kt + 128 * mt + 128),
                                      lnT(kt, 0, 512),
                                      start=(kt == 0), stop=(kt == KT - 1),
                                      waits=(pw if kt in (0, 4) else []),
                                      inc=("t" if kt == KT - 1 else None))
                        lastg = s_act(Gs(r), bank.v(), AF.Silu,
                                      waits=[("t", cgrp)], inc="s")
                        bank.free = ("s", lastg)
                    ws[sl].freed(("t", cgrp))
                gs_done = ("s", lastg)
                bank = P[4]
                ct1m = None
                for kt in range(KT):
                    ct1m = mm(bank, bank.t[0:16, 0:512],
                              gk1b[:, 16 * kt:16 * kt + 16],
                              lnT(kt, 0, 512),
                              start=(kt == 0), stop=(kt == KT - 1),
                              waits=([("s", cg1), lnw, CMS] if kt == 0 else []),
                              inc=("t" if kt == KT - 1 else None))
                ct1 = s_copy(t1ext[0:16, ts:ts + 512], bank.t[0:16, 0:512],
                             waits=[("t", ct1m)], inc="s")
                bank.free = ("s", ct1)
                gg_ms = r_gg.take()
                lastgg = None
                fsc_ms = fscreg.take()
                for tm in range(4):
                    bank = P[5 + tm % 2]
                    cz = mm(bank, bank.v(),
                            t1ext[:, ts + 128 * tm:ts + 128 * tm + 128],
                            gk2b[:, :], start=True, stop=True,
                            waits=[("s", ct1), ("s", cg2)], inc="t")
                    f0, f1, f2 = af(FSC0, 512), af(FSC1, 512), af(FSC2, 512)
                    ca = s_act(f0, bank.v(), AF.Abs,
                               waits=[("t", cz)] + fsc_ms)
                    cmin = v_ts(f1, bank.v(), 0.0, None, ALU.min,
                                waits=[("s", ca)] + (gg_ms if tm == 0 else []))
                    bank.free = ("v", cmin)
                    s_act(f2, f0, AF.Exp, scale=-1.0)
                    cl = s_act(f0, f2, AF.Ln, bias=1.0)
                    cs_ = v_tt(f1, f1, f0, ALU.subtract, waits=[("s", cl)])
                    fsc_ms = [("v", cs_)]
                    lastgg = v_ts(gg(512 * tm, 512 * tm + 512), f1,
                                  1.0 / GLN, -3.0, ALU.mult, ALU.max)
                gg_done = ("v", lastgg)
                fscreg.freed(("v", lastgg), ("s", cl))

                last_co1 = None
                for ci in range(4):
                    tm = ci
                    for hd in range(H4):
                        par = hd % 2
                        qs, ks = sc(0 + par), sc(2 + par)
                        ptm, ktl = sc(4 + par), sc(6 + par)
                        eb, en = sc(8 + par), sc(10 + par)
                        bBT = P[0]
                        cbt = mm(bBT, bBT.v(0, 128),
                                 gg(512 * tm + 128 * hd,
                                    512 * tm + 128 * hd + 128),
                                 triu[:, :], start=True, stop=True,
                                 waits=[gg_done], inc="t")
                        ceb = s_act(eb, bBT.v(0, 128), AF.Exp,
                                    waits=[("t", cbt),
                                           scan_free.get(("eb", par))])
                        cen = s_act(en, bBT.v(0, 128), AF.Exp, scale=-1.0,
                                    waits=[scan_free.get(("en", par))])
                        ced = s_act(cols[:, 2 + par:3 + par],
                                    bBT.t[:, 127:128], AF.Exp,
                                    waits=[edec_free[par]], inc="s")
                        bBT.free = ("s", ced)
                        cqs = v_tt(qs, qTs(512 * hd + 128 * tm,
                                           512 * hd + 128 * tm + 128),
                                   eb, ALU.mult,
                                   waits=[("s", cen), qk_done,
                                          scan_free.get(("qs", par))])
                        cks = v_tt(ks, kTs(512 * hd + 128 * tm,
                                           512 * hd + 128 * tm + 128),
                                   en, ALU.mult,
                                   waits=[scan_free.get(("ks", par))])
                        scan_free[("eb", par)] = ("v", cqs)
                        scan_free[("en", par)] = ("v", cks)
                        bPT = P[1 + par]
                        cpt = mm(bPT, bPT.v(0, 128), ks, qs,
                                 start=True, stop=True,
                                 waits=[("v", cks)], inc="t")
                        cptm = v_tt(ptm, bPT.v(0, 128), triu[:, :], ALU.mult,
                                    waits=[("t", cpt),
                                           scan_free.get(("ptm", par))])
                        bPT.free = ("v", cptm)
                        bO = P[3 + par]
                        cot = None
                        for u in range(2):
                            mm(bO, bO.v(128 * u, 128 * u + 128),
                               vv(1024 * tm + 256 * hd + 128 * u,
                                  1024 * tm + 256 * hd + 128 * u + 128),
                               ptm, start=True, stop=False,
                               waits=([("v", cptm), v_done] if u == 0 else []))
                            cot = mm(bO, bO.v(128 * u, 128 * u + 128),
                                     Sbf[:, 256 * hd + 128 * u:
                                         256 * hd + 128 * u + 128],
                                     qs, start=False, stop=True,
                                     waits=[sbf_ready[hd]],
                                     inc=("t" if u == 1 else None))
                        scan_free[("ptm", par)] = ("t", cot)
                        scan_free[("qs", par)] = ("t", cot)
                        s_copy(oT(2 * hd, 128 * tm, 128 * tm + 128),
                               bO.v(0, 128), waits=[("t", cot)])
                        last_co1 = s_copy(oT(2 * hd + 1, 128 * tm,
                                             128 * tm + 128),
                                          bO.v(128, 256), inc="s")
                        bO.free = ("s", last_co1)
                        ctr = mm_tr(PB, PB.t[:, 0:128], ks, idb[:, :],
                                    waits=[("v", cks), DXA],
                                    inc="t")
                        scan_free[("ks", par)] = ("t", ctr)
                        ckt = v_copy(ktl, PB.t[:, 0:128],
                                     waits=[("t", ctr),
                                            scan_free.get(("ktl", par))])
                        PB.free = ("v", ckt)
                        bDS = P[5 + par]
                        cds = mm(bDS, bDS.v(0, 256), ktl,
                                 vv(1024 * tm + 256 * hd,
                                    1024 * tm + 256 * hd + 256),
                                 start=True, stop=True,
                                 waits=[("v", ckt)], inc="t")
                        scan_free[("ktl", par)] = ("t", cds)
                        ssc = af(A_SSC, 256)
                        ca1 = v_tt(ssc, S32[:, 256 * hd:256 * hd + 256],
                                   bDS.v(0, 256), ALU.add,
                                   waits=[("t", cds), ("s", ced)])
                        bDS.free = ("v", ca1)
                        ca2 = v_ts(S32[:, 256 * hd:256 * hd + 256], ssc,
                                   cols[:, 2 + par:3 + par], None, ALU.mult)
                        edec_free[par] = ("v", ca2)
                        csb = s_copy(Sbf[:, 256 * hd:256 * hd + 256],
                                     S32[:, 256 * hd:256 * hd + 256],
                                     waits=[("v", ca2)], inc="s")
                        sbf_ready[hd] = ("s", csb)
                ot_done = ("s", B.cnt["s"])

                prev_cbb = None
                last_ogd = None
                last_sq = None
                cstat = None
                prev_cstat = None
                for hd in range(H4):
                    bR = P[hd % 2]
                    for rr in range(2):
                        sq = ab(A_SQ + 256 * rr, 512)
                        csq = s_act(sq, oT(2 * hd + rr, 0, 512), AF.Square,
                                    waits=[ot_done, prev_cstat] +
                                    (sqslot.take() if (hd == 0 and rr == 0)
                                     else []))
                        cstat = mm(bR, bR.t[0:1, 0:512], onesc_b[:, :], sq,
                                   start=(rr == 0), stop=(rr == 1),
                                   waits=[("s", csq)],
                                   inc=("t" if rr == 1 else None))
                        last_sq = csq
                    prev_cstat = ("t", cstat)
                    cv0 = v_ts(row(R_T0, 512), bR.t[0:1, 0:512],
                               1.0 / 256.0, EPS, ALU.mult, ALU.add,
                               waits=[("t", cstat), prev_cbb])
                    bR.free = ("v", cv0)
                    cl1 = s_act(row(R_T1, 512), row(R_T0, 512), AF.Ln,
                                waits=[("v", cv0)])
                    cl2 = s_act(row(R_T0, 512), row(R_T1, 512), AF.Exp,
                                scale=-0.5, inc="s")
                    bB = P[2 + hd % 2]
                    cbb = bcast(bB, row(R_T0, 512), 512,
                                waits=[("s", cl2)], inc="t")
                    prev_cbb = ("t", cbb)
                    for u in range(2):
                        r = 2 * hd + u
                        f = af(FSC0 if u == 0 else FSC1, 512)
                        v_tt(f, oT(r, 0, 512), bB.v(), ALU.mult,
                             waits=[("t", cbb)] +
                             (([gs_done] + fscreg.take())
                              if (hd == 0 and u == 0) else []))
                        last_ogd = v_tt(ogd(r), f, Gs(r), ALU.mult)
                    bB.free = ("v", last_ogd)
                ogd_done = ("v", last_ogd)
                fscreg.freed(("v", last_ogd))
                sqslot.freed(("t", cstat))
                rowreg.freed(("t", prev_cbb[1]))

                lastY = None
                for dh in range(2):
                    sl = dh
                    msw = stream_w2(w_ao, 512 * dh, sl, rms_scale=True)
                    cgrp = None
                    for dm in range(4):
                        d_glob = 4 * dh + dm
                        bank = P[4 + dm % 2]
                        for vk in range(KT):
                            cgrp = mm(bank, bank.v(),
                                      wsl[sl](512 * vk + 128 * dm,
                                              512 * vk + 128 * dm + 128),
                                      ogd(vk),
                                      start=(vk == 0), stop=(vk == KT - 1),
                                      waits=([msw[vk // 4], ogd_done]
                                             if vk in (0, 4) else []),
                                      inc=("t" if vk == KT - 1 else None))
                        cY = v_tt(xT[:, 2048 * d_glob + ts:
                                     2048 * d_glob + ts + 512],
                                  xT[:, 2048 * d_glob + ts:
                                     2048 * d_glob + ts + 512],
                                  bank.v(), ALU.add,
                                  waits=[("t", cgrp)], inc="v")
                        bank.free = ("v", cY)
                        lastY = cY
                    ws[sl].freed(("t", cgrp))
                xw = ("v", lastY)

                r_lnT.freed(("v", lastY), ("s", last_sq))
                r_qk.freed(("t", B.cnt["t"]))
                r_v.freed(("t", B.cnt["t"]))
                r_gs.freed(("v", last_ogd))
                r_gg.freed(("t", B.cnt["t"]))
            return xw

        def phase_ffn(xw):
            F_LNTH = CBASE
            F_W1 = CBASE + 4096
            F_W3 = CBASE + 8192
            F_W2 = CBASE + 12288
            F_HT = CBASE + 14336
            F_WB = CBASE + 15360
            F_SL = CBASE + 15872
            F_WEM = CBASE + 16384
            assert F_WEM + 2048 <= 24832
            lnTH = lambda kt, a, b: ab(F_LNTH, 8192)[:, 1024 * kt + a:1024 * kt + b]
            w1s = lambda sl, a, b: ab(F_W1 + 2048 * sl, 4096)[:, a:b]
            w3s = lambda sl, a, b: ab(F_W3 + 2048 * sl, 4096)[:, a:b]
            w2s = lambda a, b: ab(F_W2, 4096)[:, a:b]
            hTs = lambda a, b: ab(F_HT, 2048)[:, a:b]
            F_WB2 = 24192
            wbs = lambda par, gl: ab((F_WB if par == 0 else F_WB2) + 256 * gl,
                                     512)
            sls = lambda i: ab(F_SL + 256 * i, 512)
            wem = arena[0:8, F_WEM:F_WEM + 2048]
            lrw = arena[0:8, F_WEM:F_WEM + 2048]
            wmp = af(F_WEM + 2048, 128)
            C_RM, C_M1, C_R1, C_M2, C_R2, C_DD, C_ED, C_W1c, C_W2c, C_T0 = \
                128, 129, 130, 131, 132, 133, 134, 135, 136, 137
            C_MC, C_RC = 48, 64
            C_LT, C_LT2, C_EQ, C_TT, C_L2 = 80, 88, 96, 104, 112

            fence = [("t", B.cnt["t"]), ("v", B.cnt["v"]), ("s", B.cnt["s"])]
            wsF1 = [Reg(), Reg()]
            wsF3 = [Reg(), Reg()]
            wsF2 = Reg()
            wb_dones = [None, None]
            r_lnTH, r_hT, r_wem = Reg(), Reg(), Reg()
            r_scrd = [Reg(), Reg()]
            r_wb = [Reg(), Reg()]
            dwr_pend = [None, None]
            for rg in (wsF1[0], wsF1[1], wsF3[0], wsF3[1], wsF2, r_lnTH, r_hT,
                       r_wb[0], r_wb[1], r_wem, r_scrd[0], r_scrd[1]):
                rg.freed(*fence)

            ln_stats(0, N, F_HT, xw)
            stm, str_ = stats_ms[0], stats_ms[1]

            lrw_ms = []
            for g in range(NG):
                bank = P[2 + g % 2]
                cg = None
                for kt in range(KT):
                    cg = mm(bank, bank.t[0:8, 0:512],
                            wgf[:, E8 * kt:E8 * kt + E8],
                            xT[:, 2048 * kt + 512 * g:2048 * kt + 512 * g + 512],
                            start=(kt == 0), stop=(kt == KT - 1),
                            waits=([xw] + fence if (g == 0 and kt == 0) else []),
                            inc=("t" if kt == KT - 1 else None))
                ce = s_copy(lrw[0:8, 512 * g:512 * g + 512], bank.t[0:8, 0:512],
                            waits=[("t", cg)] + (fence if g == 0 else []),
                            inc="s")
                bank.free = ("s", ce)
                lrw_ms.append(ce)
            lrw_done = ("s", lrw_ms[-1])

            def emit_lnth(th0):
                lnth_ms = r_lnTH.take() + fscreg.take()
                lastln = None
                lnth_gl = [None, None]
                for gl in range(2):
                    ts = th0 + 512 * gl
                    bcast(P[2], row(R_M + ts, 512), 512, waits=[stm])
                    crr = bcast(P[3], row(R_RSTD + ts, 512), 512,
                                waits=[str_], inc="t")
                    for kt in range(KT):
                        f = af(FSC0, 512)
                        v_tt(f, xT[:, 2048 * kt + ts:2048 * kt + ts + 512],
                             P[2].t[:, 0:512], ALU.subtract,
                             waits=[("t", crr)] +
                             (lnth_ms if (gl == 0 and kt == 0) else []))
                        lastln = v_tt(lnTH(kt, 512 * gl, 512 * gl + 512), f,
                                      P[3].t[:, 0:512], ALU.mult)
                    P[2].free = ("v", lastln)
                    P[3].free = ("v", lastln)
                    lnth_gl[gl] = ("v", lastln)
                fscreg.freed(("v", lastln))
                return lnth_gl

            lnth0_gl = emit_lnth(0)
            last_wem_tr = None
            wem_done = None
            wmp_ms = []
            for tt in range(16):
                bL, bM, bR = P[4], P[5], P[6]
                ctl = B.emit("tensor",
                             lambda En, o=bL.t[:, 0:8],
                             x=lrw[0:8, 128 * tt:128 * tt + 128],
                             i=idf[0:8, 0:8]:
                             En.transpose(o, x, i),
                             waits=[lrw_done] + ([bL.free] if bL.free else []) +
                             (fence if tt == 0 else []),
                             inc="t")
                bL.free = None
                ctm = B.emit("tensor",
                             lambda En, o=bM.t[:, 0:1],
                             x=row(R_M + 128 * tt, 128), i=idf[0:1, 0:1]:
                             En.transpose(o, x, i),
                             waits=[stm] + ([bM.free] if bM.free else []),
                             inc="t")
                bM.free = None
                ctr = B.emit("tensor",
                             lambda En, o=bR.t[:, 0:1],
                             x=row(R_RSTD + 128 * tt, 128),
                             i=idf[0:1, 0:1]:
                             En.transpose(o, x, i),
                             waits=[str_] + ([bR.free] if bR.free else []),
                             inc="t")
                bR.free = None
                mc = cols[:, C_MC + tt:C_MC + tt + 1]
                rc = cols[:, C_RC + tt:C_RC + tt + 1]
                c1 = v_copy(mc, bM.t[:, 0:1], waits=[("t", ctm)])
                c2 = v_copy(rc, bR.t[:, 0:1], waits=[("t", ctr)])
                bM.free = ("v", c1)
                bR.free = ("v", c2)
                lt = cols[:, C_LT:C_LT + 8]
                lt2 = cols[:, C_LT2:C_LT2 + 8]
                cfree = v_ts(lt, bL.t[:, 0:8], rc, None, ALU.mult,
                             waits=[("t", ctl)])
                bL.free = ("v", cfree)
                v_tt(cols[:, C_RM:C_RM + 1], mc, rc, ALU.mult)
                v_ts(lt2, sbg[:, 0:E8], cols[:, C_RM:C_RM + 1], None, ALU.mult)
                v_tt(lt, lt, lt2, ALU.subtract)
                v_tt(lt, lt, sbg[:, E8:2 * E8], ALU.add)
                eq = cols[:, C_EQ:C_EQ + 8]
                ttm = cols[:, C_TT:C_TT + 8]
                l2 = cols[:, C_L2:C_L2 + 8]
                v_red(cols[:, C_M1:C_M1 + 1], lt, ALU.max)
                v_ts(eq, lt, cols[:, C_M1:C_M1 + 1], None, ALU.is_equal)
                v_tt(ttm, eq, iota[:, :], ALU.mult)
                v_red(cols[:, C_R1:C_R1 + 1], ttm, ALU.max)
                v_ts(eq, iota[:, :], cols[:, C_R1:C_R1 + 1], None, ALU.is_equal)
                v_stt(l2, eq, -1e30, lt, ALU.mult, ALU.add)
                v_red(cols[:, C_M2:C_M2 + 1], l2, ALU.max)
                v_ts(ttm, l2, cols[:, C_M2:C_M2 + 1], None, ALU.is_equal)
                v_tt(ttm, ttm, iota[:, :], ALU.mult)
                v_red(cols[:, C_R2:C_R2 + 1], ttm, ALU.max)
                v_ts(ttm, iota[:, :], cols[:, C_R2:C_R2 + 1], None, ALU.is_equal)
                cdd = v_tt(cols[:, C_DD:C_DD + 1], cols[:, C_M2:C_M2 + 1],
                           cols[:, C_M1:C_M1 + 1], ALU.subtract)
                ced_ = s_act(cols[:, C_ED:C_ED + 1], cols[:, C_DD:C_DD + 1],
                             AF.Exp, waits=[("v", cdd)])
                v_ts(cols[:, C_T0:C_T0 + 1], cols[:, C_ED:C_ED + 1], 1.0, None,
                     ALU.add, waits=[("s", ced_)])
                v_recip(cols[:, C_W1c:C_W1c + 1], cols[:, C_T0:C_T0 + 1])
                v_tt(cols[:, C_W2c:C_W2c + 1], cols[:, C_ED:C_ED + 1],
                     cols[:, C_W1c:C_W1c + 1], ALU.mult)
                v_ts(eq, eq, cols[:, C_W1c:C_W1c + 1], None, ALU.mult)
                v_ts(ttm, ttm, cols[:, C_W2c:C_W2c + 1], None, ALU.mult)
                cwm = v_tt(wmp[:, 8 * tt:8 * tt + 8], eq, ttm, ALU.add)
                wmp_ms.append(cwm)
            for tt in range(16):
                bW = P[4]
                ctw = B.emit("tensor",
                             lambda En, o=bW.t[0:8, 0:128],
                             x=wmp[:, 8 * tt:8 * tt + 8], i=idf[:, :]:
                             En.transpose(o, x, i),
                             waits=[("v", wmp_ms[tt])] +
                             ([bW.free] if bW.free else []),
                             inc="t")
                bW.free = None
                cwe = v_copy(wem[0:8, 128 * tt:128 * tt + 128],
                             bW.t[0:8, 0:128],
                             waits=[("t", ctw)] +
                             (r_wem.take() if tt == 0 else []))
                bW.free = ("v", cwe)
                last_wem_tr = ctw
                wem_done = ("v", cwe)


            lnth_gl_by_th = [lnth0_gl, None]
            lastY = None
            lastW2mm = None
            sl_free = [None, None]
            prev_csl = None
            rrb = [0]
            jobs = [(TH, e, s) for TH in range(2) for e in range(E8)
                    for s in range(E8)]
            NJ = len(jobs)
            castms = {}
            dinfo = {}

            def issue_dma(gp):
                if gp >= 6 * NJ:
                    return
                j, pc = gp // 6, gp % 6
                TH, e, s = jobs[j]
                par = j % 2
                if pc < 2:
                    src3 = w_f1[e][:, 512 * s:512 * s + 512] \
                        .rearrange("(a p) n -> p a n", p=128)
                    dstf, reg, nsub, i = \
                        (lambda a, b, par=par: w1s(par, a, b)), wsF1[par], 4, pc
                elif pc < 4:
                    src3 = w_f3[e][:, 512 * s:512 * s + 512] \
                        .rearrange("(a p) n -> p a n", p=128)
                    dstf, reg, nsub, i = \
                        (lambda a, b, par=par: w3s(par, a, b)), wsF3[par], 4, pc - 2
                else:
                    src3 = w_f2[e][512 * s:512 * s + 512, :] \
                        .rearrange("(a p) n -> p a n", p=128)
                    dstf, reg, nsub, i = w2s, wsF2, 2, pc - 4
                ssl = gp % 2
                qn = "dA" if ssl == 0 else "dB"
                stga = af(STG0 if ssl == 0 else STG1, 2048)
                dc = B.dma(stga.rearrange("p (a n) -> p a n", a=nsub),
                           src3[:, nsub * i:nsub * (i + 1), :],
                           waits=stg[ssl].take(), q=qn)
                dinfo[gp] = (qn, dc, stga, dstf, 2048 * i, i == 0, reg)

            def emit_cast(gp):
                if gp >= 6 * NJ:
                    return
                qn, dc, stga, dstf, off, first, reg = dinfo.pop(gp)
                wts = [(qn, dc)] + (reg.take() if first else [])
                cc = s_copy(dstf(off, off + 2048), stga, waits=wts)
                ssl = 0 if qn == "dA" else 1
                stg[ssl].freed(("s", cc))
                castms.setdefault(gp // 6, []).append(("s", cc))

            def drip(cur):
                emit_cast(cur[0])
                issue_dma(cur[0] + 2)
                cur[0] += 1

            cur = [0]
            issue_dma(0)
            issue_dma(1)
            for _ in range(6):
                drip(cur)


            for j, (TH, e, s_) in enumerate(jobs):
                th0 = 1024 * TH
                sl = j % 2
                epar = e % 2
                if s_ == 0 and e == 0 and TH == 1:
                    lnth_gl_by_th[1] = emit_lnth(th0)
                if s_ == 0 and e == 0:
                    for gl in range(2):
                        ts = th0 + 512 * gl
                        cmr = v_tt(row(R_T0, 512), row(R_M + ts, 512),
                                   row(R_RSTD + ts, 512), ALU.mult,
                                   waits=rowreg.take() +
                                   (r_scrd[1].free or []))
                        bcast(P[4], row(R_T0, 512), 512, waits=[("v", cmr)])
                        cbr = bcast(P[5], row(R_RSTD + ts, 512), 512, inc="t")
                        lastadd = None
                        for kt in range(KT):
                            f0 = af(FSC0, 512)
                            f1 = af(FSC1, 512)
                            v_tt(f0, xT[:, 2048 * kt + ts:2048 * kt + ts + 512],
                                 P[5].t[:, 0:512], ALU.mult,
                                 waits=[("t", cbr)])
                            v_tt(f1, f0, P[4].t[:, 0:512], ALU.subtract)
                            lastadd = v_tt(
                                xT[:, 2048 * kt + ts:2048 * kt + ts + 512],
                                xT[:, 2048 * kt + ts:2048 * kt + ts + 512],
                                f1, ALU.add)
                        P[4].free = ("v", lastadd)
                        P[5].free = ("v", lastadd)
                        rowreg.freed(("v", lastadd))
                lnth_gl = lnth_gl_by_th[TH]
                if s_ == 0:
                    if dwr_pend[epar] is None:
                        dwr_pend[epar] = B.dma(
                            row(R_SCRD if epar == 0 else R_T0, 1024),
                            wem[e:e + 1, th0:th0 + 1024],
                            waits=[wem_done] + r_scrd[epar].take(),
                            q=("dC" if epar == 0 else "dD"))
                    dwr = dwr_pend[epar]
                    dwr_pend[epar] = None
                    scrd0 = R_SCRD if epar == 0 else R_T0
                    cwb = None
                    cwm_ = None
                    wb_ms = r_wb[epar].take()
                    for gl in range(2):
                        cwm_ = mm(P[6], P[6].t[:, 0:512], onesr_f[0:1, :],
                                  row(scrd0 + 512 * gl, 512),
                                  start=True, stop=True,
                                  waits=[("dC" if epar == 0 else "dD", dwr),
                                         CMS], inc="t")
                        cwb = s_copy(wbs(epar, gl), P[6].t[:, 0:512],
                                     waits=[("t", cwm_)] + wb_ms, inc="s")
                        P[6].free = ("s", cwb)
                    r_scrd[epar].freed(("t", cwm_))
                    wb_dones[epar] = ("s", cwb)
                    nj = j + E8
                    if nj < NJ:
                        nTH, ne, _ = jobs[nj]
                        npar = ne % 2
                        dwr_pend[npar] = B.dma(
                            row(R_SCRD if npar == 0 else R_T0, 1024),
                            wem[ne:ne + 1, 1024 * nTH:1024 * nTH + 1024],
                            waits=[wem_done] + r_scrd[npar].take() +
                            (rowreg.free or []),
                            q=("dC" if npar == 0 else "dD"))
                wb_done = wb_dones[epar]
                ms1 = castms[j][0:2]
                ms3 = castms[j][2:4]
                ms2 = castms[j][4:6]
                ndrip = [4 if j + 1 < NJ else 0]
                hms = {}
                ubank = {}

                def emit_u_mms(gl, mt):
                    bU1 = P[rrb[0] % 6]
                    rrb[0] += 1
                    bU3 = P[rrb[0] % 6]
                    rrb[0] += 1
                    cu1 = cu3 = None
                    for kt in range(KT):
                        w0 = ([ms1[kt // 4], lnth_gl[gl]]
                              if kt in (0, 4) else [])
                        cu1 = mm(bU1, bU1.v(),
                                 w1s(sl, 512 * kt + 128 * mt,
                                     512 * kt + 128 * mt + 128),
                                 lnTH(kt, 512 * gl, 512 * gl + 512),
                                 start=(kt == 0), stop=(kt == KT - 1),
                                 waits=w0,
                                 inc=("t" if kt == KT - 1 else None))
                    for kt in range(KT):
                        w0 = ([ms3[kt // 4]] if kt in (0, 4) else [])
                        cu3 = mm(bU3, bU3.v(),
                                 w3s(sl, 512 * kt + 128 * mt,
                                     512 * kt + 128 * mt + 128),
                                 lnTH(kt, 512 * gl, 512 * gl + 512),
                                 start=(kt == 0), stop=(kt == KT - 1),
                                 waits=w0,
                                 inc=("t" if kt == KT - 1 else None))
                    ubank[gl, mt] = (bU1, bU3, cu1, cu3)

                def emit_chain(gl, mt, hT_ms):
                    nonlocal prev_csl
                    bU1, bU3, cu1, cu3 = ubank.pop((gl, mt))
                    f0 = af(FSC0, 512)
                    f1 = af(FSC1, 512)
                    ct1_ = v_tt(f0, bU1.v(), wbs(epar, gl), ALU.mult,
                                waits=[("t", cu1), wb_done, prev_csl])
                    csl = s_act(sls(mt % 2), f0, AF.Silu,
                                waits=[("v", ct1_), sl_free[mt % 2]])
                    prev_csl = ("s", csl)
                    if ndrip[0] > 0:
                        drip(cur)
                        ndrip[0] -= 1
                    ct3_ = v_tt(f1, bU3.v(), wbs(epar, gl), ALU.mult,
                                waits=[("t", cu3)])
                    bU1.free = ("v", ct1_)
                    bU3.free = ("v", ct3_)
                    lasth = v_tt(hTs(512 * mt, 512 * mt + 512),
                                 sls(mt % 2), f1, ALU.mult,
                                 waits=[("s", csl)] + hT_ms)
                    sl_free[mt % 2] = ("v", lasth)
                    hms[gl][mt] = ("v", lasth)

                def emit_y(gl):
                    nonlocal lastY, lastW2mm
                    ts = th0 + 512 * gl
                    h_ms = hms[gl]
                    held = set()
                    for (bb1, bb3, _, _) in ubank.values():
                        held.add(id(bb1))
                        held.add(id(bb3))
                    cylast = None
                    for d in range(KT):
                        bY = P[rrb[0] % 6]
                        rrb[0] += 1
                        while id(bY) in held:
                            bY = P[rrb[0] % 6]
                            rrb[0] += 1
                        cy = None
                        for k in range(4):
                            w0 = [h_ms[k]] + \
                                ([ms2[k // 2]] if k in (0, 2) else [])
                            cy = mm(bY, bY.v(),
                                    w2s(1024 * k + 128 * d,
                                        1024 * k + 128 * d + 128),
                                    hTs(512 * k, 512 * k + 512),
                                    start=(k == 0), stop=(k == 3),
                                    waits=w0,
                                    inc=("t" if k == 3 else None))
                            cylast = cy
                        lastY = v_tt(
                            xT[:, 2048 * d + ts:2048 * d + ts + 512],
                            xT[:, 2048 * d + ts:2048 * d + ts + 512],
                            bY.v(), ALU.add,
                            waits=[("t", cy)], inc="v")
                        bY.free = ("v", lastY)
                        lastW2mm = cy
                    return cylast

                hms[0] = [None] * 4
                hT_ms0 = r_hT.take()
                for mt in range(4):
                    emit_u_mms(0, mt)
                    emit_chain(0, mt, hT_ms0 if mt == 0 else [])
                emit_u_mms(1, 0)
                cyl0 = emit_y(0)
                hms[1] = [None] * 4
                hT_ms1 = r_hT.take()
                r_hT.freed(("t", cyl0))
                hT_ms1 = hT_ms1 + r_hT.take()
                emit_chain(1, 0, hT_ms1)
                for mt in range(1, 4):
                    emit_u_mms(1, mt)
                    emit_chain(1, mt, [])
                cyl1 = emit_y(1)
                r_hT.freed(("t", cyl1))
                wsF1[sl].freed(("t", lastW2mm))
                wsF3[sl].freed(("t", lastW2mm))
                wsF2.freed(("t", lastW2mm))
                if j + 1 < NJ:
                    drip(cur)
                    drip(cur)
                if s_ == E8 - 1:
                    r_wb[epar].freed(("v", B.cnt["v"]))
                if s_ == E8 - 1 and e == E8 - 1:
                    fscreg.freed(("v", B.cnt["v"]), ("s", B.cnt["s"]))
                    r_lnTH.freed(("t", lastW2mm))
            return ("v", lastY)

        def phase_attn_b(xw):
            B_LNT = CBASE
            B_QS = CBASE + 8192
            B_KS = CBASE + 9216
            B_VM = CBASE + 10240
            B_AAB = CBASE + 11264
            B_SQ = CBASE + 11392
            B_US = CBASE + 11392
            B_WT = CBASE + 11904
            B_WS = CBASE + 12416
            B_BIAS = CBASE + 13440
            assert B_BIAS + 4096 <= 24832
            lnT = lambda kt, a, b: ab(B_LNT, 16384)[:, 2048 * kt + a:2048 * kt + b]
            QS = lambda a, b: ab(B_QS, 2048)[:, a:b]
            KS = lambda a, b: ab(B_KS, 2048)[:, a:b]
            VM = lambda a, b: ab(B_VM, 2048)[:, a:b]
            aab = lambda i: ab(B_AAB + 64 * i, 128)
            uss = lambda i: ab(B_US + 256 * i, 512)
            B_WT2 = 23680
            B_FSC3 = 23168
            wts = lambda i: ab((B_WT + 256 * (i % 2)) if i < 2
                               else (B_WT2 + 256 * (i - 2)), 512)
            bfsc = lambda i: af([FSC0, FSC1, FSC2, B_FSC3][i], 512)
            wsp = lambda sl: ab(B_WS + 512 * sl, 1024)
            bias = lambda di, a, b: ab(B_BIAS, 8192)[:, 512 * di + a:512 * di + b]

            fence = [("t", B.cnt["t"]), ("v", B.cnt["v"]), ("s", B.cnt["s"])]
            r_lnT, r_qs, r_ks, r_vm, r_bias = Reg(), Reg(), Reg(), Reg(), Reg()
            stg2 = [Reg(), Reg()]
            stg2[0].freed(*(fence + (stg[0].free or [])))
            stg2[1].freed(*(fence + (stg[1].free or [])))
            dq_next = [None, None]
            r_us = [Reg(), Reg()]
            r_wt = [Reg(), Reg(), Reg(), Reg()]
            r_aab = [Reg(), Reg()]
            wsB = [Reg(), Reg()]
            for rg in (r_lnT, r_qs, r_ks, r_vm, r_bias, r_us[0], r_us[1],
                       r_wt[0], r_wt[1], r_wt[2], r_wt[3],
                       r_aab[0], r_aab[1], wsB[0], wsB[1]):
                rg.freed(*fence)

            ln_stats(0, N, B_SQ, xw)
            stm, str_ = stats_ms[0], stats_ms[1]
            ms_stats = ("t", B.cnt["t"])
            for rg in (r_us[0], r_us[1], r_wt[0], r_wt[1]):
                rg.freed(*(fence + [ms_stats]))
            lnT_ms = r_lnT.take() + fscreg.take()
            lastln = None
            for g in range(NG):
                ts = 512 * g
                bcast(P[2], row(R_M + ts, 512), 512, waits=[stm])
                crr = bcast(P[3], row(R_RSTD + ts, 512), 512,
                            waits=[str_], inc="t")
                for kt in range(KT):
                    f = af(FSC0, 512)
                    v_tt(f, xT[:, 2048 * kt + ts:2048 * kt + ts + 512],
                         P[2].t[:, 0:512], ALU.subtract,
                         waits=[("t", crr)] +
                         (lnT_ms if (g == 0 and kt == 0) else []))
                    lastln = v_tt(lnT(kt, ts, ts + 512), f,
                                  P[3].t[:, 0:512], ALU.mult)
                P[2].free = ("v", lastln)
                P[3].free = ("v", lastln)
            lnw = ("v", lastln)
            fscreg.freed(lnw)

            def proj_dma(w_hbm, h, sl):
                src3 = w_hbm[:, 128 * h:128 * h + 128] \
                    .rearrange("(a p) n -> p a n", p=128)
                qn = "dC" if sl == 0 else "dD"
                stga = af((STG0 if sl == 0 else STG1) + 1024, 1024)
                dc = B.dma(stga.rearrange("p (a n) -> p a n", a=8),
                           src3, waits=stg2[sl].take(), q=qn)
                return (qn, dc, stga)

            def proj_cast(dmainfo, sl, reg):
                qn, dc, stga = dmainfo
                cc = s_copy(wsp(sl), stga, waits=[(qn, dc)] + reg.take())
                stg2[sl].freed(("s", cc))
                return ("s", cc)

            lastxT = None
            for h in range(H8):
                if dq_next[0] is not None:
                    dq, dk = dq_next
                    dq_next[0] = dq_next[1] = None
                else:
                    dq = proj_dma(w_bq, h, 0)
                    dk = proj_dma(w_bk, h, 1)
                bias_ms = r_bias.take()
                bias_dmas = {}
                bsrc = bias_exp[h].rearrange("(a p) n -> p a n", p=128)

                def bias_dma(i):
                    ssl = i % 2
                    qn = "dA" if ssl == 0 else "dB"
                    stga = af(STG0 if ssl == 0 else STG1, 1024)
                    dcb = B.dma(stga.rearrange("p (a n) -> p a n", a=2),
                                bsrc[:, 2 * i:2 * (i + 1), :],
                                waits=stg[ssl].take(), q=qn)
                    bias_dmas[i] = (qn, dcb, stga)
                bias_dma(7)
                bias_dma(6)
                cq = proj_cast(dq, 0, wsB[0])
                ck = proj_cast(dk, 1, wsB[1])
                qs_ms = r_qs.take()
                lastq = None
                for g in range(NG):
                    bank = P[0 + g % 2]
                    cgrp = None
                    for kt in range(KT):
                        cgrp = mm(bank, bank.v(),
                                  wsp(0)[:, 128 * kt:128 * kt + 128],
                                  lnT(kt, 512 * g, 512 * g + 512),
                                  start=(kt == 0), stop=(kt == KT - 1),
                                  waits=([cq, lnw] + (qs_ms if g == 0 else [])
                                         if kt == 0 else []),
                                  inc=("t" if kt == KT - 1 else None))
                    lastq = s_act(QS(512 * g, 512 * g + 512), bank.v(),
                                  AF.Silu, waits=[("t", cgrp)], inc="s")
                    bank.free = ("s", lastq)
                qs_done = ("s", lastq)
                wsB[0].freed(("t", cgrp))
                ks_ms = r_ks.take()
                lastk = None
                for g in range(NG):
                    bank = P[2 + g % 2]
                    cgrp = None
                    for kt in range(KT):
                        cgrp = mm(bank, bank.v(),
                                  wsp(1)[:, 128 * kt:128 * kt + 128],
                                  lnT(kt, 512 * g, 512 * g + 512),
                                  start=(kt == 0), stop=(kt == KT - 1),
                                  waits=([ck, lnw] + (ks_ms if g == 0 else [])
                                         if kt == 0 else []),
                                  inc=("t" if kt == KT - 1 else None))
                    lastk = s_act(KS(512 * g, 512 * g + 512), bank.v(),
                                  AF.Silu, waits=[("t", cgrp)], inc="s")
                    bank.free = ("s", lastk)
                ks_done = ("s", lastk)
                wsB[1].freed(("t", cgrp))
                dv_ = proj_dma(w_bv, h, 0)
                du_ = proj_dma(w_bu, h, 1)
                cv = proj_cast(dv_, 0, wsB[0])
                cu = proj_cast(du_, 1, wsB[1])
                if h + 1 < H8:
                    dq_next[0] = proj_dma(w_bq, h + 1, 0)
                    dq_next[1] = proj_dma(w_bk, h + 1, 1)
                for a in range(16):
                    par = a % 2
                    caa = s_act(aab(par), KS(128 * a, 128 * a + 128), AF.Abs,
                                waits=[ks_done] + r_aab[par].take())
                    cms_ = mm(P[6], P[6].t[:, 0:1], aab(par), onesc_b[:, :],
                              start=True, stop=True,
                              waits=[("s", caa), CMS], inc="t")
                    r_aab[par].freed(("t", cms_))
                    cmk = v_ts(cols[:, 32 + a:33 + a], P[6].t[:, 0:1], 0.0,
                               None, ALU.is_gt, waits=[("t", cms_)])
                    P[6].free = ("v", cmk)
                mk_done = ("v", cmk)
                vm_ms = r_vm.take()
                lastvm = None
                for m in range(16):
                    bank = P[4 + m % 2]
                    cgrp = None
                    for kt in range(KT):
                        cgrp = mm(bank, bank.v(0, 128),
                                  lnT(kt, 128 * m, 128 * m + 128),
                                  wsp(0)[:, 128 * kt:128 * kt + 128],
                                  start=(kt == 0), stop=(kt == KT - 1),
                                  waits=([cv, lnw] if kt == 0 else []),
                                  inc=("t" if kt == KT - 1 else None))
                    par = m % 2
                    csv = s_act(aab(par), bank.v(0, 128), AF.Silu,
                                waits=[("t", cgrp)] + r_aab[par].take())
                    lastvm = v_ts(VM(128 * m, 128 * m + 128), aab(par),
                                  cols[:, 32 + m:33 + m], None, ALU.mult,
                                  waits=[("s", csv), mk_done] +
                                  (vm_ms if m == 0 else []))
                    r_aab[par].freed(("v", lastvm))
                    bank.free = ("v", lastvm)
                vm_done = ("v", lastvm)
                wsB[0].freed(("t", cgrp))
                bias_cms = {}
                first_b = True
                for i in range(7, -1, -1):
                    qn, dcb, stga = bias_dmas[i]
                    cb = s_copy(ab(B_BIAS + 512 * i, 1024), stga,
                                waits=[(qn, dcb)] +
                                (bias_ms if first_b else []))
                    first_b = False
                    stg[0 if qn == "dA" else 1].freed(("s", cb))
                    bias_cms[i] = ("s", cb)
                    if i - 2 >= 0:
                        bias_dma(i - 2)
                for G in range(NG):
                    gp = G % 2
                    bank = P[6]
                    cgrp = None
                    for kt in range(KT):
                        cgrp = mm(bank, bank.v(),
                                  wsp(1)[:, 128 * kt:128 * kt + 128],
                                  lnT(kt, 512 * G, 512 * G + 512),
                                  start=(kt == 0), stop=(kt == KT - 1),
                                  waits=([cu, lnw] if kt == 0 else []),
                                  inc=("t" if kt == KT - 1 else None))
                    cus = s_act(uss(gp), bank.v(), AF.Silu,
                                waits=[("t", cgrp)] + r_us[gp].take(),
                                inc="s")
                    bank.free = ("s", cus)
                    if G == NG - 1:
                        wsB[1].freed(("t", cgrp))
                    bA = P[5]
                    for a in range(4 * G + 4):
                        par = a % 4
                        bS = P[par]
                        di = a - 4 * G + 12
                        mm(bS, bS.v(), KS(128 * a, 128 * a + 128),
                           QS(512 * G, 512 * G + 512),
                           start=True, stop=False,
                           waits=[qs_done, ks_done])
                        csc = mm(bS, bS.v(), idb[:, :], bias(di, 0, 512),
                                 start=False, stop=True,
                                 waits=[bias_cms[di // 2]], inc="t")
                        cwt = s_act(wts(par), bS.v(), AF.Silu,
                                    scale=INV_SQRT_HD,
                                    waits=[("t", csc)] + r_wt[par].take(),
                                    inc="s")
                        bS.free = ("s", cwt)
                        r = a - 4 * G
                        if r >= 0:
                            cwm2 = v_tt(wts(par), wts(par),
                                        bmask[:, 512 * r:512 * r + 512],
                                        ALU.mult, waits=[("s", cwt)])
                            wt_rdy = ("v", cwm2)
                        else:
                            wt_rdy = ("s", cwt)
                        catt = mm(bA, bA.v(), VM(128 * a, 128 * a + 128),
                                  wts(par),
                                  start=(a == 0), stop=(a == 4 * G + 3),
                                  waits=[wt_rdy] +
                                  ([vm_done] if a == 0 else []),
                                  inc="t")
                        r_wt[par].freed(("t", catt))
                    f2 = af(FSC2, 512)
                    cg1_ = v_tt(f2, bA.v(), uss(gp), ALU.mult,
                                waits=[("t", catt), ("s", cus)])
                    bA.free = ("v", cg1_)
                    r_us[gp].freed(("v", cg1_))
                    lastxT = v_tt(
                        xT[:, 2048 * h + 512 * G:2048 * h + 512 * G + 512],
                        xT[:, 2048 * h + 512 * G:2048 * h + 512 * G + 512],
                        f2, ALU.add, inc="v")
                r_bias.freed(("t", csc))
            r_lnT.freed(("t", B.cnt["t"]))
            for _i in range(2):
                stg[_i].free = (stg[_i].free or []) + (stg2[_i].free or [])
            return ("v", lastxT)

        if "attn_a" in BUILD_PHASES:
            xw = phase_attn_a(xw)
        if "ffn1" in BUILD_PHASES:
            xw = phase_ffn(xw)
        if "attn_b" in BUILD_PHASES:
            xw = phase_attn_b(xw)
        if "ffn2" in BUILD_PHASES:
            xw = phase_ffn(xw)

        out3 = out.rearrange("(a p) n -> p a n", p=128)
        dS0 = B.dma(out3[:, 0:4, :],
                    xT[:, 0:8192].rearrange("p (a n) -> p a n", a=4),
                    waits=[xw], q="dO")
        dS1 = B.dma(out3[:, 4:8, :],
                    xT[:, 8192:16384].rearrange("p (a n) -> p a n", a=4),
                    q="dO")
        B.wait("sync", "dO", dS1)

        with nc.Block() as block:
            @block.sync
            def _(En):
                for f in B.st["sync"]:
                    f(En)

            @block.tensor
            def _(En):
                for f in B.st["tensor"]:
                    f(En)

            @block.vector
            def _(En):
                for f in B.st["vector"]:
                    f(En)

            @block.scalar
            def _(En):
                for f in B.st["scalar"]:
                    f(En)

            @block.gpsimd
            def _(En):
                for f in B.st["gpsimd"]:
                    f(En)
    return nc


def _host_inputs(inputs):
    f32 = np.float32
    bf = ml_dtypes.bfloat16
    x = np.asarray(inputs["L0_attn_a_x"], f32)
    gk2pad = np.zeros((128, KD), f32)
    gk2pad[0:16] = np.asarray(inputs["L0_attn_a_Wgk2"], f32)
    gk2pad[32] = np.asarray(inputs["L0_attn_a_bgk2"], f32)
    rmsw2 = np.ascontiguousarray(
        np.asarray(inputs["L0_attn_a_rms_w"], f32).reshape(2, 128).T)
    rel = np.asarray(inputs["L0_attn_b_rel_bias"], f32)
    I = np.arange(512)[None, :] - np.arange(128)[:, None]
    idx = np.stack([I - 128 * (a - 12) + 2047 for a in range(16)])
    bias_exp = np.ascontiguousarray(
        rel[idx].transpose(3, 0, 1, 2).reshape(H8, 16 * 128, 512))
    bmask = np.concatenate(
        [(np.arange(512)[None, :] - 128 * r >= np.arange(128)[:, None])
         .astype(bf) for r in range(4)], axis=1)
    shared = {
        "w_aq": np.asarray(inputs["L0_attn_a_Wq"], f32),
        "w_ak": np.asarray(inputs["L0_attn_a_Wk"], f32),
        "w_av": np.asarray(inputs["L0_attn_a_Wv"], f32),
        "w_ag": np.asarray(inputs["L0_attn_a_Wg"], f32),
        "w_ao": np.asarray(inputs["L0_attn_a_Wout"], f32),
        "w_gk1": np.asarray(inputs["L0_attn_a_Wgk1"], f32),
        "gk2pad": gk2pad, "rmsw2": rmsw2,
        "w_bq": np.asarray(inputs["L0_attn_b_Wq"], f32),
        "w_bk": np.asarray(inputs["L0_attn_b_Wk"], f32),
        "w_bv": np.asarray(inputs["L0_attn_b_Wv"], f32),
        "w_bu": np.asarray(inputs["L0_attn_b_Wu"], f32),
        "bias_exp": bias_exp,
        "w_fg": np.asarray(inputs["L0_ffn_Wg"], f32),
        "bg_row": np.asarray(inputs["L0_ffn_bg"], f32).reshape(1, E8),
        "w_f1": np.asarray(inputs["L0_ffn_W1"], f32),
        "w_f2": np.asarray(inputs["L0_ffn_W2"], f32),
        "w_f3": np.asarray(inputs["L0_ffn_W3"], f32),
        "c_triu": np.triu(np.ones((128, 128), f32)),
        "c_idf": np.eye(128, dtype=f32),
        "c_idb": np.eye(128, dtype=bf),
        "c_iota": np.ascontiguousarray(np.broadcast_to(
            np.arange(7, -1, -1, dtype=f32)[None, :], (128, E8))),
        "c_bmask": bmask,
    }
    in_maps = []
    for b in range(8):
        m = dict(shared)
        m["xin"] = np.ascontiguousarray(x[b].T)
        in_maps.append(m)
    return in_maps


def kernel(**inputs):
    in_maps = _host_inputs(inputs)
    nc = build_program()
    res = run_bass_kernel_spmd(nc, in_maps, core_ids=list(range(8)))
    outs = [np.ascontiguousarray(r["out"].T) for r in res.results]
    return np.stack(outs).astype(np.float32)


# version 14 (final)
# speedup vs baseline: 1.1718x; 1.0264x over previous
import math
from collections import defaultdict
from contextlib import ExitStack

import numpy as np
import ml_dtypes

import concourse.bass as bass
import concourse.mybir as mybir
from concourse.bass_utils import run_bass_kernel_spmd

F32 = mybir.dt.float32
BF16 = mybir.dt.bfloat16
AF = mybir.ActivationFunctionType
ALU = mybir.AluOpType
AX = mybir.AxisListType

D, KT, N, NG = 1024, 8, 2048, 4
EPS = 1e-5
GLN = 16.0
H4, DK, DV, KD, VD = 4, 128, 256, 512, 1024
H8, HD = 8, 128
INV_SQRT_HD = 1.0 / math.sqrt(128.0)
E8, HF = 8, 4096

BUILD_PHASES = ("attn_a", "ffn1", "attn_b", "ffn2")

ENGINES = ("sync", "tensor", "vector", "scalar", "gpsimd")


class Builder:
    def __init__(self, nc):
        self.nc = nc
        self.st = {e: [] for e in ENGINES}
        self.semh = {}
        self.cnt = defaultdict(int)
        self.lastwait = {}

    def sem(self, name, stack):
        self.semh[name] = stack.enter_context(self.nc.semaphore(name))

    def wait(self, eng, nm, val):
        if val is None or val <= 0:
            return
        key = (eng, nm)
        if self.lastwait.get(key, -1) >= val:
            return
        self.lastwait[key] = val
        h = self.semh[nm]
        self.st[eng].append(lambda E, h=h, v=val: E.wait_ge(h, v))

    AUTO_SELF = {"vector": "v", "scalar": "s", "gpsimd": "g"}

    def emit(self, eng, f, waits=(), inc=None, k=1):
        own = self.AUTO_SELF.get(eng)
        if own is not None and self.cnt[own] > 0:
            self.wait(eng, own, self.cnt[own])
        for wt in waits:
            if wt is None:
                continue
            if isinstance(wt, list):
                for w2 in wt:
                    if w2 is not None:
                        self.wait(eng, w2[0], w2[1])
            else:
                self.wait(eng, wt[0], wt[1])
        if inc is not None:
            self.cnt[inc] += k
            h = self.semh[inc]
            self.st[eng].append(lambda E, f=f, h=h, kk=k: f(E).then_inc(h, kk))
            return self.cnt[inc]
        self.st[eng].append(f)
        return None

    def dma(self, out_ap, in_ap, waits=(), q="dX"):
        return self.emit(
            "sync", lambda E, o=out_ap, i=in_ap: E.dma_start(out=o, in_=i),
            waits=waits, inc=q, k=16)


class Bank:
    def __init__(self, t):
        self.t = t
        self.free = None

    def v(self, lo=0, hi=512):
        return self.t[:, lo:hi]


class Reg:
    def __init__(self):
        self.free = None

    def take(self):
        w, self.free = self.free, None
        if w is None:
            return []
        return w if isinstance(w, list) else [w]

    def freed(self, *ms):
        self.free = [m for m in ms if m is not None]


def build_program():
    nc = bass.Bass("TRN2", target_bir_lowering=False)
    B = Builder(nc)

    def din(name, shape, dt=F32):
        return nc.dram_tensor(name, shape, dt, kind="ExternalInput").ap()

    xin = din("xin", [D, N])
    out = nc.dram_tensor("out", [D, N], F32, kind="ExternalOutput").ap()
    w_aq = din("w_aq", [D, KD]); w_ak = din("w_ak", [D, KD])
    w_av = din("w_av", [D, VD]); w_ag = din("w_ag", [D, VD])
    w_ao = din("w_ao", [VD, D])
    w_gk1 = din("w_gk1", [D, 16])
    gk2pad = din("gk2pad", [128, KD])
    rmsw2 = din("rmsw2", [128, 2])
    w_bq = din("w_bq", [D, D]); w_bk = din("w_bk", [D, D])
    w_bv = din("w_bv", [D, D]); w_bu = din("w_bu", [D, D])
    bias_exp = din("bias_exp", [H8, 16 * 128, 512])
    w_fg = din("w_fg", [D, E8])
    bg_row = din("bg_row", [1, E8])
    w_f1 = din("w_f1", [E8, D, HF]); w_f2 = din("w_f2", [E8, HF, D])
    w_f3 = din("w_f3", [E8, D, HF])
    c_triu = din("c_triu", [128, 128])
    c_idf = din("c_idf", [128, 128])
    c_idb = din("c_idb", [128, 128], BF16)
    c_iota = din("c_iota", [128, E8])
    c_bmask = din("c_bmask", [128, 4 * 512], BF16)

    with ExitStack() as stack:
        E = stack.enter_context
        for s in ("dX", "dA", "dB", "dC", "dD", "dO", "t", "v", "s", "g"):
            B.sem(s, stack)

        xT = E(nc.sbuf_tensor("xT", [128, KT * N], F32))
        arena = E(nc.sbuf_tensor("arena", [128, 24832], F32))
        rows = E(nc.sbuf_tensor("rows", [1, 6144], F32))
        cols = E(nc.sbuf_tensor("cols", [128, 160], F32))
        onesr_f = E(nc.sbuf_tensor("onesr_f", [1, 128], F32))
        onesc_f = E(nc.sbuf_tensor("onesc_f", [128, 1], F32))
        onesc_b = E(nc.sbuf_tensor("onesc_b", [128, 1], BF16))
        t1ext = E(nc.sbuf_tensor("t1ext", [128, N], BF16))
        S32 = E(nc.sbuf_tensor("S32", [128, H4 * DV], F32))
        Sbf = E(nc.sbuf_tensor("Sbf", [128, H4 * DV], BF16))
        triu = E(nc.sbuf_tensor("triu", [128, 128], F32))
        idf = E(nc.sbuf_tensor("idf", [128, 128], F32))
        idb = E(nc.sbuf_tensor("idb", [128, 128], BF16))
        iota = E(nc.sbuf_tensor("iota", [128, E8], F32))
        bmask = E(nc.sbuf_tensor("bmask", [128, 4 * 512], BF16))
        gk2b = E(nc.sbuf_tensor("gk2b", [128, KD], BF16))
        gk1b = E(nc.sbuf_tensor("gk1b", [128, KT * 16], BF16))
        wgf = E(nc.sbuf_tensor("wgf", [128, KT * E8], F32))
        bgr = E(nc.sbuf_tensor("bgr", [1, E8], F32))
        sbg = E(nc.sbuf_tensor("sbg", [128, 2 * E8], F32))

        P = [Bank(E(nc.psum_tensor(f"p{i}", [128, 512], F32))) for i in range(7)]
        PB = Bank(E(nc.psum_tensor("pb", [128, 1024], BF16)))

        R_M, R_RSTD, R_SCRD, R_T0, R_T1 = 0, 2048, 4096, 5120, 5632

        def row(off, n):
            return rows[0:1, off:off + n]

        def af(off, n):
            return arena[:, off:off + n]

        def ab(off_f32, nbf):
            return arena[:, off_f32:off_f32 + (nbf + 1) // 2] \
                .bitcast(BF16)[:, 0:nbf]

        STG0, STG1 = 0, 2048
        FSC0, FSC1, FSC2 = 4096, 4608, 5120
        CBASE = 5632

        def v_tt(o, a, b, op, waits=(), inc="v"):
            return B.emit("vector",
                          lambda En, oo=o, x=a, y=b, p=op:
                          En.tensor_tensor(out=oo, in0=x, in1=y, op=p),
                          waits=waits, inc=inc)

        def v_ts(o, a, s1, s2, op0, op1=None, waits=(), inc="v"):
            if op1 is None:
                return B.emit("vector",
                              lambda En, oo=o, x=a, aa=s1, p=op0:
                              En.tensor_single_scalar(out=oo, in_=x, scalar=aa,
                                                      op=p),
                              waits=waits, inc=inc)
            return B.emit("vector",
                          lambda En, oo=o, x=a, aa=s1, bb=s2, p0=op0, p1=op1:
                          En.tensor_scalar(out=oo, in0=x, scalar1=aa,
                                           scalar2=bb, op0=p0, op1=p1),
                          waits=waits, inc=inc)

        def v_stt(o, a, s, b, op0, op1, waits=(), inc="v"):
            return B.emit("vector",
                          lambda En, oo=o, x=a, ss=s, y=b, p0=op0, p1=op1:
                          En.scalar_tensor_tensor(out=oo, in0=x, scalar=ss,
                                                  in1=y, op0=p0, op1=p1),
                          waits=waits, inc=inc)

        def v_copy(o, a, waits=(), inc="v"):
            return B.emit("vector",
                          lambda En, oo=o, x=a: En.tensor_copy(out=oo, in_=x),
                          waits=waits, inc=inc)

        def g_copy(o, a, waits=(), inc="g"):
            return B.emit("gpsimd",
                          lambda En, oo=o, x=a: En.tensor_copy(out=oo, in_=x),
                          waits=waits, inc=inc)

        def v_red(o, a, op, waits=(), inc="v"):
            return B.emit("vector",
                          lambda En, oo=o, x=a, p=op:
                          En.tensor_reduce(out=oo, in_=x, axis=AX.X, op=p),
                          waits=waits, inc=inc)

        def v_recip(o, a, waits=(), inc="v"):
            return B.emit("vector",
                          lambda En, oo=o, x=a: En.reciprocal(out=oo, in_=x),
                          waits=waits, inc=inc)

        def v_memset(ap, c, waits=(), inc="v"):
            return B.emit("vector", lambda En, a=ap, cc=c: En.memset(a, cc),
                          waits=waits, inc=inc)

        def s_act(o, a, func, bias=None, scale=None, waits=(), inc="s"):
            kw = {}
            if bias is not None:
                kw["bias"] = bias
            if scale is not None:
                kw["scale"] = scale
            return B.emit("scalar",
                          lambda En, oo=o, x=a, f=func, k=dict(kw):
                          En.activation(out=oo, in_=x, func=f, **k),
                          waits=waits, inc=inc)

        def s_copy(o, a, waits=(), inc="s"):
            return B.emit("scalar",
                          lambda En, oo=o, x=a: En.copy(out=oo, in_=x),
                          waits=waits, inc=inc)

        def mm(bank, o, lhsT, rhs, start, stop, waits=(), inc=None):
            w = list(waits)
            if start and bank.free is not None:
                w.append(bank.free)
                bank.free = None
            return B.emit("tensor",
                          lambda En, oo=o, l=lhsT, r=rhs, st=start, sp=stop:
                          En.matmul(oo, l, r, start=st, stop=sp),
                          waits=w, inc=inc)

        def mm_tr(bank, o, a, ident, waits=(), inc=None):
            w = list(waits)
            if bank.free is not None:
                w.append(bank.free)
                bank.free = None
            return B.emit("tensor",
                          lambda En, oo=o, x=a, i=ident: En.transpose(oo, x, i),
                          waits=w, inc=inc)

        stg = [Reg(), Reg()]

        xin3 = xin.rearrange("(a p) n -> p a n", p=128)
        B.dma(xT[:, 0:8192].rearrange("p (a n) -> p a n", a=4),
              xin3[:, 0:4, :])
        d_x1 = B.dma(xT[:, 8192:16384].rearrange("p (a n) -> p a n", a=4),
                     xin3[:, 4:8, :])
        B.dma(triu[:, :], c_triu)
        B.dma(idf[:, :], c_idf)
        d_idb = B.dma(idb[:, :], c_idb)
        d_io = B.dma(iota[:, :], c_iota)
        d_bm = B.dma(bmask[:, :], c_bmask)
        d_wgf = B.dma(wgf[:, :].rearrange("p (a n) -> p a n", a=8),
                      w_fg.rearrange("(a p) n -> p a n", p=128))
        d_bgr = B.dma(bgr[0:1, :], bg_row)
        d_rms = B.dma(cols[:, 0:2], rmsw2)
        v_memset(onesr_f[0:1, :], 1.0)
        v_memset(onesc_f[:, :], 1.0)
        v_memset(onesc_b[:, :], 1.0)
        cmz = v_memset(t1ext[:, :], 0.0)
        v_memset(t1ext[32:33, :], 1.0, waits=[("v", cmz)])
        v_memset(S32[:, :], 0.0)
        cms = v_memset(Sbf[:, :], 0.0)
        CMS = ("v", cms)
        dg2 = B.dma(af(STG0, 512), gk2pad)
        dg1 = B.dma(af(STG0 + 512, 128).rearrange("p (a n) -> p a n", a=8),
                    w_gk1.rearrange("(a p) n -> p a n", p=128))
        DXA = ("dX", dg1)
        cg2 = s_copy(gk2b[:, :], af(STG0, 512), waits=[DXA])
        cg1 = s_copy(gk1b[:, :], af(STG0 + 512, 128), waits=[DXA])
        stg[0].freed(("s", cg1))
        csm = None
        for kt in range(KT):
            csm = mm(P[2], P[2].t[0:1, 0:E8], onesc_f[:, :],
                     wgf[:, E8 * kt:E8 * kt + E8],
                     start=(kt == 0), stop=(kt == KT - 1),
                     waits=([DXA, CMS] if kt == 0 else []),
                     inc=("t" if kt == KT - 1 else None))
        cS = v_copy(row(R_T0, E8), P[2].t[0:1, 0:E8], waits=[("t", csm)])
        cNC = v_ts(cols[0:1, 152:160], P[2].t[0:1, 0:E8], -1.0, None,
                   ALU.mult)
        NCMS = ("v", cNC)
        P[2].free = ("v", cNC)
        c_sb = mm(P[3], P[3].t[:, 0:E8], onesr_f[0:1, :], row(R_T0, E8),
                  start=True, stop=True, waits=[("v", cS)], inc="t")
        cS2 = v_copy(sbg[:, 0:E8], P[3].t[:, 0:E8], waits=[("t", c_sb)])
        P[3].free = ("v", cS2)
        c_bg = mm(P[4], P[4].t[:, 0:E8], onesr_f[0:1, :], bgr[0:1, :],
                  start=True, stop=True,
                  waits=[DXA, CMS], inc="t")
        cS3 = v_copy(sbg[:, E8:2 * E8], P[4].t[:, 0:E8], waits=[("t", c_bg)])
        P[4].free = ("v", cS3)

        xw = DXA

        sqslot = Reg()
        rowreg = Reg()
        fscreg = Reg()
        stats_ms = [None, None]

        def ln_stats(t0, ntok, sq_off, xw):
            for gg in range(ntok // 512):
                ts = t0 + 512 * gg
                b0, b1 = P[0], P[1]
                sq_ms = sqslot.take()
                mml = [None] * KT
                for kt in range(KT):
                    sl = kt % 2
                    sq = ab(sq_off + sl * 256, 512)
                    xb = ab(sq_off + 512 + sl * 256, 512)
                    wts = [xw] + (sq_ms if kt < 2 else
                                  [("t", mml[kt - 2])])
                    s_act(sq, xT[:, 2048 * kt + ts:2048 * kt + ts + 512],
                          AF.Square, waits=wts)
                    c2 = s_copy(xb, xT[:, 2048 * kt + ts:2048 * kt + ts + 512])
                    mm(b0, b0.t[0:1, 0:512], onesc_b[:, :], xb,
                       start=(kt == 0), stop=(kt == KT - 1),
                       waits=[("s", c2)] + ([CMS] if kt == 0 else []))
                    mml[kt] = mm(b1, b1.t[0:1, 0:512], onesc_b[:, :], sq,
                                 start=(kt == 0), stop=(kt == KT - 1), inc="t")
                lastmm = mml[KT - 1]
                sqslot.freed(("t", lastmm))
                rw = rowreg.take()
                mrow = row(R_M + ts, 512)
                rrow = row(R_RSTD + ts, 512)
                t0r = row(R_T0, 512)
                t1r = row(R_T1, 512)
                c = v_ts(mrow, b0.t[0:1, 0:512], 1.0 / 1024.0, None, ALU.mult,
                         waits=[("t", lastmm)] + rw)
                c2 = v_ts(t0r, b1.t[0:1, 0:512], 1.0 / 1024.0, EPS,
                          ALU.mult, ALU.add)
                b0.free = ("v", c)
                b1.free = ("v", c2)
                c3 = s_act(t1r, mrow, AF.Square, waits=[("v", c2)] + rw)
                c4 = v_tt(t0r, t0r, t1r, ALU.subtract, waits=[("s", c3)])
                c5 = s_act(t1r, t0r, AF.Ln, waits=[("v", c4)])
                c6 = s_act(rrow, t1r, AF.Exp, scale=-0.5, inc="s")
                stats_ms[0] = ("v", c)
                stats_ms[1] = ("s", c6)

        def bcast(bank, rowap, n, waits=(), inc=None):
            return mm(bank, bank.t[:, 0:n], onesr_f[0:1, :], rowap,
                      start=True, stop=True, waits=list(waits) + [CMS], inc=inc)

        def phase_attn_a(xw):
            A_GG = CBASE
            A_LNT = CBASE + 2048
            A_QK = CBASE + 4096
            A_V = CBASE + 6144
            A_GS = CBASE + 8192
            A_WS0 = CBASE + 10240
            A_WS1 = CBASE + 12288
            A_SC = CBASE + 14336
            A_SQ = CBASE + 15360
            A_SSC = CBASE + 16384

            lnT = lambda kt, a, b: ab(A_LNT, 4096)[:, 512 * kt + a:512 * kt + b]
            oT = lambda r, a, b: ab(A_LNT, 4096)[:, 512 * r + a:512 * r + b]
            qTs = lambda a, b: ab(A_QK, 2048)[:, a:b]
            kTs = lambda a, b: ab(A_QK + 1024, 2048)[:, a:b]
            ogd = lambda r: ab(A_QK, 4096)[:, 512 * r:512 * r + 512]
            vv = lambda a, b: ab(A_V, 4096)[:, a:b]
            Gs = lambda r: ab(A_GS, 4096)[:, 512 * r:512 * r + 512]
            gg = lambda a, b: af(A_GG, 2048)[:, a:b]
            wsl = [lambda a, b: ab(A_WS0, 4096)[:, a:b],
                   lambda a, b: ab(A_WS1, 4096)[:, a:b]]
            sc = lambda i: ab(A_SC + 64 * i, 128)

            ws = [Reg(), Reg()]
            r_lnT, r_qk, r_v, r_gs, r_gg = Reg(), Reg(), Reg(), Reg(), Reg()
            edec_free = [None, None]
            sbf_ready = [CMS] * H4
            scan_free = {}

            def stream_w2(w_hbm, col0, sl, rms_scale=False):
                src3 = w_hbm[:, col0:col0 + 512] \
                    .rearrange("(a p) n -> p a n", p=128)
                slot_ms = ws[sl].take()
                outms = []
                for i in range(2):
                    ssl = i % 2
                    qn = "dA" if ssl == 0 else "dB"
                    stga = af(STG0 if ssl == 0 else STG1, 2048)
                    dc = B.dma(stga.rearrange("p (a n) -> p a n", a=4),
                               src3[:, 4 * i:4 * (i + 1), :],
                               waits=stg[ssl].take(), q=qn)
                    wts = [(qn, dc)] + (slot_ms if i == 0 else [])
                    if not rms_scale:
                        cc = s_copy(wsl[sl](2048 * i, 2048 * (i + 1)), stga,
                                    waits=wts)
                        stg[ssl].freed(("s", cc))
                        outms.append(("s", cc))
                    else:
                        cc = None
                        for j in range(4):
                            kt = 4 * i + j
                            cc = v_ts(wsl[sl](2048 * i + 512 * j,
                                              2048 * i + 512 * j + 512),
                                      stga[:, 512 * j:512 * j + 512],
                                      cols[:, kt % 2:kt % 2 + 1], None,
                                      ALU.mult,
                                      waits=(wts if j == 0 else []) +
                                      ([DXA] if j == 0 else []))
                        stg[ssl].freed(("v", cc))
                        outms.append(("v", cc))
                return outms

            for gt in range(NG):
                ts = 512 * gt
                ln_stats(ts, 512, A_SQ, xw)
                bcast(P[2], row(R_M + ts, 512), 512, waits=[stats_ms[0]])
                cr_ = bcast(P[3], row(R_RSTD + ts, 512), 512,
                            waits=[stats_ms[1]], inc="t")
                lnw_ms = r_lnT.take()
                lastln = None
                for kt in range(KT):
                    f = af(FSC0, 512)
                    v_tt(f, xT[:, 2048 * kt + ts:2048 * kt + ts + 512],
                         P[2].t[:, 0:512], ALU.subtract,
                         waits=[("t", cr_)] +
                         ((lnw_ms + fscreg.take()) if kt == 0 else []))
                    lastln = v_tt(lnT(kt, 0, 512), f, P[3].t[:, 0:512],
                                  ALU.mult)
                P[2].free = ("v", lastln)
                P[3].free = ("v", lastln)
                lnw = ("v", lastln)
                fscreg.freed(lnw)

                qk_ms = r_qk.take()
                msq = stream_w2(w_aq, 0, 0)
                lastq = None
                for mt in range(4):
                    bank = P[mt % 2]
                    cgrp = None
                    for kt in range(KT):
                        pw = [msq[kt // 4], lnw] + (qk_ms if mt == 0 else [])
                        cgrp = mm(bank, bank.v(),
                                  wsl[0](512 * kt + 128 * mt,
                                         512 * kt + 128 * mt + 128),
                                  lnT(kt, 0, 512),
                                  start=(kt == 0), stop=(kt == KT - 1),
                                  waits=(pw if kt in (0, 4) else []),
                                  inc=("t" if kt == KT - 1 else None))
                    lastq = s_copy(qTs(512 * mt, 512 * mt + 512), bank.v(),
                                   waits=[("t", cgrp)], inc="s")
                    bank.free = ("s", lastq)
                    qk_t = cgrp
                msk = stream_w2(w_ak, 0, 1)
                lastk = None
                for mt in range(4):
                    bank = P[2 + mt % 2]
                    cgrp = None
                    for kt in range(KT):
                        pw = [msk[kt // 4], lnw]
                        cgrp = mm(bank, bank.v(),
                                  wsl[1](512 * kt + 128 * mt,
                                         512 * kt + 128 * mt + 128),
                                  lnT(kt, 0, 512),
                                  start=(kt == 0), stop=(kt == KT - 1),
                                  waits=(pw if kt in (0, 4) else []),
                                  inc=("t" if kt == KT - 1 else None))
                    lastk = s_copy(kTs(512 * mt, 512 * mt + 512), bank.v(),
                                   waits=[("t", cgrp)], inc="s")
                    bank.free = ("s", lastk)
                ws[0].freed(("t", qk_t))
                ws[1].freed(("t", cgrp))
                qk_done = ("s", lastk)
                v_ms = r_v.take()
                lastv = None
                for vh in range(2):
                    sl = vh
                    msv = stream_w2(w_av, 512 * vh, sl)
                    cgrp = None
                    for tm in range(4):
                        bank = P[(4 * vh + tm) % 2]
                        for kt in range(KT):
                            pw = [msv[kt // 4], lnw] + \
                                (v_ms if (vh == 0 and tm == 0) else [])
                            cgrp = mm(bank, bank.v(),
                                      lnT(kt, 128 * tm, 128 * tm + 128),
                                      wsl[sl](512 * kt, 512 * kt + 512),
                                      start=(kt == 0), stop=(kt == KT - 1),
                                      waits=(pw if kt in (0, 4) else []),
                                      inc=("t" if kt == KT - 1 else None))
                        lastv = s_copy(vv(1024 * tm + 512 * vh,
                                          1024 * tm + 512 * vh + 512),
                                       bank.v(), waits=[("t", cgrp)], inc="s")
                        bank.free = ("s", lastv)
                    ws[sl].freed(("t", cgrp))
                v_done = ("s", lastv)
                gs_ms = r_gs.take()
                lastg = None
                for vh in range(2):
                    sl = vh
                    msg = stream_w2(w_ag, 512 * vh, sl)
                    cgrp = None
                    for mt in range(4):
                        r = 4 * vh + mt
                        bank = P[2 + r % 2]
                        for kt in range(KT):
                            pw = [msg[kt // 4], lnw] + (gs_ms if r == 0 else [])
                            cgrp = mm(bank, bank.v(),
                                      wsl[sl](512 * kt + 128 * mt,
                                              512 * kt + 128 * mt + 128),
                                      lnT(kt, 0, 512),
                                      start=(kt == 0), stop=(kt == KT - 1),
                                      waits=(pw if kt in (0, 4) else []),
                                      inc=("t" if kt == KT - 1 else None))
                        lastg = s_act(Gs(r), bank.v(), AF.Silu,
                                      waits=[("t", cgrp)], inc="s")
                        bank.free = ("s", lastg)
                    ws[sl].freed(("t", cgrp))
                gs_done = ("s", lastg)
                bank = P[4]
                ct1m = None
                for kt in range(KT):
                    ct1m = mm(bank, bank.t[0:16, 0:512],
                              gk1b[:, 16 * kt:16 * kt + 16],
                              lnT(kt, 0, 512),
                              start=(kt == 0), stop=(kt == KT - 1),
                              waits=([("s", cg1), lnw, CMS] if kt == 0 else []),
                              inc=("t" if kt == KT - 1 else None))
                ct1 = s_copy(t1ext[0:16, ts:ts + 512], bank.t[0:16, 0:512],
                             waits=[("t", ct1m)], inc="s")
                bank.free = ("s", ct1)
                gg_ms = r_gg.take()
                lastgg = None
                fsc_ms = fscreg.take()
                for tm in range(4):
                    bank = P[5 + tm % 2]
                    cz = mm(bank, bank.v(),
                            t1ext[:, ts + 128 * tm:ts + 128 * tm + 128],
                            gk2b[:, :], start=True, stop=True,
                            waits=[("s", ct1), ("s", cg2)], inc="t")
                    f0, f1, f2 = af(FSC0, 512), af(FSC1, 512), af(FSC2, 512)
                    ca = s_act(f0, bank.v(), AF.Abs,
                               waits=[("t", cz)] + fsc_ms)
                    cmin = v_ts(f1, bank.v(), 0.0, None, ALU.min,
                                waits=[("s", ca)] + (gg_ms if tm == 0 else []))
                    bank.free = ("v", cmin)
                    s_act(f2, f0, AF.Exp, scale=-1.0)
                    cl = s_act(f0, f2, AF.Ln, bias=1.0)
                    cs_ = v_tt(f1, f1, f0, ALU.subtract, waits=[("s", cl)])
                    fsc_ms = [("v", cs_)]
                    lastgg = v_ts(gg(512 * tm, 512 * tm + 512), f1,
                                  1.0 / GLN, -3.0, ALU.mult, ALU.max)
                gg_done = ("v", lastgg)
                fscreg.freed(("v", lastgg), ("s", cl))

                last_co1 = None
                for ci in range(4):
                    tm = ci
                    for hd in range(H4):
                        par = hd % 2
                        qs, ks = sc(0 + par), sc(2 + par)
                        ptm, ktl = sc(4 + par), sc(6 + par)
                        eb, en = sc(8 + par), sc(10 + par)
                        bBT = P[0]
                        cbt = mm(bBT, bBT.v(0, 128),
                                 gg(512 * tm + 128 * hd,
                                    512 * tm + 128 * hd + 128),
                                 triu[:, :], start=True, stop=True,
                                 waits=[gg_done], inc="t")
                        ceb = s_act(eb, bBT.v(0, 128), AF.Exp,
                                    waits=[("t", cbt),
                                           scan_free.get(("eb", par))])
                        cen = s_act(en, bBT.v(0, 128), AF.Exp, scale=-1.0,
                                    waits=[scan_free.get(("en", par))])
                        ced = s_act(cols[:, 2 + par:3 + par],
                                    bBT.t[:, 127:128], AF.Exp,
                                    waits=[edec_free[par]], inc="s")
                        bBT.free = ("s", ced)
                        cqs = v_tt(qs, qTs(512 * hd + 128 * tm,
                                           512 * hd + 128 * tm + 128),
                                   eb, ALU.mult,
                                   waits=[("s", cen), qk_done,
                                          scan_free.get(("qs", par))])
                        cks = v_tt(ks, kTs(512 * hd + 128 * tm,
                                           512 * hd + 128 * tm + 128),
                                   en, ALU.mult,
                                   waits=[scan_free.get(("ks", par))])
                        scan_free[("eb", par)] = ("v", cqs)
                        scan_free[("en", par)] = ("v", cks)
                        bPT = P[1 + par]
                        cpt = mm(bPT, bPT.v(0, 128), ks, qs,
                                 start=True, stop=True,
                                 waits=[("v", cks)], inc="t")
                        cptm = v_tt(ptm, bPT.v(0, 128), triu[:, :], ALU.mult,
                                    waits=[("t", cpt),
                                           scan_free.get(("ptm", par))])
                        bPT.free = ("v", cptm)
                        bO = P[3 + par]
                        cot = None
                        for u in range(2):
                            mm(bO, bO.v(128 * u, 128 * u + 128),
                               vv(1024 * tm + 256 * hd + 128 * u,
                                  1024 * tm + 256 * hd + 128 * u + 128),
                               ptm, start=True, stop=False,
                               waits=([("v", cptm), v_done] if u == 0 else []))
                            cot = mm(bO, bO.v(128 * u, 128 * u + 128),
                                     Sbf[:, 256 * hd + 128 * u:
                                         256 * hd + 128 * u + 128],
                                     qs, start=False, stop=True,
                                     waits=[sbf_ready[hd]],
                                     inc=("t" if u == 1 else None))
                        scan_free[("ptm", par)] = ("t", cot)
                        scan_free[("qs", par)] = ("t", cot)
                        s_copy(oT(2 * hd, 128 * tm, 128 * tm + 128),
                               bO.v(0, 128), waits=[("t", cot)])
                        last_co1 = s_copy(oT(2 * hd + 1, 128 * tm,
                                             128 * tm + 128),
                                          bO.v(128, 256), inc="s")
                        bO.free = ("s", last_co1)
                        ctr = mm_tr(PB, PB.t[:, 0:128], ks, idb[:, :],
                                    waits=[("v", cks), DXA],
                                    inc="t")
                        scan_free[("ks", par)] = ("t", ctr)
                        ckt = v_copy(ktl, PB.t[:, 0:128],
                                     waits=[("t", ctr),
                                            scan_free.get(("ktl", par))])
                        PB.free = ("v", ckt)
                        bDS = P[5 + par]
                        cds = mm(bDS, bDS.v(0, 256), ktl,
                                 vv(1024 * tm + 256 * hd,
                                    1024 * tm + 256 * hd + 256),
                                 start=True, stop=True,
                                 waits=[("v", ckt)], inc="t")
                        scan_free[("ktl", par)] = ("t", cds)
                        ssc = af(A_SSC, 256)
                        ca1 = v_tt(ssc, S32[:, 256 * hd:256 * hd + 256],
                                   bDS.v(0, 256), ALU.add,
                                   waits=[("t", cds), ("s", ced)])
                        bDS.free = ("v", ca1)
                        ca2 = v_ts(S32[:, 256 * hd:256 * hd + 256], ssc,
                                   cols[:, 2 + par:3 + par], None, ALU.mult)
                        edec_free[par] = ("v", ca2)
                        csb = s_copy(Sbf[:, 256 * hd:256 * hd + 256],
                                     S32[:, 256 * hd:256 * hd + 256],
                                     waits=[("v", ca2)], inc="s")
                        sbf_ready[hd] = ("s", csb)
                ot_done = ("s", B.cnt["s"])

                prev_cbb = None
                last_ogd = None
                last_sq = None
                cstat = None
                prev_cstat = None
                for hd in range(H4):
                    bR = P[hd % 2]
                    for rr in range(2):
                        sq = ab(A_SQ + 256 * rr, 512)
                        csq = s_act(sq, oT(2 * hd + rr, 0, 512), AF.Square,
                                    waits=[ot_done, prev_cstat] +
                                    (sqslot.take() if (hd == 0 and rr == 0)
                                     else []))
                        cstat = mm(bR, bR.t[0:1, 0:512], onesc_b[:, :], sq,
                                   start=(rr == 0), stop=(rr == 1),
                                   waits=[("s", csq)],
                                   inc=("t" if rr == 1 else None))
                        last_sq = csq
                    prev_cstat = ("t", cstat)
                    cv0 = v_ts(row(R_T0, 512), bR.t[0:1, 0:512],
                               1.0 / 256.0, EPS, ALU.mult, ALU.add,
                               waits=[("t", cstat), prev_cbb])
                    bR.free = ("v", cv0)
                    cl1 = s_act(row(R_T1, 512), row(R_T0, 512), AF.Ln,
                                waits=[("v", cv0)])
                    cl2 = s_act(row(R_T0, 512), row(R_T1, 512), AF.Exp,
                                scale=-0.5, inc="s")
                    bB = P[2 + hd % 2]
                    cbb = bcast(bB, row(R_T0, 512), 512,
                                waits=[("s", cl2)], inc="t")
                    prev_cbb = ("t", cbb)
                    for u in range(2):
                        r = 2 * hd + u
                        f = af(FSC0 if u == 0 else FSC1, 512)
                        v_tt(f, oT(r, 0, 512), bB.v(), ALU.mult,
                             waits=[("t", cbb)] +
                             (([gs_done] + fscreg.take())
                              if (hd == 0 and u == 0) else []))
                        last_ogd = v_tt(ogd(r), f, Gs(r), ALU.mult)
                    bB.free = ("v", last_ogd)
                ogd_done = ("v", last_ogd)
                fscreg.freed(("v", last_ogd))
                sqslot.freed(("t", cstat))
                rowreg.freed(("t", prev_cbb[1]))

                lastY = None
                for dh in range(2):
                    sl = dh
                    msw = stream_w2(w_ao, 512 * dh, sl, rms_scale=True)
                    cgrp = None
                    for dm in range(4):
                        d_glob = 4 * dh + dm
                        bank = P[4 + dm % 2]
                        for vk in range(KT):
                            cgrp = mm(bank, bank.v(),
                                      wsl[sl](512 * vk + 128 * dm,
                                              512 * vk + 128 * dm + 128),
                                      ogd(vk),
                                      start=(vk == 0), stop=(vk == KT - 1),
                                      waits=([msw[vk // 4], ogd_done]
                                             if vk in (0, 4) else []),
                                      inc=("t" if vk == KT - 1 else None))
                        cY = v_tt(xT[:, 2048 * d_glob + ts:
                                     2048 * d_glob + ts + 512],
                                  xT[:, 2048 * d_glob + ts:
                                     2048 * d_glob + ts + 512],
                                  bank.v(), ALU.add,
                                  waits=[("t", cgrp)], inc="v")
                        bank.free = ("v", cY)
                        lastY = cY
                    ws[sl].freed(("t", cgrp))
                xw = ("v", lastY)

                r_lnT.freed(("v", lastY), ("s", last_sq))
                r_qk.freed(("t", B.cnt["t"]))
                r_v.freed(("t", B.cnt["t"]))
                r_gs.freed(("v", last_ogd))
                r_gg.freed(("t", B.cnt["t"]))
            return xw

        def phase_ffn(xw):
            F_LNTH = CBASE
            F_W1 = CBASE + 4096
            F_W3 = CBASE + 8192
            F_W2 = CBASE + 12288
            F_HT = CBASE + 14336
            F_WB = CBASE + 15360
            F_SL = CBASE + 15872
            F_WEM = CBASE + 16384
            assert F_WEM + 2048 <= 24832
            lnTH = lambda kt, a, b: ab(F_LNTH, 8192)[:, 1024 * kt + a:1024 * kt + b]
            w1s = lambda sl, a, b: ab(F_W1 + 2048 * sl, 4096)[:, a:b]
            w3s = lambda sl, a, b: ab(F_W3 + 2048 * sl, 4096)[:, a:b]
            w2s = lambda a, b: ab(F_W2, 4096)[:, a:b]
            hTs = lambda a, b: ab(F_HT, 2048)[:, a:b]
            F_WB2 = 24192
            wbs = lambda par, gl: ab((F_WB if par == 0 else F_WB2) + 256 * gl,
                                     512)
            sls = lambda i: ab(F_SL + 256 * i, 512)
            wem = arena[0:8, F_WEM:F_WEM + 2048]
            lrw = arena[0:8, F_WEM:F_WEM + 2048]
            wmp = af(F_WEM + 2048, 128)
            C_RM, C_M1, C_R1, C_M2, C_R2, C_DD, C_ED, C_W1c, C_W2c, C_T0 = \
                128, 129, 130, 131, 132, 133, 134, 135, 136, 137
            C_MC, C_RC = 48, 64
            C_LT, C_LT2, C_EQ, C_TT, C_L2 = 80, 88, 96, 104, 112

            fence = [("t", B.cnt["t"]), ("v", B.cnt["v"]), ("s", B.cnt["s"])]
            wsF1 = [Reg(), Reg()]
            wsF3 = [Reg(), Reg()]
            wsF2 = Reg()
            wb_dones = [None, None]
            r_lnTH, r_hT, r_wem = Reg(), Reg(), Reg()
            r_scrd = [Reg(), Reg()]
            r_wb = [Reg(), Reg()]
            dwr_pend = [None, None]
            for rg in (wsF1[0], wsF1[1], wsF3[0], wsF3[1], wsF2, r_lnTH, r_hT,
                       r_wb[0], r_wb[1], r_wem, r_scrd[0], r_scrd[1]):
                rg.freed(*fence)

            ln_stats(0, N, F_HT, xw)
            stm, str_ = stats_ms[0], stats_ms[1]

            lrw_ms = []
            for g in range(NG):
                bank = P[2 + g % 2]
                for kt in range(KT):
                    mm(bank, bank.t[0:8, 0:512],
                       wgf[:, E8 * kt:E8 * kt + E8],
                       xT[:, 2048 * kt + 512 * g:2048 * kt + 512 * g + 512],
                       start=(kt == 0), stop=False,
                       waits=([xw] + fence if (g == 0 and kt == 0) else []))
                cg = mm(bank, bank.t[0:8, 0:512], cols[0:1, 152:160],
                        row(R_M + 512 * g, 512),
                        start=False, stop=True,
                        waits=[stm, NCMS] if g == 0 else [], inc="t")
                ce = s_copy(lrw[0:8, 512 * g:512 * g + 512], bank.t[0:8, 0:512],
                            waits=[("t", cg)] + (fence if g == 0 else []),
                            inc="s")
                bank.free = ("s", ce)
                lrw_ms.append(ce)
            lrw_done = ("s", lrw_ms[-1])

            def emit_lnth(th0):
                lnth_ms = r_lnTH.take() + fscreg.take()
                lastln = None
                lnth_gl = [None, None]
                for gl in range(2):
                    ts = th0 + 512 * gl
                    bcast(P[2], row(R_M + ts, 512), 512, waits=[stm])
                    crr = bcast(P[3], row(R_RSTD + ts, 512), 512,
                                waits=[str_], inc="t")
                    for kt in range(KT):
                        f = af(FSC0, 512)
                        v_tt(f, xT[:, 2048 * kt + ts:2048 * kt + ts + 512],
                             P[2].t[:, 0:512], ALU.subtract,
                             waits=[("t", crr)] +
                             (lnth_ms if (gl == 0 and kt == 0) else []))
                        lastln = v_tt(lnTH(kt, 512 * gl, 512 * gl + 512), f,
                                      P[3].t[:, 0:512], ALU.mult)
                    P[2].free = ("v", lastln)
                    P[3].free = ("v", lastln)
                    lnth_gl[gl] = ("v", lastln)
                fscreg.freed(("v", lastln))
                return lnth_gl

            lnth0_gl = emit_lnth(0)
            stq = sqslot.take()
            S_LT, S_IO, S_EQ, S_L2, S_T2, S_EX, S_RB, S_TA, S_TB, S_DD, \
                S_ED = (F_HT + o for o in
                        (0, 128, 256, 384, 512, 640, 768, 896, 960, 992, 1008))
            C_M1n, C_RCn, C_R12, C_M2n, C_T0n = 48, 64, 80, 96, 112
            v3 = lambda off, t, e: af(off, t * e).rearrange(
                "p (t e) -> p t e", e=e)
            col3 = lambda c0: cols[:, c0:c0 + 16].rearrange(
                "p (t o) -> p t o", o=1)

            def expand8(src_c0, waits=()):
                d = v3(S_EX, 16, 8)
                v_copy(d[:, :, 0:1], col3(src_c0), waits=waits)
                for k in (1, 2, 4):
                    v_copy(d[:, :, k:2 * k], d[:, :, 0:k])

            def treemax(src_off, dst_c0):
                a = v3(src_off, 16, 8)
                ta = v3(S_TA, 16, 4)
                tb = v3(S_TB, 16, 2)
                v_tt(ta, a[:, :, 0:4], a[:, :, 4:8], ALU.max)
                v_tt(tb, ta[:, :, 0:2], ta[:, :, 2:4], ALU.max)
                return v_tt(col3(dst_c0), tb[:, :, 0:1], tb[:, :, 1:2],
                            ALU.max)

            ct45 = [None, None]
            for tt in range(16):
                bk = P[4 + tt // 8]
                c_ = mm_tr(bk, bk.t[:, 8 * (tt % 8):8 * (tt % 8) + 8],
                           lrw[0:8, 128 * tt:128 * tt + 128], idf[0:8, 0:8],
                           waits=([lrw_done] + fence) if tt == 0 else [],
                           inc=("t" if tt % 8 == 7 else None))
                if tt % 8 == 7:
                    ct45[tt // 8] = c_
            ctr16 = None
            for tt in range(16):
                ctr16 = mm_tr(P[6], P[6].t[:, tt:tt + 1],
                              row(R_RSTD + 128 * tt, 128), idf[0:1, 0:1],
                              waits=[str_] if tt == 0 else [],
                              inc=("t" if tt == 15 else None))
            c_io = v_copy(af(S_IO, 8), iota[:, :], waits=stq + fence)
            for k in (8, 16, 32, 64):
                v_copy(af(S_IO + k, k), af(S_IO, k))
            crc = v_copy(cols[:, C_RCn:C_RCn + 16], P[6].t[:, 0:16],
                         waits=[("t", ctr16)])
            P[6].free = ("v", crc)
            d = v3(S_RB, 16, 8)
            v_copy(d[:, :, 0:1], col3(C_RCn))
            for k in (1, 2, 4):
                v_copy(d[:, :, k:2 * k], d[:, :, 0:k])
            ce4 = v_copy(af(S_LT, 64), P[4].t[:, 0:64], waits=[("t", ct45[0])])
            P[4].free = ("v", ce4)
            ce5 = v_copy(af(S_LT + 64, 64), P[5].t[:, 0:64],
                         waits=[("t", ct45[1])])
            P[5].free = ("v", ce5)
            v_tt(af(S_LT, 128), af(S_LT, 128), af(S_RB, 128), ALU.mult)
            v_copy(af(S_RB, 8), sbg[:, E8:2 * E8])
            for k in (8, 16, 32, 64):
                v_copy(af(S_RB + k, k), af(S_RB, k))
            v_tt(af(S_LT, 128), af(S_LT, 128), af(S_RB, 128), ALU.add)
            treemax(S_LT, C_M1n)
            expand8(C_M1n)
            v_tt(af(S_EQ, 128), af(S_LT, 128), af(S_EX, 128), ALU.is_equal)
            v_tt(af(S_EQ, 128), af(S_EQ, 128), af(S_IO, 128), ALU.mult)
            treemax(S_EQ, C_R12)
            expand8(C_R12)
            v_tt(af(S_EQ, 128), af(S_IO, 128), af(S_EX, 128), ALU.is_equal)
            v_stt(af(S_L2, 128), af(S_EQ, 128), -1e30, af(S_LT, 128),
                  ALU.mult, ALU.add)
            treemax(S_L2, C_M2n)
            expand8(C_M2n)
            v_tt(af(S_T2, 128), af(S_L2, 128), af(S_EX, 128), ALU.is_equal)
            v_tt(af(S_T2, 128), af(S_T2, 128), af(S_IO, 128), ALU.mult)
            treemax(S_T2, C_R12)
            expand8(C_R12)
            v_tt(af(S_T2, 128), af(S_IO, 128), af(S_EX, 128), ALU.is_equal)
            cdd = v_tt(af(S_DD, 16), cols[:, C_M2n:C_M2n + 16],
                       cols[:, C_M1n:C_M1n + 16], ALU.subtract)
            ced = s_act(af(S_ED, 16), af(S_DD, 16), AF.Exp,
                        waits=[("v", cdd)])
            v_ts(cols[:, C_T0n:C_T0n + 16], af(S_ED, 16), 1.0, None,
                 ALU.add, waits=[("s", ced)])
            v_recip(cols[:, C_M2n:C_M2n + 16], cols[:, C_T0n:C_T0n + 16])
            v_tt(cols[:, C_R12:C_R12 + 16], af(S_ED, 16),
                 cols[:, C_M2n:C_M2n + 16], ALU.mult)
            expand8(C_M2n)
            v_tt(af(S_EQ, 128), af(S_EQ, 128), af(S_EX, 128), ALU.mult)
            expand8(C_R12)
            v_tt(af(S_T2, 128), af(S_T2, 128), af(S_EX, 128), ALU.mult)
            cwm = v_tt(wmp[:, 0:128], af(S_EQ, 128), af(S_T2, 128), ALU.add)
            sqslot.freed(("v", cwm), ("s", ced))
            r_hT.freed(*(fence + [("v", cwm), ("s", ced)]))
            wem_done = None
            for q in range(4):
                bW = P[2 + q]
                cl = None
                for i in range(4):
                    tt = 4 * q + i
                    cl = mm_tr(bW, bW.t[0:8, 128 * i:128 * i + 128],
                               wmp[:, 8 * tt:8 * tt + 8], idf[:, :],
                               waits=[("v", cwm)] if tt == 0 else [],
                               inc=("t" if i == 3 else None))
                cwe = v_copy(wem[0:8, 512 * q:512 * q + 512],
                             bW.t[0:8, 0:512],
                             waits=[("t", cl)] +
                             (r_wem.take() if q == 0 else []))
                bW.free = ("v", cwe)
                wem_done = ("v", cwe)


            lnth_gl_by_th = [lnth0_gl, None]
            lastY = None
            lastW2mm = None
            sl_free = [None, None]
            prev_csl = None
            rrb = [0]
            jobs = [(TH, e, s) for TH in range(2) for e in range(E8)
                    for s in range(E8)]
            NJ = len(jobs)
            castms = {}
            dinfo = {}

            def issue_dma(gp):
                if gp >= 6 * NJ:
                    return
                j, pc = gp // 6, gp % 6
                TH, e, s = jobs[j]
                par = j % 2
                if pc < 2:
                    src3 = w_f1[e][:, 512 * s:512 * s + 512] \
                        .rearrange("(a p) n -> p a n", p=128)
                    dstf, reg, nsub, i = \
                        (lambda a, b, par=par: w1s(par, a, b)), wsF1[par], 4, pc
                elif pc < 4:
                    src3 = w_f3[e][:, 512 * s:512 * s + 512] \
                        .rearrange("(a p) n -> p a n", p=128)
                    dstf, reg, nsub, i = \
                        (lambda a, b, par=par: w3s(par, a, b)), wsF3[par], 4, pc - 2
                else:
                    src3 = w_f2[e][512 * s:512 * s + 512, :] \
                        .rearrange("(a p) n -> p a n", p=128)
                    dstf, reg, nsub, i = w2s, wsF2, 2, pc - 4
                ssl = gp % 2
                qn = "dA" if ssl == 0 else "dB"
                stga = af(STG0 if ssl == 0 else STG1, 2048)
                dc = B.dma(stga.rearrange("p (a n) -> p a n", a=nsub),
                           src3[:, nsub * i:nsub * (i + 1), :],
                           waits=stg[ssl].take(), q=qn)
                dinfo[gp] = (qn, dc, stga, dstf, 2048 * i, i == 0, reg)

            def emit_cast(gp):
                if gp >= 6 * NJ:
                    return
                qn, dc, stga, dstf, off, first, reg = dinfo.pop(gp)
                wts = [(qn, dc)] + (reg.take() if first else [])
                cc = s_copy(dstf(off, off + 2048), stga, waits=wts)
                ssl = 0 if qn == "dA" else 1
                stg[ssl].freed(("s", cc))
                castms.setdefault(gp // 6, []).append(("s", cc))

            def drip(cur):
                emit_cast(cur[0])
                issue_dma(cur[0] + 2)
                cur[0] += 1

            cur = [0]
            issue_dma(0)
            issue_dma(1)
            for _ in range(6):
                drip(cur)


            for j, (TH, e, s_) in enumerate(jobs):
                th0 = 1024 * TH
                sl = j % 2
                epar = e % 2
                if s_ == 0 and e == 0 and TH == 1:
                    lnth_gl_by_th[1] = emit_lnth(th0)
                if s_ == 0 and e == 0:
                    for gl in range(2):
                        ts = th0 + 512 * gl
                        cmr = v_tt(row(R_T0, 512), row(R_M + ts, 512),
                                   row(R_RSTD + ts, 512), ALU.mult,
                                   waits=rowreg.take() +
                                   (r_scrd[1].free or []))
                        bcast(P[4], row(R_T0, 512), 512, waits=[("v", cmr)])
                        cbr = bcast(P[5], row(R_RSTD + ts, 512), 512, inc="t")
                        lastadd = None
                        for kt in range(KT):
                            f0 = af(FSC0, 512)
                            f1 = af(FSC1, 512)
                            v_tt(f0, xT[:, 2048 * kt + ts:2048 * kt + ts + 512],
                                 P[5].t[:, 0:512], ALU.mult,
                                 waits=[("t", cbr)])
                            v_tt(f1, f0, P[4].t[:, 0:512], ALU.subtract)
                            lastadd = v_tt(
                                xT[:, 2048 * kt + ts:2048 * kt + ts + 512],
                                xT[:, 2048 * kt + ts:2048 * kt + ts + 512],
                                f1, ALU.add)
                        P[4].free = ("v", lastadd)
                        P[5].free = ("v", lastadd)
                        rowreg.freed(("v", lastadd))
                lnth_gl = lnth_gl_by_th[TH]
                if s_ == 0:
                    if dwr_pend[epar] is None:
                        dwr_pend[epar] = B.dma(
                            row(R_SCRD if epar == 0 else R_T0, 1024),
                            wem[e:e + 1, th0:th0 + 1024],
                            waits=[wem_done] + r_scrd[epar].take(),
                            q=("dC" if epar == 0 else "dD"))
                    dwr = dwr_pend[epar]
                    dwr_pend[epar] = None
                    scrd0 = R_SCRD if epar == 0 else R_T0
                    cwb = None
                    cwm_ = None
                    wb_ms = r_wb[epar].take()
                    for gl in range(2):
                        cwm_ = mm(P[6], P[6].t[:, 0:512], onesr_f[0:1, :],
                                  row(scrd0 + 512 * gl, 512),
                                  start=True, stop=True,
                                  waits=[("dC" if epar == 0 else "dD", dwr),
                                         CMS], inc="t")
                        cwb = s_copy(wbs(epar, gl), P[6].t[:, 0:512],
                                     waits=[("t", cwm_)] + wb_ms, inc="s")
                        P[6].free = ("s", cwb)
                    r_scrd[epar].freed(("t", cwm_))
                    wb_dones[epar] = ("s", cwb)
                    nj = j + E8
                    if nj < NJ:
                        nTH, ne, _ = jobs[nj]
                        npar = ne % 2
                        dwr_pend[npar] = B.dma(
                            row(R_SCRD if npar == 0 else R_T0, 1024),
                            wem[ne:ne + 1, 1024 * nTH:1024 * nTH + 1024],
                            waits=[wem_done] + r_scrd[npar].take() +
                            (rowreg.free or []),
                            q=("dC" if npar == 0 else "dD"))
                wb_done = wb_dones[epar]
                ms1 = castms[j][0:2]
                ms3 = castms[j][2:4]
                ms2 = castms[j][4:6]
                ndrip = [4 if j + 1 < NJ else 0]
                hms = {}
                ubank = {}

                def emit_u_mms(gl, mt):
                    bU1 = P[rrb[0] % 6]
                    rrb[0] += 1
                    bU3 = P[rrb[0] % 6]
                    rrb[0] += 1
                    cu1 = cu3 = None
                    for kt in range(KT):
                        w0 = ([ms1[kt // 4], lnth_gl[gl]]
                              if kt in (0, 4) else [])
                        cu1 = mm(bU1, bU1.v(),
                                 w1s(sl, 512 * kt + 128 * mt,
                                     512 * kt + 128 * mt + 128),
                                 lnTH(kt, 512 * gl, 512 * gl + 512),
                                 start=(kt == 0), stop=(kt == KT - 1),
                                 waits=w0,
                                 inc=("t" if kt == KT - 1 else None))
                    for kt in range(KT):
                        w0 = ([ms3[kt // 4]] if kt in (0, 4) else [])
                        cu3 = mm(bU3, bU3.v(),
                                 w3s(sl, 512 * kt + 128 * mt,
                                     512 * kt + 128 * mt + 128),
                                 lnTH(kt, 512 * gl, 512 * gl + 512),
                                 start=(kt == 0), stop=(kt == KT - 1),
                                 waits=w0,
                                 inc=("t" if kt == KT - 1 else None))
                    ubank[gl, mt] = (bU1, bU3, cu1, cu3)

                def emit_chain(gl, mt, hT_ms):
                    nonlocal prev_csl
                    bU1, bU3, cu1, cu3 = ubank.pop((gl, mt))
                    f0 = af(FSC0, 512)
                    f1 = af(FSC1, 512)
                    ct1_ = v_tt(f0, bU1.v(), wbs(epar, gl), ALU.mult,
                                waits=[("t", cu1), wb_done, prev_csl])
                    csl = s_act(sls(mt % 2), f0, AF.Silu,
                                waits=[("v", ct1_), sl_free[mt % 2]])
                    prev_csl = ("s", csl)
                    if ndrip[0] > 0:
                        drip(cur)
                        ndrip[0] -= 1
                    ct3_ = v_tt(f1, bU3.v(), wbs(epar, gl), ALU.mult,
                                waits=[("t", cu3)])
                    bU1.free = ("v", ct1_)
                    bU3.free = ("v", ct3_)
                    lasth = v_tt(hTs(512 * mt, 512 * mt + 512),
                                 sls(mt % 2), f1, ALU.mult,
                                 waits=[("s", csl)] + hT_ms)
                    sl_free[mt % 2] = ("v", lasth)
                    hms[gl][mt] = ("v", lasth)

                def emit_y(gl):
                    nonlocal lastY, lastW2mm
                    ts = th0 + 512 * gl
                    h_ms = hms[gl]
                    held = set()
                    for (bb1, bb3, _, _) in ubank.values():
                        held.add(id(bb1))
                        held.add(id(bb3))
                    cylast = None
                    for d in range(KT):
                        bY = P[rrb[0] % 6]
                        rrb[0] += 1
                        while id(bY) in held:
                            bY = P[rrb[0] % 6]
                            rrb[0] += 1
                        cy = None
                        for k in range(4):
                            w0 = [h_ms[k]] + \
                                ([ms2[k // 2]] if k in (0, 2) else [])
                            cy = mm(bY, bY.v(),
                                    w2s(1024 * k + 128 * d,
                                        1024 * k + 128 * d + 128),
                                    hTs(512 * k, 512 * k + 512),
                                    start=(k == 0), stop=(k == 3),
                                    waits=w0,
                                    inc=("t" if k == 3 else None))
                            cylast = cy
                        lastY = v_tt(
                            xT[:, 2048 * d + ts:2048 * d + ts + 512],
                            xT[:, 2048 * d + ts:2048 * d + ts + 512],
                            bY.v(), ALU.add,
                            waits=[("t", cy)], inc="v")
                        bY.free = ("v", lastY)
                        lastW2mm = cy
                    return cylast

                hms[0] = [None] * 4
                hT_ms0 = r_hT.take()
                for mt in range(4):
                    emit_u_mms(0, mt)
                    emit_chain(0, mt, hT_ms0 if mt == 0 else [])
                emit_u_mms(1, 0)
                cyl0 = emit_y(0)
                hms[1] = [None] * 4
                hT_ms1 = r_hT.take()
                r_hT.freed(("t", cyl0))
                hT_ms1 = hT_ms1 + r_hT.take()
                emit_chain(1, 0, hT_ms1)
                for mt in range(1, 4):
                    emit_u_mms(1, mt)
                    emit_chain(1, mt, [])
                cyl1 = emit_y(1)
                r_hT.freed(("t", cyl1))
                wsF1[sl].freed(("t", lastW2mm))
                wsF3[sl].freed(("t", lastW2mm))
                wsF2.freed(("t", lastW2mm))
                if j + 1 < NJ:
                    drip(cur)
                    drip(cur)
                if s_ == E8 - 1:
                    r_wb[epar].freed(("v", B.cnt["v"]))
                if s_ == E8 - 1 and e == E8 - 1:
                    fscreg.freed(("v", B.cnt["v"]), ("s", B.cnt["s"]))
                    r_lnTH.freed(("t", lastW2mm))
            return ("v", lastY)

        def phase_attn_b(xw):
            B_LNT = CBASE
            B_QS = CBASE + 8192
            B_KS = CBASE + 9216
            B_VM = CBASE + 10240
            B_AAB = CBASE + 11264
            B_SQ = CBASE + 11392
            B_US = CBASE + 11392
            B_WT = CBASE + 11904
            B_WS = CBASE + 12416
            B_BIAS = CBASE + 13440
            assert B_BIAS + 4096 <= 24832
            lnT = lambda kt, a, b: ab(B_LNT, 16384)[:, 2048 * kt + a:2048 * kt + b]
            QS = lambda a, b: ab(B_QS, 2048)[:, a:b]
            KS = lambda a, b: ab(B_KS, 2048)[:, a:b]
            VM = lambda a, b: ab(B_VM, 2048)[:, a:b]
            aab = lambda i: ab(B_AAB + 64 * i, 128)
            uss = lambda i: ab(B_US + 256 * i, 512)
            B_WT2 = 23680
            B_FSC3 = 23168
            wts = lambda i: ab((B_WT + 256 * (i % 2)) if i < 2
                               else (B_WT2 + 256 * (i - 2)), 512)
            bfsc = lambda i: af([FSC0, FSC1, FSC2, B_FSC3][i], 512)
            wsp = lambda sl: ab(B_WS + 512 * sl, 1024)
            bias = lambda di, a, b: ab(B_BIAS, 8192)[:, 512 * di + a:512 * di + b]

            fence = [("t", B.cnt["t"]), ("v", B.cnt["v"]), ("s", B.cnt["s"])]
            r_lnT, r_qs, r_ks, r_vm, r_bias = Reg(), Reg(), Reg(), Reg(), Reg()
            stg2 = [Reg(), Reg()]
            stg2[0].freed(*(fence + (stg[0].free or [])))
            stg2[1].freed(*(fence + (stg[1].free or [])))
            dq_next = [None, None]
            r_us = [Reg(), Reg()]
            r_wt = [Reg(), Reg(), Reg(), Reg()]
            r_aab = [Reg(), Reg()]
            wsB = [Reg(), Reg()]
            for rg in (r_lnT, r_qs, r_ks, r_vm, r_bias, r_us[0], r_us[1],
                       r_wt[0], r_wt[1], r_wt[2], r_wt[3],
                       r_aab[0], r_aab[1], wsB[0], wsB[1]):
                rg.freed(*fence)

            ln_stats(0, N, B_SQ, xw)
            stm, str_ = stats_ms[0], stats_ms[1]
            ms_stats = ("t", B.cnt["t"])
            for rg in (r_us[0], r_us[1], r_wt[0], r_wt[1]):
                rg.freed(*(fence + [ms_stats]))
            lnT_ms = r_lnT.take() + fscreg.take()
            lastln = None
            for g in range(NG):
                ts = 512 * g
                bcast(P[2], row(R_M + ts, 512), 512, waits=[stm])
                crr = bcast(P[3], row(R_RSTD + ts, 512), 512,
                            waits=[str_], inc="t")
                for kt in range(KT):
                    f = af(FSC0, 512)
                    v_tt(f, xT[:, 2048 * kt + ts:2048 * kt + ts + 512],
                         P[2].t[:, 0:512], ALU.subtract,
                         waits=[("t", crr)] +
                         (lnT_ms if (g == 0 and kt == 0) else []))
                    lastln = v_tt(lnT(kt, ts, ts + 512), f,
                                  P[3].t[:, 0:512], ALU.mult)
                P[2].free = ("v", lastln)
                P[3].free = ("v", lastln)
            lnw = ("v", lastln)
            fscreg.freed(lnw)

            def proj_dma(w_hbm, h, sl):
                src3 = w_hbm[:, 128 * h:128 * h + 128] \
                    .rearrange("(a p) n -> p a n", p=128)
                qn = "dC" if sl == 0 else "dD"
                stga = af((STG0 if sl == 0 else STG1) + 1024, 1024)
                dc = B.dma(stga.rearrange("p (a n) -> p a n", a=8),
                           src3, waits=stg2[sl].take(), q=qn)
                return (qn, dc, stga)

            def proj_cast(dmainfo, sl, reg):
                qn, dc, stga = dmainfo
                cc = s_copy(wsp(sl), stga, waits=[(qn, dc)] + reg.take())
                stg2[sl].freed(("s", cc))
                return ("s", cc)

            lastxT = None
            for h in range(H8):
                if dq_next[0] is not None:
                    dq, dk = dq_next
                    dq_next[0] = dq_next[1] = None
                else:
                    dq = proj_dma(w_bq, h, 0)
                    dk = proj_dma(w_bk, h, 1)
                bias_ms = r_bias.take()
                bias_dmas = {}
                bsrc = bias_exp[h].rearrange("(a p) n -> p a n", p=128)

                def bias_dma(i):
                    ssl = i % 2
                    qn = "dA" if ssl == 0 else "dB"
                    stga = af(STG0 if ssl == 0 else STG1, 1024)
                    dcb = B.dma(stga.rearrange("p (a n) -> p a n", a=2),
                                bsrc[:, 2 * i:2 * (i + 1), :],
                                waits=stg[ssl].take(), q=qn)
                    bias_dmas[i] = (qn, dcb, stga)
                bias_dma(7)
                bias_dma(6)
                cq = proj_cast(dq, 0, wsB[0])
                ck = proj_cast(dk, 1, wsB[1])
                qs_ms = r_qs.take()
                lastq = None
                for g in range(NG):
                    bank = P[g]
                    cgrp = None
                    for kt in range(KT):
                        cgrp = mm(bank, bank.v(),
                                  wsp(0)[:, 128 * kt:128 * kt + 128],
                                  lnT(kt, 512 * g, 512 * g + 512),
                                  start=(kt == 0), stop=(kt == KT - 1),
                                  waits=([cq, lnw] + (qs_ms if g == 0 else [])
                                         if kt == 0 else []),
                                  inc=("t" if kt == KT - 1 else None))
                    lastq = s_act(QS(512 * g, 512 * g + 512), bank.v(),
                                  AF.Silu, waits=[("t", cgrp)], inc="s")
                    bank.free = ("s", lastq)
                qs_done = ("s", lastq)
                wsB[0].freed(("t", cgrp))
                ks_ms = r_ks.take()
                lastk = None
                for g in range(NG):
                    bank = P[g]
                    cgrp = None
                    for kt in range(KT):
                        cgrp = mm(bank, bank.v(),
                                  wsp(1)[:, 128 * kt:128 * kt + 128],
                                  lnT(kt, 512 * g, 512 * g + 512),
                                  start=(kt == 0), stop=(kt == KT - 1),
                                  waits=([ck, lnw] + (ks_ms if g == 0 else [])
                                         if kt == 0 else []),
                                  inc=("t" if kt == KT - 1 else None))
                    lastk = s_act(KS(512 * g, 512 * g + 512), bank.v(),
                                  AF.Silu, waits=[("t", cgrp)], inc="s")
                    bank.free = ("s", lastk)
                ks_done = ("s", lastk)
                wsB[1].freed(("t", cgrp))
                dv_ = proj_dma(w_bv, h, 0)
                du_ = proj_dma(w_bu, h, 1)
                cv = proj_cast(dv_, 0, wsB[0])
                cu = proj_cast(du_, 1, wsB[1])
                if h + 1 < H8:
                    dq_next[0] = proj_dma(w_bq, h + 1, 0)
                    dq_next[1] = proj_dma(w_bk, h + 1, 1)
                for a in range(16):
                    par = a % 2
                    caa = s_act(aab(par), KS(128 * a, 128 * a + 128), AF.Abs,
                                waits=[ks_done] + r_aab[par].take())
                    cms_ = mm(P[6], P[6].t[:, 0:1], aab(par), onesc_b[:, :],
                              start=True, stop=True,
                              waits=[("s", caa), CMS], inc="t")
                    r_aab[par].freed(("t", cms_))
                    cmk = v_ts(cols[:, 32 + a:33 + a], P[6].t[:, 0:1], 0.0,
                               None, ALU.is_gt, waits=[("t", cms_)])
                    P[6].free = ("v", cmk)
                mk_done = ("v", cmk)
                vm_ms = r_vm.take()
                lastvm = None
                for m in range(16):
                    bank = P[4 + m % 3]
                    cgrp = None
                    for kt in range(KT):
                        cgrp = mm(bank, bank.v(0, 128),
                                  lnT(kt, 128 * m, 128 * m + 128),
                                  wsp(0)[:, 128 * kt:128 * kt + 128],
                                  start=(kt == 0), stop=(kt == KT - 1),
                                  waits=([cv, lnw] if kt == 0 else []),
                                  inc=("t" if kt == KT - 1 else None))
                    par = m % 2
                    csv = s_act(aab(par), bank.v(0, 128), AF.Silu,
                                waits=[("t", cgrp)] + r_aab[par].take())
                    lastvm = v_ts(VM(128 * m, 128 * m + 128), aab(par),
                                  cols[:, 32 + m:33 + m], None, ALU.mult,
                                  waits=[("s", csv), mk_done] +
                                  (vm_ms if m == 0 else []))
                    r_aab[par].freed(("v", lastvm))
                    bank.free = ("v", lastvm)
                vm_done = ("v", lastvm)
                wsB[0].freed(("t", cgrp))
                bias_cms = {}

                def bias_cast_one(i):
                    qn, dcb, stga = bias_dmas[i]
                    cb = s_copy(ab(B_BIAS + 512 * i, 1024), stga,
                                waits=[(qn, dcb)] +
                                (bias_ms if i == 7 else []))
                    stg[0 if qn == "dA" else 1].freed(("s", cb))
                    bias_cms[i] = ("s", cb)
                    if i - 2 >= 0:
                        bias_dma(i - 2)

                bias_cast_one(7)
                bias_cast_one(6)
                drip_pieces = {0: [5, 4], 1: [3, 2], 2: [1, 0], 3: []}
                for G in range(NG):
                    gp = G % 2
                    bank = P[6]
                    cgrp = None
                    for kt in range(KT):
                        cgrp = mm(bank, bank.v(),
                                  wsp(1)[:, 128 * kt:128 * kt + 128],
                                  lnT(kt, 512 * G, 512 * G + 512),
                                  start=(kt == 0), stop=(kt == KT - 1),
                                  waits=([cu, lnw] if kt == 0 else []),
                                  inc=("t" if kt == KT - 1 else None))
                    cus = s_act(uss(gp), bank.v(), AF.Silu,
                                waits=[("t", cgrp)] + r_us[gp].take(),
                                inc="s")
                    bank.free = ("s", cus)
                    if G == NG - 1:
                        wsB[1].freed(("t", cgrp))
                    bA = P[5]
                    nsteps = 4 * G + 4
                    pend = []
                    catt = None

                    def flush_att():
                        nonlocal catt
                        a0, wr = pend.pop(0)
                        catt = mm(bA, bA.v(), VM(128 * a0, 128 * a0 + 128),
                                  wts(a0 % 4),
                                  start=(a0 == 0), stop=(a0 == nsteps - 1),
                                  waits=[wr] +
                                  ([vm_done] if a0 == 0 else []),
                                  inc="t")
                        r_wt[a0 % 4].freed(("t", catt))

                    dqs = drip_pieces[G]
                    for a in range(nsteps):
                        par = a % 4
                        bS = P[par]
                        di = a - 4 * G + 12
                        mm(bS, bS.v(), KS(128 * a, 128 * a + 128),
                           QS(512 * G, 512 * G + 512),
                           start=True, stop=False,
                           waits=[qs_done, ks_done])
                        csc = mm(bS, bS.v(), idb[:, :], bias(di, 0, 512),
                                 start=False, stop=True,
                                 waits=[bias_cms[di // 2]], inc="t")
                        if len(pend) >= 2:
                            flush_att()
                        cwt = s_act(wts(par), bS.v(), AF.Silu,
                                    scale=INV_SQRT_HD,
                                    waits=[("t", csc)] + r_wt[par].take(),
                                    inc="s")
                        bS.free = ("s", cwt)
                        r = a - 4 * G
                        if r >= 0:
                            cwm2 = v_tt(wts(par), wts(par),
                                        bmask[:, 512 * r:512 * r + 512],
                                        ALU.mult, waits=[("s", cwt)])
                            wt_rdy = ("v", cwm2)
                        else:
                            wt_rdy = ("s", cwt)
                        pend.append((a, wt_rdy))
                        if dqs:
                            bias_cast_one(dqs.pop(0))
                    while pend:
                        flush_att()
                    f2 = af(FSC2, 512)
                    cg1_ = v_tt(f2, bA.v(), uss(gp), ALU.mult,
                                waits=[("t", catt), ("s", cus)])
                    bA.free = ("v", cg1_)
                    r_us[gp].freed(("v", cg1_))
                    lastxT = v_tt(
                        xT[:, 2048 * h + 512 * G:2048 * h + 512 * G + 512],
                        xT[:, 2048 * h + 512 * G:2048 * h + 512 * G + 512],
                        f2, ALU.add, inc="v")
                r_bias.freed(("t", csc))
            r_lnT.freed(("t", B.cnt["t"]))
            for _i in range(2):
                stg[_i].free = (stg[_i].free or []) + (stg2[_i].free or [])
            return ("v", lastxT)

        if "attn_a" in BUILD_PHASES:
            xw = phase_attn_a(xw)
        if "ffn1" in BUILD_PHASES:
            xw = phase_ffn(xw)
        if "attn_b" in BUILD_PHASES:
            xw = phase_attn_b(xw)
        if "ffn2" in BUILD_PHASES:
            xw = phase_ffn(xw)

        out3 = out.rearrange("(a p) n -> p a n", p=128)
        dS0 = B.dma(out3[:, 0:4, :],
                    xT[:, 0:8192].rearrange("p (a n) -> p a n", a=4),
                    waits=[xw], q="dO")
        dS1 = B.dma(out3[:, 4:8, :],
                    xT[:, 8192:16384].rearrange("p (a n) -> p a n", a=4),
                    q="dO")
        B.wait("sync", "dO", dS1)

        with nc.Block() as block:
            @block.sync
            def _(En):
                for f in B.st["sync"]:
                    f(En)

            @block.tensor
            def _(En):
                for f in B.st["tensor"]:
                    f(En)

            @block.vector
            def _(En):
                for f in B.st["vector"]:
                    f(En)

            @block.scalar
            def _(En):
                for f in B.st["scalar"]:
                    f(En)

            @block.gpsimd
            def _(En):
                for f in B.st["gpsimd"]:
                    f(En)
    return nc


def _host_inputs(inputs):
    f32 = np.float32
    bf = ml_dtypes.bfloat16
    x = np.asarray(inputs["L0_attn_a_x"], f32)
    gk2pad = np.zeros((128, KD), f32)
    gk2pad[0:16] = np.asarray(inputs["L0_attn_a_Wgk2"], f32)
    gk2pad[32] = np.asarray(inputs["L0_attn_a_bgk2"], f32)
    rmsw2 = np.ascontiguousarray(
        np.asarray(inputs["L0_attn_a_rms_w"], f32).reshape(2, 128).T)
    rel = np.asarray(inputs["L0_attn_b_rel_bias"], f32)
    I = np.arange(512)[None, :] - np.arange(128)[:, None]
    idx = np.stack([I - 128 * (a - 12) + 2047 for a in range(16)])
    bias_exp = np.ascontiguousarray(
        rel[idx].transpose(3, 0, 1, 2).reshape(H8, 16 * 128, 512))
    bmask = np.concatenate(
        [(np.arange(512)[None, :] - 128 * r >= np.arange(128)[:, None])
         .astype(bf) for r in range(4)], axis=1)
    shared = {
        "w_aq": np.asarray(inputs["L0_attn_a_Wq"], f32),
        "w_ak": np.asarray(inputs["L0_attn_a_Wk"], f32),
        "w_av": np.asarray(inputs["L0_attn_a_Wv"], f32),
        "w_ag": np.asarray(inputs["L0_attn_a_Wg"], f32),
        "w_ao": np.asarray(inputs["L0_attn_a_Wout"], f32),
        "w_gk1": np.asarray(inputs["L0_attn_a_Wgk1"], f32),
        "gk2pad": gk2pad, "rmsw2": rmsw2,
        "w_bq": np.asarray(inputs["L0_attn_b_Wq"], f32),
        "w_bk": np.asarray(inputs["L0_attn_b_Wk"], f32),
        "w_bv": np.asarray(inputs["L0_attn_b_Wv"], f32),
        "w_bu": np.asarray(inputs["L0_attn_b_Wu"], f32),
        "bias_exp": bias_exp,
        "w_fg": np.asarray(inputs["L0_ffn_Wg"], f32),
        "bg_row": np.asarray(inputs["L0_ffn_bg"], f32).reshape(1, E8),
        "w_f1": np.asarray(inputs["L0_ffn_W1"], f32),
        "w_f2": np.asarray(inputs["L0_ffn_W2"], f32),
        "w_f3": np.asarray(inputs["L0_ffn_W3"], f32),
        "c_triu": np.triu(np.ones((128, 128), f32)),
        "c_idf": np.eye(128, dtype=f32),
        "c_idb": np.eye(128, dtype=bf),
        "c_iota": np.ascontiguousarray(np.broadcast_to(
            np.arange(7, -1, -1, dtype=f32)[None, :], (128, E8))),
        "c_bmask": bmask,
    }
    in_maps = []
    for b in range(8):
        m = dict(shared)
        m["xin"] = np.ascontiguousarray(x[b].T)
        in_maps.append(m)
    return in_maps


def kernel(**inputs):
    in_maps = _host_inputs(inputs)
    nc = build_program()
    res = run_bass_kernel_spmd(nc, in_maps, core_ids=list(range(8)))
    outs = [np.ascontiguousarray(r["out"].T) for r in res.results]
    return np.stack(outs).astype(np.float32)

